# version 04
# baseline (speedup 1.0000x reference)
import contextlib
import numpy as np
import ml_dtypes

import concourse.bass as bass
import concourse.mybir as mybir
from concourse.bass_utils import run_bass_kernel_spmd

F32 = mybir.dt.float32
BF16 = mybir.dt.bfloat16
AF = mybir.ActivationFunctionType
ALU = mybir.AluOpType
AX = mybir.AxisListType
bfl = ml_dtypes.bfloat16

B, L, D = 2, 8192, 1024
H, KV, HD = 16, 4, 64
E, M = 8, 4096
NCORE = 8
TPB = L // NCORE
T = 2 * TPB
DC = D // 128
MC = M // 128
EPS = 1e-5

ENGINES = ("sync", "tensor", "scalar", "vector", "gpsimd")


def _alpha_full():
    pos = np.arange(L, dtype=np.float64)
    s = np.asarray([2.0 ** (-0.5 * (i + 1)) for i in range(H)], dtype=np.float64)
    return np.exp(s[None, :] * (pos[:, None] - (L - 1))).astype(np.float32)


_ALPHA = _alpha_full()
LIVE = [
    [kc for kc in range(L // 128) if np.any(_ALPHA[kc * 128:(kc + 1) * 128, h] != 0.0)]
    for h in range(H)
]


def _host_consts(core):
    pos = np.arange(TPB * core, TPB * (core + 1), dtype=np.float64)
    d = np.arange(128)
    f = 1.0 / (10000.0 ** (2.0 * ((d % 64) % 32) / 64.0))
    ang = pos[None, :] * f[:, None]
    cos, sin = np.cos(ang), np.sin(ang)
    a = _ALPHA.reshape(64, 128, H).transpose(1, 0, 2).reshape(128, 64 * H)
    R = np.zeros((128, 128), dtype=np.float32)
    for o in (0, 64):
        for m in range(32):
            R[o + m + 32, o + m] = -1.0
            R[o + m, o + m + 32] = 1.0
    return {
        "cosq": (cos / 8.0).astype(bfl),
        "sinq": (sin / 8.0).astype(bfl),
        "cosk": cos.astype(bfl),
        "sink": sin.astype(bfl),
        "alpha_img": np.ascontiguousarray(a),
        "rot_b": R.astype(bfl),
        "ident_b": np.eye(128, dtype=np.float32).astype(bfl),
        "ident_f": np.eye(128, dtype=np.float32),
    }


class Sem:
    def __init__(self, nc, stack, name):
        self.h = stack.enter_context(nc.semaphore(name))
        self.n = 0

    def now(self):
        return (self.h, self.n)


class Prog:

    def __init__(self, nc):
        self.nc = nc
        self.steps = []

    def emit(self, eng, fn, waits=(), inc=None, k=1):
        ws = tuple((h, v) for (h, v) in waits if v > 0)
        ih = None
        if inc is not None:
            inc.n += k
            ih = inc.h
        self.steps.append((eng, ws, fn, ih, k))

    def wait(self, eng, waits):
        self.emit(eng, None, waits=waits)

    def flush(self):
        steps = self.steps
        self.steps = []
        with self.nc.Block() as block:
            for name in ENGINES:
                def runner(e, name=name):
                    for (eng, ws, fn, ih, k) in steps:
                        if eng != name:
                            continue
                        for (h, v) in ws:
                            e.wait_ge(h, v)
                        ins = fn(e) if fn is not None else None
                        if ih is not None:
                            assert ins is not None
                            ins.then_inc(ih, k)
                getattr(block, name)(runner)


def build_nc():
    nc = bass.Bass("TRN2", target_bir_lowering=False)

    xT = nc.dram_tensor("xT", [D, T], F32, kind="ExternalInput").ap()
    Wq = nc.dram_tensor("Wq", [D, D], F32, kind="ExternalInput").ap()
    Wk = nc.dram_tensor("Wk", [D, KV * HD], F32, kind="ExternalInput").ap()
    Wv = nc.dram_tensor("Wv", [D, KV * HD], F32, kind="ExternalInput").ap()
    Wo = nc.dram_tensor("Wo", [D, D], F32, kind="ExternalInput").ap()
    Wg = nc.dram_tensor("Wg", [E, D, M], F32, kind="ExternalInput").ap()
    W1 = nc.dram_tensor("W1", [E, D, M], F32, kind="ExternalInput").ap()
    W2 = nc.dram_tensor("W2", [E, M, D], F32, kind="ExternalInput").ap()
    Wgate_img = nc.dram_tensor("Wgate_img", [128, DC * E], F32, kind="ExternalInput").ap()
    bgate_r = nc.dram_tensor("bgate_r", [1, E], F32, kind="ExternalInput").ap()
    bg_img = nc.dram_tensor("bg_img", [128, E * MC], F32, kind="ExternalInput").ap()
    b1_img = nc.dram_tensor("b1_img", [128, E * MC], F32, kind="ExternalInput").ap()
    b2_img = nc.dram_tensor("b2_img", [128, E * DC], F32, kind="ExternalInput").ap()
    cosq = nc.dram_tensor("cosq", [128, TPB], BF16, kind="ExternalInput").ap()
    sinq = nc.dram_tensor("sinq", [128, TPB], BF16, kind="ExternalInput").ap()
    cosk = nc.dram_tensor("cosk", [128, TPB], BF16, kind="ExternalInput").ap()
    sink = nc.dram_tensor("sink", [128, TPB], BF16, kind="ExternalInput").ap()
    alpha_img = nc.dram_tensor("alpha_img", [128, 64 * H], F32, kind="ExternalInput").ap()
    rot_b = nc.dram_tensor("rot_b", [128, 128], BF16, kind="ExternalInput").ap()
    ident_b = nc.dram_tensor("ident_b", [128, 128], BF16, kind="ExternalInput").ap()
    ident_f = nc.dram_tensor("ident_f", [128, 128], F32, kind="ExternalInput").ap()
    out_d = nc.dram_tensor("out", [D, T], F32, kind="ExternalOutput").ap()

    ccki = nc.dram_tensor("ccki", [KV * HD, T], BF16)
    ccko = nc.dram_tensor("ccko", [NCORE * KV * HD, T], BF16)
    ccvi = nc.dram_tensor("ccvi", [T, KV * HD], BF16)
    ccvo = nc.dram_tensor("ccvo", [NCORE * T, KV * HD], BF16)

    wg_t = Wg.rearrange("e (dc p) m -> e p dc m", p=128)
    w1_t = W1.rearrange("e (dc p) m -> e p dc m", p=128)
    w2_t = W2.rearrange("e (mc p) d -> e p mc d", p=128)
    wq_t = Wq.rearrange("(dc p) m -> p dc m", p=128)
    wk_t = Wk.rearrange("(dc p) m -> p dc m", p=128)
    wv_t = Wv.rearrange("(dc p) m -> p dc m", p=128)
    wo_t = Wo.rearrange("(dc p) m -> p dc m", p=128)
    out_t = out_d.rearrange("(dc p) t -> dc p t", p=128)

    st = contextlib.ExitStack()
    with st:
        E_ = st.enter_context
        x2 = E_(nc.sbuf_tensor("x2", [128, DC * T], F32))
        arena = E_(nc.sbuf_tensor("arena", [128, 43008], BF16))
        af32 = E_(nc.sbuf_tensor("af32", [128, 3072], F32))
        alpha_sb = E_(nc.sbuf_tensor("alpha_sb", [128, 64 * H], F32))
        wgbuf = E_(nc.sbuf_tensor("wgbuf", [128, 2 * 1024], BF16))
        w1buf = E_(nc.sbuf_tensor("w1buf", [128, 2 * 1024], BF16))
        w2buf = E_(nc.sbuf_tensor("w2buf", [128, 3 * 1024], BF16))
        ktb = E_(nc.sbuf_tensor("ktb", [128, 4 * 128], BF16))
        vtb = E_(nc.sbuf_tensor("vtb", [128, 4 * 64], BF16))
        vhb = E_(nc.sbuf_tensor("vhb", [128, 4 * 65], BF16))
        bg_sb = E_(nc.sbuf_tensor("bg_sb", [128, E * MC], F32))
        b1_sb = E_(nc.sbuf_tensor("b1_sb", [128, E * MC], F32))
        b2_sb = E_(nc.sbuf_tensor("b2_sb", [128, E * DC], F32))
        wgate_f = E_(nc.sbuf_tensor("wgate_f", [128, DC * E], F32))
        wgate_hi = E_(nc.sbuf_tensor("wgate_hi", [128, DC * E], BF16))
        wgate_lo = E_(nc.sbuf_tensor("wgate_lo", [128, DC * E], BF16))
        wgate_d = E_(nc.sbuf_tensor("wgate_d", [128, DC * E], F32))
        bgate_f = E_(nc.sbuf_tensor("bgate_f", [1, E], F32))
        bgate_hi = E_(nc.sbuf_tensor("bgate_hi", [1, E], BF16))
        bgate_lo = E_(nc.sbuf_tensor("bgate_lo", [1, E], BF16))
        bgate_d = E_(nc.sbuf_tensor("bgate_d", [1, E], F32))
        rot_sb = E_(nc.sbuf_tensor("rot_sb", [128, 128], BF16))
        idb_sb = E_(nc.sbuf_tensor("idb_sb", [128, 128], BF16))
        idf_sb = E_(nc.sbuf_tensor("idf_sb", [128, 128], F32))
        ones_c = E_(nc.sbuf_tensor("ones_c", [128, 1], F32))
        ones_r = E_(nc.sbuf_tensor("ones_r", [1, 128], F32))
        ones_rb = E_(nc.sbuf_tensor("ones_rb", [1, 128], BF16))
        ceps = E_(nc.sbuf_tensor("ceps", [1, 1], F32))
        cm05 = E_(nc.sbuf_tensor("cm05", [1, 1], F32))
        m_row = E_(nc.sbuf_tensor("m_row", [1, 1024], F32))
        v_row = E_(nc.sbuf_tensor("v_row", [1, 1024], F32))
        w_row = E_(nc.sbuf_tensor("w_row", [1, 2048], F32))
        zsb = E_(nc.sbuf_tensor("zsb", [128, 64], F32))
        t8 = E_(nc.sbuf_tensor("t8", [128, 64], F32))
        zm8 = E_(nc.sbuf_tensor("zm8", [128, 64], F32))
        msk1 = E_(nc.sbuf_tensor("msk1", [128, 64], F32))
        msk2 = E_(nc.sbuf_tensor("msk2", [128, 64], F32))
        m18 = E_(nc.sbuf_tensor("m18", [128, 8], F32))
        m28 = E_(nc.sbuf_tensor("m28", [128, 8], F32))
        d8 = E_(nc.sbuf_tensor("d8", [128, 8], F32))
        w1h = E_(nc.sbuf_tensor("w1h", [128, 8], F32))
        w2h = E_(nc.sbuf_tensor("w2h", [128, 8], F32))
        wvec = E_(nc.sbuf_tensor("wvec", [128, 64], F32))
        wT_sb = E_(nc.sbuf_tensor("wT_sb", [64, 128], F32))

        P0 = E_(nc.psum_tensor("P0", [128, 1024], F32))
        P1 = E_(nc.psum_tensor("P1", [128, 1024], F32))
        P2 = E_(nc.psum_tensor("P2", [128, 1024], F32))
        P3 = E_(nc.psum_tensor("P3", [128, 512], F32))
        PT = E_(nc.psum_tensor("PT", [128, 128], BF16))

        names = ("IN", "PE", "SC", "DV", "PL", "VB", "SB", "CC", "ST", "KVC",
                 "WGA", "WGB", "W1A", "W1B", "W2A", "W2B", "W2C",
                 "KVS0", "KVS1", "KVS2", "KVS3", "WRA", "WRB", "AWA", "AWB")
        S = {n: Sem(nc, st, "s" + n) for n in names}

        P = Prog(nc)

        A_H, A_XT, A_LO, A_T1 = 0, 32768, 40960, 41984
        A_Q, A_OC, A_PS, A_KC, A_VT, A_VF, A_KB = 0, 4096, 8192, 9216, 13312, 17408, 18432
        A_COS = 19456
        F_T, F_U, F_XSQ = 0, 1024, 2048

        def x2s(dc, c0, w):
            return x2[:, dc * T + c0: dc * T + c0 + w]

        def dma(eng, sem, outap, inap, waits=()):
            P.emit(eng, lambda e, o=outap, i=inap: e.dma_start(out=o, in_=i),
                   waits=waits, inc=sem, k=16)

        dma("sync", S["IN"], x2[:].rearrange("p (dc t) -> p dc t", dc=DC),
            xT.rearrange("(dc p) t -> p dc t", p=128))
        dma("sync", S["IN"], alpha_sb[:], alpha_img)
        dma("sync", S["IN"], rot_sb[:], rot_b)
        dma("sync", S["IN"], idb_sb[:], ident_b)
        dma("sync", S["IN"], idf_sb[:], ident_f)
        dma("sync", S["IN"], bg_sb[:], bg_img)
        dma("sync", S["IN"], b1_sb[:], b1_img)
        dma("sync", S["IN"], b2_sb[:], b2_img)
        dma("sync", S["IN"], wgate_f[:], Wgate_img)
        dma("sync", S["IN"], bgate_f[:], bgate_r)
        all_in = S["IN"].now()

        def setup_v(v):
            v.memset(ones_c[:], 1.0)
            v.memset(ones_r[:], 1.0)
            v.memset(ones_rb[:], 1.0)
            v.memset(ceps[:], EPS)
            return v.memset(cm05[:], -0.5)
        P.emit("vector", setup_v, waits=[all_in], inc=S["VB"])

        P.emit("gpsimd", lambda e: e.tensor_copy(out=wgate_hi[:], in_=wgate_f[:]),
               waits=[all_in], inc=S["PL"])
        P.emit("gpsimd", lambda e: e.tensor_copy(out=bgate_hi[:], in_=bgate_f[:]),
               inc=S["PL"])
        P.emit("vector", lambda e: e.tensor_tensor(
            out=wgate_d[:], in0=wgate_f[:], in1=wgate_hi[:], op=ALU.subtract),
            waits=[S["PL"].now(), S["VB"].now()], inc=S["VB"])
        P.emit("vector", lambda e: e.tensor_tensor(
            out=bgate_d[:], in0=bgate_f[:], in1=bgate_hi[:], op=ALU.subtract),
            waits=[S["VB"].now()], inc=S["VB"])
        P.emit("gpsimd", lambda e: e.tensor_copy(out=wgate_lo[:], in_=wgate_d[:]),
               waits=[S["VB"].now()], inc=S["PL"])
        P.emit("gpsimd", lambda e: e.tensor_copy(out=bgate_lo[:], in_=bgate_d[:]),
               inc=S["PL"])
        P.flush()

        def emit_ln(c0, width, gate=False):
            nh = width // 512
            for n in range(nh):
                for dc in range(DC):
                    xs = x2s(dc, c0 + n * 512, 512)
                    sq = af32[:, F_XSQ + (dc % 2) * 512: F_XSQ + (dc % 2) * 512 + 512]
                    P.emit("scalar", lambda e, o=sq, i=xs: e.activation(
                        out=o, in_=i, func=AF.Square),
                        waits=[S["PE"].now()], inc=S["SC"])
                    P.emit("tensor", lambda e, i=xs, n=n, dc=dc: e.matmul(
                        P0[0:1, n * 512:(n + 1) * 512], ones_c[:], i,
                        start=(dc == 0), stop=(dc == DC - 1), skip_group_check=True))
                    P.emit("tensor", lambda e, q=sq, n=n, dc=dc: e.matmul(
                        P1[0:1, n * 512:(n + 1) * 512], ones_c[:], q,
                        start=(dc == 0), stop=(dc == DC - 1), skip_group_check=True),
                        waits=[S["SC"].now()], inc=S["PE"])
            stats = S["PE"].now()
            mr = m_row[:, 0:width]
            vr = v_row[:, 0:width]
            qr = af32[0:1, F_XSQ:F_XSQ + width]
            P.emit("vector", lambda e: e.tensor_scalar(
                out=mr, in0=P0[0:1, 0:width], scalar1=1.0 / D, scalar2=None,
                op0=ALU.mult), waits=[stats, S["VB"].now()], inc=S["VB"])
            P.emit("vector", lambda e: e.tensor_scalar(
                out=vr, in0=P1[0:1, 0:width], scalar1=1.0 / D, scalar2=None,
                op0=ALU.mult), waits=[S["VB"].now()], inc=S["VB"])
            P.emit("vector", lambda e: e.scalar_tensor_tensor(
                out=qr, in0=mr, scalar=-1.0, in1=mr, op0=ALU.mult, op1=ALU.mult),
                waits=[S["VB"].now()], inc=S["VB"])
            P.emit("vector", lambda e: e.tensor_tensor(
                out=vr, in0=vr, in1=qr, op=ALU.add),
                waits=[S["VB"].now()], inc=S["VB"])
            row_dv = S["VB"].now()
            P.emit("scalar", lambda e: e.activation(
                out=vr, in_=vr, func=AF.Ln, bias=ceps[:]),
                waits=[row_dv, S["SB"].now()], inc=S["SB"])
            P.emit("scalar", lambda e: e.activation(
                out=vr, in_=vr, func=AF.Exp, scale=cm05[:]),
                waits=[S["SB"].now()], inc=S["SB"])
            P.emit("vector", lambda e: e.tensor_tensor(
                out=mr, in0=mr, in1=vr, op=ALU.mult),
                waits=[S["SB"].now(), S["VB"].now()], inc=S["VB"])
            rows = S["VB"].now()
            for n in range(nh):
                P.emit("tensor", lambda e, n=n: e.matmul(
                    P0[:, n * 512:(n + 1) * 512], ones_r[:],
                    v_row[:, n * 512:(n + 1) * 512], start=True, stop=True),
                    waits=[rows] if n == 0 else ())
                P.emit("tensor", lambda e, n=n: e.matmul(
                    P1[:, n * 512:(n + 1) * 512], ones_r[:],
                    m_row[:, n * 512:(n + 1) * 512], start=True, stop=True),
                    inc=S["PE"])
            bcast = S["PE"].now()
            for dc in range(DC):
                tbuf = af32[:, F_T:F_T + width]
                xtsl = arena[:, A_XT + dc * width: A_XT + (dc + 1) * width]
                losl = arena[:, A_LO: A_LO + width]
                P.emit("vector", lambda e, dc=dc, tb=tbuf: e.tensor_tensor(
                    out=tb, in0=x2s(dc, c0, width), in1=P0[:, 0:width], op=ALU.mult),
                    waits=[bcast] + ([S["PE"].now()] if gate else []), inc=S["DV"])
                P.emit("vector", lambda e, tb=tbuf: e.tensor_tensor(
                    out=tb, in0=tb, in1=P1[:, 0:width], op=ALU.subtract),
                    inc=S["DV"])
                P.emit("vector", lambda e, o=xtsl, tb=tbuf: e.tensor_copy(
                    out=o, in_=tb), inc=S["DV"])
                if gate:
                    P.emit("vector", lambda e, o=losl, tb=tbuf, x=xtsl: e.tensor_tensor(
                        out=o, in0=tb, in1=x, op=ALU.subtract), inc=S["DV"])
                    g_dv = S["DV"].now()
                    ntch = width // 128
                    for tch in range(ntch):
                        xsl = arena[:, A_XT + dc * width + tch * 128:
                                    A_XT + dc * width + tch * 128 + 128]
                        lsl = arena[:, A_LO + tch * 128: A_LO + tch * 128 + 128]
                        zr = P3[:, tch * 8:(tch + 1) * 8]
                        for pi, (lhs, rhs) in enumerate((
                                (xsl, wgate_hi), (xsl, wgate_lo), (lsl, wgate_hi))):
                            P.emit("tensor", lambda e, zr=zr, l=lhs, r=rhs, dc=dc,
                                   pi=pi, tch=tch: e.matmul(
                                zr, l, r[:, dc * 8:(dc + 1) * 8],
                                start=(dc == 0 and pi == 0 and tch == 0), stop=False,
                                skip_group_check=True),
                                waits=[g_dv] if (pi == 0) else (), inc=S["PE"])
            xt_ready = S["DV"].now()
            wT_ready = None
            if gate:
                ntch = width // 128
                for tch in range(ntch):
                    zr = P3[:, tch * 8:(tch + 1) * 8]
                    P.emit("tensor", lambda e, zr=zr: e.matmul(
                        zr, ones_rb[:], bgate_hi[:], start=False, stop=False,
                        skip_group_check=True))
                    P.emit("tensor", lambda e, zr=zr, tch=tch: e.matmul(
                        zr, ones_rb[:], bgate_lo[:], start=False, stop=True,
                        skip_group_check=True), inc=S["PE"])
                z_done = S["PE"].now()
                nt = ntch * 8

                def vb(fn, extra=()):
                    P.emit("vector", fn, waits=[S["VB"].now()] + list(extra),
                           inc=S["VB"])

                z3 = zsb[:, 0:nt].rearrange("p (t e) -> p t e", e=8)
                zm3 = zm8[:, 0:nt].rearrange("p (t e) -> p t e", e=8)
                mk13 = msk1[:, 0:nt].rearrange("p (t e) -> p t e", e=8)
                mk23 = msk2[:, 0:nt].rearrange("p (t e) -> p t e", e=8)
                m1b = m18[:, 0:ntch].rearrange("p (t e) -> p t e", e=1) \
                    .broadcast_to([128, ntch, 8])
                m2b = m28[:, 0:ntch].rearrange("p (t e) -> p t e", e=1) \
                    .broadcast_to([128, ntch, 8])
                w1b = w1h[:, 0:ntch].rearrange("p (t e) -> p t e", e=1) \
                    .broadcast_to([128, ntch, 8])
                w2b = w2h[:, 0:ntch].rearrange("p (t e) -> p t e", e=1) \
                    .broadcast_to([128, ntch, 8])
                vb(lambda e: e.tensor_copy(out=zsb[:, 0:nt], in_=P3[:, 0:nt]),
                   extra=[z_done])
                vb(lambda e: e.tensor_reduce(out=m18[:, 0:ntch], in_=z3, axis=AX.X,
                                             op=ALU.max))
                vb(lambda e: e.tensor_tensor(out=mk13, in0=z3, in1=m1b,
                                             op=ALU.is_equal))
                vb(lambda e: e.tensor_scalar(out=t8[:, 0:nt], in0=msk1[:, 0:nt],
                                             scalar1=-1e30, scalar2=None, op0=ALU.mult))
                vb(lambda e: e.tensor_tensor(out=zm8[:, 0:nt], in0=zsb[:, 0:nt],
                                             in1=t8[:, 0:nt], op=ALU.add))
                vb(lambda e: e.tensor_reduce(out=m28[:, 0:ntch], in_=zm3, axis=AX.X,
                                             op=ALU.max))
                vb(lambda e: e.tensor_tensor(out=mk23, in0=zm3, in1=m2b,
                                             op=ALU.is_equal))
                vb(lambda e: e.tensor_tensor(out=d8[:, 0:ntch], in0=m18[:, 0:ntch],
                                             in1=m28[:, 0:ntch], op=ALU.subtract))
                P.emit("scalar", lambda e: e.activation(
                    out=w1h[:, 0:ntch], in_=d8[:, 0:ntch], func=AF.Sigmoid),
                    waits=[S["VB"].now(), S["SB"].now()], inc=S["SB"])
                vb(lambda e: e.tensor_scalar(out=w1h[:, 0:ntch], in0=w1h[:, 0:ntch],
                                             scalar1=0.5, scalar2=None, op0=ALU.mult),
                   extra=[S["SB"].now()])
                vb(lambda e: e.tensor_scalar(out=w2h[:, 0:ntch], in0=w1h[:, 0:ntch],
                                             scalar1=-1.0, scalar2=0.5,
                                             op0=ALU.mult, op1=ALU.add))
                vb(lambda e: e.tensor_tensor(out=mk13, in0=mk13, in1=w1b, op=ALU.mult))
                vb(lambda e: e.tensor_tensor(out=mk23, in0=mk23, in1=w2b, op=ALU.mult))
                vb(lambda e: e.tensor_tensor(out=wvec[:, 0:nt], in0=msk1[:, 0:nt],
                                             in1=msk2[:, 0:nt], op=ALU.add))
                wv_done = S["VB"].now()
                P.emit("tensor", lambda e: e.transpose(
                    P3[0:nt, 0:128], wvec[:, 0:nt], idf_sb[:]),
                    waits=[wv_done], inc=S["PE"])
                P.emit("vector", lambda e: e.tensor_copy(
                    out=wT_sb[0:nt, :], in_=P3[0:nt, 0:128]),
                    waits=[S["PE"].now(), S["VB"].now()], inc=S["VB"])
                wT_ready = S["VB"].now()
            return xt_ready, wT_ready

        wr_state = [None, None]

        def emit_ffn(c0, nexp=E, sub='all'):
            xt_ready, wT_ready = emit_ln(c0, 1024, gate=True)
            wg_cons = [None, None]
            w1_cons = [None, None]
            w2_cons = [None, None, None]
            wg_fill = [None, None]
            w1_fill = [None, None]
            w2_fill = [None, None, None]
            wr_fill = [None, None]
            p3_free = S["VB"].now()
            silu_done = None
            h_done = None
            u_done = [None, None]
            sub_r = sub in ('all', 'pa', 'wbs') or 'r' in sub
            sub_d = sub in ('all', 'pa', 'wbs') or 'd' in sub
            sub_m = sub in ('all', 'pa', 'wbs') or 'm' in sub
            sub_s = sub in ('all', 'pa', 'wbs') or 's' in sub
            sub_h = sub in ('all', 'pa', 'wbs') or 'h' in sub
            for e in range(nexp):
                ws = e % 2
                sem = S["WRA" if ws == 0 else "WRB"]
                if sub_r:
                  for n in range(2):
                    for tq in range(4):
                        src = wT_sb[(4 * n + tq) * 8 + e:(4 * n + tq) * 8 + e + 1, :]
                        dst = w_row[0:1, ws * 1024 + n * 512 + tq * 128:
                                    ws * 1024 + n * 512 + tq * 128 + 128]
                        dma("sync", sem, dst, src,
                            waits=[wT_ready] + ([wr_state[ws]] if wr_state[ws] else []))
                wr_fill[ws] = sem.now()
                for mc in range(MC):
                    slot = mc % 2
                    gsem = S["WGA" if slot == 0 else "WGB"]
                    osem = S["W1A" if slot == 0 else "W1B"]
                    if not sub_d:
                        continue
                    dma("gpsimd", gsem,
                        wgbuf[:, slot * 1024:(slot + 1) * 1024]
                        .rearrange("p (dc j) -> p dc j", dc=DC),
                        wg_t[e, :, :, mc * 128:(mc + 1) * 128],
                        waits=[wg_cons[slot]] if wg_cons[slot] else ())
                    wg_fill[slot] = gsem.now()
                    dma("gpsimd", osem,
                        w1buf[:, slot * 1024:(slot + 1) * 1024]
                        .rearrange("p (dc j) -> p dc j", dc=DC),
                        w1_t[e, :, :, mc * 128:(mc + 1) * 128],
                        waits=[w1_cons[slot]] if w1_cons[slot] else ())
                    w1_fill[slot] = osem.now()
                    if not sub_m:
                        continue
                    wA = [wg_fill[slot], xt_ready]
                    if silu_done is not None:
                        wA.append(silu_done)
                    for dc in range(DC):
                        for n in range(2):
                            P.emit("tensor", lambda e_, dc=dc, n=n, slot=slot: e_.matmul(
                                P0[:, n * 512:(n + 1) * 512],
                                wgbuf[:, slot * 1024 + dc * 128:slot * 1024 + (dc + 1) * 128],
                                arena[:, A_XT + dc * 1024 + n * 512:
                                      A_XT + dc * 1024 + n * 512 + 512],
                                start=(dc == 0), stop=(dc == DC - 1),
                                skip_group_check=True),
                                waits=wA if (n == 0 and dc == 0) else (),
                                inc=S["PE"] if (n == 1 and dc == DC - 1) else None)
                    a_done = S["PE"].now()
                    wB = [w1_fill[slot]]
                    if h_done is not None:
                        wB.append(h_done)
                    for dc in range(DC):
                        for n in range(2):
                            P.emit("tensor", lambda e_, dc=dc, n=n, slot=slot: e_.matmul(
                                P1[:, n * 512:(n + 1) * 512],
                                w1buf[:, slot * 1024 + dc * 128:slot * 1024 + (dc + 1) * 128],
                                arena[:, A_XT + dc * 1024 + n * 512:
                                      A_XT + dc * 1024 + n * 512 + 512],
                                start=(dc == 0), stop=(dc == DC - 1),
                                skip_group_check=True),
                                waits=wB if (n == 0 and dc == 0) else (),
                                inc=S["PE"] if (n == 1 and dc == DC - 1) else None)
                    b_done = S["PE"].now()
                    wg_cons[slot] = a_done
                    w1_cons[slot] = b_done
                    if not sub_s:
                        silu_done = a_done
                        h_done = b_done
                        continue
                    bcol = bg_sb[:, e * MC + mc: e * MC + mc + 1]
                    P.emit("scalar", lambda e_, bc=bcol: e_.activation(
                        out=arena[:, A_T1:A_T1 + 1024], in_=P0[:, 0:1024],
                        func=AF.Silu, bias=bc),
                        waits=[a_done, S["DV"].now()], inc=S["SC"])
                    silu_done = S["SC"].now()
                    if not sub_h:
                        h_done = b_done
                        continue
                    b1col = b1_sb[:, e * MC + mc: e * MC + mc + 1]
                    P.emit("vector", lambda e_, mc=mc, bc=b1col: e_.scalar_tensor_tensor(
                        out=arena[:, A_H + mc * 1024:A_H + (mc + 1) * 1024],
                        in0=P1[:, 0:1024], scalar=bc,
                        in1=arena[:, A_T1:A_T1 + 1024], op0=ALU.add, op1=ALU.mult),
                        waits=[b_done, silu_done], inc=S["DV"])
                    h_done = S["DV"].now()
                if sub == 'pa' or sub.startswith('x'):
                    continue
                wbs_done = None
                for n in range(2):
                    P.emit("tensor", lambda e_, ws=ws, n=n: e_.matmul(
                        P3[:, 0:512], ones_r[:],
                        w_row[0:1, ws * 1024 + n * 512: ws * 1024 + n * 512 + 512],
                        start=True, stop=True),
                        waits=[wr_fill[ws], p3_free], inc=S["PE"])
                    P.emit("vector", lambda e_, n=n: e_.tensor_copy(
                        out=af32[:, F_T + n * 512:F_T + n * 512 + 512],
                        in_=P3[:, 0:512]),
                        waits=[S["PE"].now()], inc=S["DV"])
                    p3_free = S["DV"].now()
                wbs_done = S["DV"].now()
                wr_state[ws] = S["PE"].now()
                if sub == 'wbs':
                    continue
                w2_fill_by = {}

                def w2dma(qg):
                    s3 = qg % 3
                    qsem = S[("W2A", "W2B", "W2C")[s3]]
                    dma("gpsimd", qsem,
                        w2buf[:, s3 * 1024:(s3 + 1) * 1024]
                        .rearrange("p (mq j) -> p mq j", mq=8),
                        w2_t[e, :, (qg % 4) * 8:(qg % 4) * 8 + 8,
                             (qg // 4) * 128:(qg // 4) * 128 + 128],
                        waits=[w2_cons[s3]] if w2_cons[s3] else ())
                    w2_fill_by[qg] = qsem.now()

                for qg in range(3):
                    w2dma(qg)
                next_qg = 3
                for dout in range(DC):
                    for j in range(MC):
                        qg = dout * 4 + j // 8
                        s3 = qg % 3
                        for n in range(2):
                            wts = []
                            if n == 0 and j % 8 == 0:
                                wts.append(w2_fill_by[qg])
                            if n == 0 and j == 0:
                                wts.append(h_done)
                                if u_done[0] is not None:
                                    wts.append(u_done[1])
                            P.emit("tensor", lambda e_, j=j, s3=s3, n=n: e_.matmul(
                                P2[:, n * 512:(n + 1) * 512],
                                w2buf[:, s3 * 1024 + (j % 8) * 128:
                                      s3 * 1024 + (j % 8) * 128 + 128],
                                arena[:, A_H + j * 1024 + n * 512:
                                      A_H + j * 1024 + n * 512 + 512],
                                start=(j == 0), stop=(j == MC - 1),
                                skip_group_check=True),
                                waits=wts,
                                inc=S["PE"] if (n == 1 and j % 8 == 7) else None)
                        if j % 8 == 7:
                            w2_cons[s3] = S["PE"].now()
                            if next_qg < 32:
                                w2dma(next_qg)
                                next_qg += 1
                    y_done = S["PE"].now()
                    b2col = b2_sb[:, e * DC + dout: e * DC + dout + 1]
                    for n in range(2):
                        P.emit("vector", lambda e_, n=n, bc=b2col: e_.scalar_tensor_tensor(
                            out=af32[:, F_U:F_U + 512],
                            in0=P2[:, n * 512:(n + 1) * 512], scalar=bc,
                            in1=af32[:, F_T + n * 512:F_T + n * 512 + 512],
                            op0=ALU.add, op1=ALU.mult),
                            waits=[y_done, wbs_done], inc=S["DV"])
                        P.emit("vector", lambda e_, dout=dout, n=n: e_.tensor_tensor(
                            out=x2s(dout, c0 + n * 512, 512),
                            in0=x2s(dout, c0 + n * 512, 512),
                            in1=af32[:, F_U:F_U + 512], op=ALU.add),
                            inc=S["DV"])
                    u_done[0] = u_done[1] = S["DV"].now()

        def emit_kv():
            for ap_, off_ in ((cosq, 0), (sinq, TPB), (cosk, 2 * TPB), (sink, 3 * TPB)):
                dma("sync", S["IN"], arena[:, A_COS + off_:A_COS + off_ + TPB], ap_)
            cos_ready = S["IN"].now()
            for tc in range(2):
                xt_ready, _ = emit_ln(tc * 1024, 1024, gate=False)
                for vc in range(2):
                    asem = S["AWA"]
                    dma("gpsimd", asem,
                        wgbuf[:, 0:1024].rearrange("p (dc j) -> p dc j", dc=DC),
                        wk_t[:, :, vc * 128:(vc + 1) * 128],
                        waits=[S["PE"].now()])
                    kw = asem.now()
                    for dc in range(DC):
                        for n in range(2):
                            P.emit("tensor", lambda e_, dc=dc, n=n: e_.matmul(
                                P1[:, n * 512:(n + 1) * 512],
                                wgbuf[:, dc * 128:(dc + 1) * 128],
                                arena[:, A_XT + dc * 1024 + n * 512:
                                      A_XT + dc * 1024 + n * 512 + 512],
                                start=(dc == 0), stop=(dc == DC - 1),
                                skip_group_check=True),
                                waits=[kw, xt_ready, S["DV"].now()]
                                if (n == 0 and dc == 0) else (),
                                inc=S["PE"] if (n == 1 and dc == DC - 1) else None)
                    kproj = S["PE"].now()
                    P.emit("vector", lambda e_: e_.tensor_copy(
                        out=arena[:, A_KB:A_KB + 1024], in_=P1[:, 0:1024]),
                        waits=[kproj], inc=S["DV"])
                    kb_done = S["DV"].now()
                    for n in range(2):
                        P.emit("tensor", lambda e_, n=n: e_.matmul(
                            P2[:, n * 512:(n + 1) * 512], rot_sb[:],
                            arena[:, A_KB + n * 512:A_KB + n * 512 + 512],
                            start=True, stop=True),
                            waits=[kb_done, S["DV"].now()] if n == 0 else (),
                            inc=S["PE"] if n == 1 else None)
                    krot = S["PE"].now()
                    P.emit("vector", lambda e_: e_.tensor_tensor(
                        out=af32[:, F_T:F_T + 1024], in0=P1[:, 0:1024],
                        in1=arena[:, A_COS + 2 * TPB:A_COS + 3 * TPB], op=ALU.mult),
                        waits=[krot, cos_ready], inc=S["DV"])
                    P.emit("vector", lambda e_: e_.tensor_tensor(
                        out=af32[:, F_U:F_U + 1024], in0=P2[:, 0:1024],
                        in1=arena[:, A_COS + 3 * TPB:A_COS + 4 * TPB], op=ALU.mult),
                        inc=S["DV"])
                    P.emit("vector", lambda e_, tc=tc, vc=vc: e_.tensor_tensor(
                        out=arena[:, A_KC + vc * 2048 + tc * 1024:
                                  A_KC + vc * 2048 + tc * 1024 + 1024],
                        in0=af32[:, F_T:F_T + 1024], in1=af32[:, F_U:F_U + 1024],
                        op=ALU.add), inc=S["DV"])
                    kf_done = S["DV"].now()
                    dma("gpsimd", S["AWB"],
                        w1buf[:, 0:1024].rearrange("p (dc j) -> p dc j", dc=DC),
                        wv_t[:, :, vc * 128:(vc + 1) * 128],
                        waits=[S["PE"].now()])
                    vw = S["AWB"].now()
                    for dc in range(DC):
                        for n in range(2):
                            P.emit("tensor", lambda e_, dc=dc, n=n: e_.matmul(
                                P1[:, n * 512:(n + 1) * 512],
                                w1buf[:, dc * 128:(dc + 1) * 128],
                                arena[:, A_XT + dc * 1024 + n * 512:
                                      A_XT + dc * 1024 + n * 512 + 512],
                                start=(dc == 0), stop=(dc == DC - 1),
                                skip_group_check=True),
                                waits=[vw, kf_done] if (n == 0 and dc == 0) else (),
                                inc=S["PE"] if (n == 1 and dc == DC - 1) else None)
                    vproj = S["PE"].now()
                    P.emit("vector", lambda e_: e_.tensor_copy(
                        out=arena[:, A_VF:A_VF + 1024], in_=P1[:, 0:1024]),
                        waits=[vproj], inc=S["DV"])
                    vf_done = S["DV"].now()
                    for tch in range(8):
                        P.emit("tensor", lambda e_, tch=tch: e_.transpose(
                            PT[:, 0:128],
                            arena[:, A_VF + tch * 128:A_VF + tch * 128 + 128],
                            idb_sb[:]),
                            waits=[vf_done, S["DV"].now()], inc=S["PE"])
                        P.emit("vector", lambda e_, tc=tc, vc=vc, tch=tch: e_.tensor_copy(
                            out=arena[:, A_VT + (tc * 8 + tch) * 256 + vc * 128:
                                      A_VT + (tc * 8 + tch) * 256 + vc * 128 + 128],
                            in_=PT[:, 0:128]),
                            waits=[S["PE"].now()], inc=S["DV"])
            vt_done = S["DV"].now()
            dma("sync", S["KVC"],
                ccki.rearrange("(c p) t -> p c t", p=128),
                arena[:, A_KC:A_KC + 4096].rearrange("p (c t) -> p c t", c=2),
                waits=[vt_done])
            dma("sync", S["KVC"],
                ccvi.rearrange("(tc p) f -> p tc f", p=128),
                arena[:, A_VT:A_VT + 4096].rearrange("p (tc f) -> p tc f", tc=16),
                waits=[vt_done])
            kvc = S["KVC"].now()
            P.emit("gpsimd", lambda e_: e_.collective_compute(
                "AllGather", ALU.bypass, replica_groups=[list(range(NCORE))],
                ins=[ccki.ap().opt()], outs=[ccko.ap().opt()]),
                waits=[kvc], inc=S["CC"])
            P.emit("gpsimd", lambda e_: e_.collective_compute(
                "AllGather", ALU.bypass, replica_groups=[list(range(NCORE))],
                ins=[ccvi.ap().opt()], outs=[ccvo.ap().opt()]),
                inc=S["CC"])
            return S["CC"].now()

        def emit_qc(qc, cc_ready):
            batch, poff = qc // 2, (qc % 2) * 512
            c0 = qc * 512
            xt_ready, _ = emit_ln(c0, 512, gate=False)
            for ch in range(DC):
                asem = S["AWA" if ch % 2 == 0 else "AWB"]
                buf = wgbuf if ch % 2 == 0 else w1buf
                dma("gpsimd", asem,
                    buf[:, 0:1024].rearrange("p (dc j) -> p dc j", dc=DC),
                    wq_t[:, :, ch * 128:(ch + 1) * 128],
                    waits=[S["PE"].now()])
                qw_ = asem.now()
                for dc in range(DC):
                    P.emit("tensor", lambda e_, dc=dc, buf=buf: e_.matmul(
                        P1[:, 0:512], buf[:, dc * 128:(dc + 1) * 128],
                        arena[:, A_XT + dc * 512:A_XT + dc * 512 + 512],
                        start=(dc == 0), stop=(dc == DC - 1),
                        skip_group_check=True),
                        waits=[qw_, xt_ready, S["DV"].now()] if dc == 0 else (),
                        inc=S["PE"] if dc == DC - 1 else None)
                qproj = S["PE"].now()
                P.emit("vector", lambda e_: e_.tensor_copy(
                    out=arena[:, A_KB:A_KB + 512], in_=P1[:, 0:512]),
                    waits=[qproj], inc=S["DV"])
                qb_done = S["DV"].now()
                P.emit("tensor", lambda e_: e_.matmul(
                    P3[:, 0:512], rot_sb[:], arena[:, A_KB:A_KB + 512],
                    start=True, stop=True),
                    waits=[qb_done, S["DV"].now()], inc=S["PE"])
                qrot = S["PE"].now()
                P.emit("vector", lambda e_: e_.tensor_tensor(
                    out=af32[:, F_T:F_T + 512], in0=P1[:, 0:512],
                    in1=arena[:, A_COS + poff:A_COS + poff + 512], op=ALU.mult),
                    waits=[qrot], inc=S["DV"])
                P.emit("vector", lambda e_: e_.tensor_tensor(
                    out=af32[:, F_U:F_U + 512], in0=P3[:, 0:512],
                    in1=arena[:, A_COS + TPB + poff:A_COS + TPB + poff + 512], op=ALU.mult),
                    inc=S["DV"])
                P.emit("vector", lambda e_, ch=ch: e_.tensor_tensor(
                    out=arena[:, A_Q + ch * 512:A_Q + (ch + 1) * 512],
                    in0=af32[:, F_T:F_T + 512], in1=af32[:, F_U:F_U + 512],
                    op=ALU.add), inc=S["DV"])
            q_ready = S["DV"].now()
            p3_free = S["DV"].now()
            kv_cons = [None, None, None, None]
            ps_cons = [None, None]
            p0_cons = [None, None]
            o_evac = None
            for h in range(H):
                ch, base, kvh = h // 2, (h % 2) * 64, h // 4
                live = LIVE[h]
                qk_done = {}
                exp_done = {}
                if h > 0:
                    exp_done[-2] = prev_exps[0]
                    exp_done[-1] = prev_exps[1]
                vh_done = {}
                pv_done_i = {}
                for idx, kc in enumerate(live):
                    r, j = kc // 8, kc % 8
                    slot = idx % 4
                    ssem = S[("KVS0", "KVS1", "KVS2", "KVS3")[slot]]
                    wts = [cc_ready]
                    if kv_cons[slot] is not None:
                        wts += kv_cons[slot]
                    dma("sync", ssem,
                        ktb[base:base + 64, slot * 128:(slot + 1) * 128],
                        ccko.ap()[256 * r + 128 * (kvh // 2) + 64 * (kvh % 2):
                                      256 * r + 128 * (kvh // 2) + 64 * (kvh % 2) + 64,
                                      batch * 1024 + 128 * j:
                                      batch * 1024 + 128 * j + 128],
                        waits=wts)
                    dma("sync", ssem,
                        vtb[:, slot * 64:(slot + 1) * 64],
                        ccvo.ap()[2048 * r + batch * 1024 + 128 * j:
                                      2048 * r + batch * 1024 + 128 * j + 128,
                                      kvh * 64:(kvh + 1) * 64])
                    fill = ssem.now()
                    acol = alpha_sb[:, kc * H + h: kc * H + h + 1]
                    vh_w = [fill]
                    if idx >= 4:
                        vh_w.append(pv_done_i[idx - 4])
                    elif o_evac is not None:
                        vh_w.append(prev_pv)
                    P.emit("vector", lambda e_, slot=slot, ac=acol: e_.tensor_scalar(
                        out=vhb[:, slot * 65:slot * 65 + 64],
                        in0=vtb[:, slot * 64:(slot + 1) * 64],
                        scalar1=ac, scalar2=None, op0=ALU.mult),
                        waits=vh_w, inc=S["DV"])
                    P.emit("vector", lambda e_, slot=slot, ac=acol: e_.tensor_copy(
                        out=vhb[:, slot * 65 + 64:slot * 65 + 65], in_=ac),
                        inc=S["DV"])
                    vh_done[idx] = S["DV"].now()
                    half = idx % 2
                    wts = [fill, q_ready]
                    if (idx - 2) in exp_done:
                        wts.append(exp_done[idx - 2])
                    P.emit("tensor", lambda e_, slot=slot, half=half, base=base,
                           ch=ch: e_.matmul(
                        P0[:, half * 512:(half + 1) * 512],
                        ktb[base:base + 64, slot * 128:(slot + 1) * 128],
                        arena[base:base + 64, A_Q + ch * 512:A_Q + (ch + 1) * 512],
                        start=True, stop=True, skip_group_check=True),
                        waits=wts, inc=S["PE"])
                    qk_done[idx] = S["PE"].now()
                    wts = [qk_done[idx]]
                    if idx >= 2:
                        wts.append(S["PE"].now())
                    P.emit("scalar", lambda e_, half=half: e_.activation(
                        out=arena[:, A_PS + half * 512:A_PS + (half + 1) * 512],
                        in_=P0[:, half * 512:(half + 1) * 512], func=AF.Exp),
                        waits=wts, inc=S["SC"])
                    exp_done[idx] = S["SC"].now()
                    wts = [exp_done[idx], vh_done[idx]]
                    if idx == 0 and o_evac is not None:
                        wts.append(o_evac)
                    P.emit("tensor", lambda e_, slot=slot, half=half, idx=idx,
                           n_live=len(live): e_.matmul(
                        P2[0:65, 0:512], vhb[:, slot * 65:(slot + 1) * 65],
                        arena[:, A_PS + half * 512:A_PS + (half + 1) * 512],
                        start=(idx == 0), stop=(idx == n_live - 1),
                        skip_group_check=True),
                        waits=wts, inc=S["PE"])
                    pv_done_i[idx] = S["PE"].now()
                    kv_cons[slot] = [qk_done[idx], vh_done[idx]]
                pv_done = S["PE"].now()
                prev_pv = pv_done
                nl = len(live)
                prev_exps = (exp_done[nl - 2] if nl >= 2 else exp_done[nl - 1],
                             exp_done[nl - 1])
                P.emit("vector", lambda e_: e_.reciprocal(
                    out=af32[0:1, F_U:F_U + 512], in_=P2[64:65, 0:512]),
                    waits=[pv_done], inc=S["DV"])
                P.emit("tensor", lambda e_: e_.matmul(
                    P3[0:64, 0:512], ones_r[:, 0:64], af32[0:1, F_U:F_U + 512],
                    start=True, stop=True),
                    waits=[S["DV"].now(), p3_free], inc=S["PE"])
                P.emit("vector", lambda e_: e_.tensor_copy(
                    out=af32[0:64, F_T:F_T + 512], in_=P3[0:64, 0:512]),
                    waits=[S["PE"].now()], inc=S["DV"])
                p3_free = S["DV"].now()
                P.emit("vector", lambda e_, base=base, ch=ch: e_.tensor_tensor(
                    out=arena[base:base + 64, A_OC + ch * 512:A_OC + (ch + 1) * 512],
                    in0=P2[0:64, 0:512], in1=af32[0:64, F_T:F_T + 512], op=ALU.mult),
                    inc=S["DV"])
                o_evac = S["DV"].now()
            x2add = [None, None]
            for dout in range(DC):
                asem = S["AWA" if dout % 2 == 0 else "AWB"]
                buf = wgbuf if dout % 2 == 0 else w1buf
                dma("gpsimd", asem,
                    buf[:, 0:1024].rearrange("p (dc j) -> p dc j", dc=DC),
                    wo_t[:, :, dout * 128:(dout + 1) * 128],
                    waits=[S["PE"].now()])
                ww = asem.now()
                half = dout % 2
                for fc in range(DC):
                    wts = [ww, o_evac] if fc == 0 else []
                    if fc == 0 and x2add[half] is not None:
                        wts.append(x2add[half])
                    P.emit("tensor", lambda e_, fc=fc, buf=buf, half=half: e_.matmul(
                        P1[:, half * 512:(half + 1) * 512],
                        buf[:, fc * 128:(fc + 1) * 128],
                        arena[:, A_OC + fc * 512:A_OC + (fc + 1) * 512],
                        start=(fc == 0), stop=(fc == DC - 1),
                        skip_group_check=True),
                        waits=wts,
                        inc=S["PE"] if fc == DC - 1 else None)
                wo_done = S["PE"].now()
                P.emit("vector", lambda e_, dout=dout, half=half: e_.tensor_tensor(
                    out=x2s(dout, c0, 512), in0=x2s(dout, c0, 512),
                    in1=P1[:, half * 512:(half + 1) * 512], op=ALU.add),
                    waits=[wo_done], inc=S["DV"])
                x2add[half] = S["DV"].now()

        stage = globals().get("_STAGE", "full")
        if stage == "xrdw":
            emit_ffn(0, nexp=1, sub="xrd")
            for nm in ("WGA", "WGB", "W1A", "W1B", "WRA"):
                P.wait("gpsimd", [S[nm].now()])
            P.flush()
        if stage == "s0":
            pass
        elif stage == "xrdw":
            pass
        elif stage in ("pa", "wbs") or stage.startswith("x"):
            emit_ffn(0, nexp=1, sub=stage)
            P.flush()
        elif stage.startswith("e"):
            emit_ffn(0, nexp=int(stage[1:]))
            P.flush()
        else:
            emit_ffn(0)
            P.flush()
            if stage != "t0":
                emit_ffn(1024)
                P.flush()
        if stage in ("f1a", "full"):
            cc_ready = emit_kv()
            P.flush()
            for qc in range(4):
                emit_qc(qc, cc_ready)
                P.flush()
        if stage == "full":
            emit_ffn(0)
            P.flush()
            emit_ffn(1024)
            P.flush()
        for dc in range(DC):
            dma("sync", S["ST"], out_t[dc], x2[:, dc * T:(dc + 1) * T])
        P.wait("sync", [S["ST"].now()])
        P.flush()

    return nc


_NC = None


def _get_nc():
    global _NC
    if _NC is None:
        _NC = build_nc()
    return _NC


def kernel(L0_attn_x, L0_attn_Wq, L0_attn_Wk, L0_attn_Wv, L0_attn_Wo,
           L0_ffn_Wgate, L0_ffn_bgate, L0_ffn_Wg, L0_ffn_bg,
           L0_ffn_W1, L0_ffn_b1, L0_ffn_W2, L0_ffn_b2):
    x = np.asarray(L0_attn_x, dtype=np.float32)
    f32c = lambda a: np.ascontiguousarray(np.asarray(a, dtype=np.float32))
    shared = {
        "Wq": f32c(L0_attn_Wq), "Wk": f32c(L0_attn_Wk), "Wv": f32c(L0_attn_Wv),
        "Wo": f32c(L0_attn_Wo), "Wg": f32c(L0_ffn_Wg), "W1": f32c(L0_ffn_W1),
        "W2": f32c(L0_ffn_W2),
        "Wgate_img": np.ascontiguousarray(
            f32c(L0_ffn_Wgate).reshape(DC, 128, E).transpose(1, 0, 2).reshape(128, DC * E)),
        "bgate_r": f32c(L0_ffn_bgate).reshape(1, E),
        "bg_img": np.ascontiguousarray(
            f32c(L0_ffn_bg).reshape(E, MC, 128).transpose(2, 0, 1).reshape(128, E * MC)),
        "b1_img": np.ascontiguousarray(
            f32c(L0_ffn_b1).reshape(E, MC, 128).transpose(2, 0, 1).reshape(128, E * MC)),
        "b2_img": np.ascontiguousarray(
            f32c(L0_ffn_b2).reshape(E, DC, 128).transpose(2, 0, 1).reshape(128, E * DC)),
    }
    in_maps = []
    for c in range(NCORE):
        sl = dict(shared)
        sl.update(_host_consts(c))
        xc = np.concatenate([x[0, TPB * c:TPB * (c + 1)],
                             x[1, TPB * c:TPB * (c + 1)]], axis=0)
        sl["xT"] = np.ascontiguousarray(xc.T)
        in_maps.append(sl)
    nc = _get_nc()
    res = run_bass_kernel_spmd(nc, in_maps, core_ids=list(range(NCORE)))
    out = np.empty((B, L, D), dtype=np.float32)
    for c, r in enumerate(res.results):
        oc = np.asarray(r["out"])
        out[0, TPB * c:TPB * (c + 1)] = oc[:, 0:TPB].T
        out[1, TPB * c:TPB * (c + 1)] = oc[:, TPB:T].T
    return out


# version 06
# speedup vs baseline: 1.0024x; 1.0024x over previous
import contextlib
import numpy as np
import ml_dtypes

import concourse.bass as bass
import concourse.mybir as mybir
from concourse.bass_utils import run_bass_kernel_spmd

F32 = mybir.dt.float32
BF16 = mybir.dt.bfloat16
AF = mybir.ActivationFunctionType
ALU = mybir.AluOpType
AX = mybir.AxisListType
bfl = ml_dtypes.bfloat16

B, L, D = 2, 8192, 1024
H, KV, HD = 16, 4, 64
E, M = 8, 4096
NCORE = 8
TPB = L // NCORE
T = 2 * TPB
DC = D // 128
MC = M // 128
EPS = 1e-5

ENGINES = ("sync", "tensor", "scalar", "vector", "gpsimd")


def _alpha_full():
    pos = np.arange(L, dtype=np.float64)
    s = np.asarray([2.0 ** (-0.5 * (i + 1)) for i in range(H)], dtype=np.float64)
    return np.exp(s[None, :] * (pos[:, None] - (L - 1))).astype(np.float32)


_ALPHA = _alpha_full()
LIVE = [
    [kc for kc in range(L // 128) if np.any(_ALPHA[kc * 128:(kc + 1) * 128, h] != 0.0)]
    for h in range(H)
]


def _host_consts(core):
    pos = np.arange(TPB * core, TPB * (core + 1), dtype=np.float64)
    d = np.arange(128)
    f = 1.0 / (10000.0 ** (2.0 * ((d % 64) % 32) / 64.0))
    ang = pos[None, :] * f[:, None]
    cos, sin = np.cos(ang), np.sin(ang)
    a = _ALPHA.reshape(64, 128, H).transpose(1, 0, 2).reshape(128, 64 * H)
    R = np.zeros((128, 128), dtype=np.float32)
    for o in (0, 64):
        for m in range(32):
            R[o + m + 32, o + m] = -1.0
            R[o + m, o + m + 32] = 1.0
    return {
        "cosq": (cos / 8.0).astype(bfl),
        "sinq": (sin / 8.0).astype(bfl),
        "cosk": cos.astype(bfl),
        "sink": sin.astype(bfl),
        "alpha_img": np.ascontiguousarray(a),
        "rot_b": R.astype(bfl),
        "ident_b": np.eye(128, dtype=np.float32).astype(bfl),
        "ident_f": np.eye(128, dtype=np.float32),
    }


class Sem:
    def __init__(self, nc, stack, name):
        self.h = stack.enter_context(nc.semaphore(name))
        self.n = 0

    def now(self):
        return (self.h, self.n)


class Prog:

    def __init__(self, nc):
        self.nc = nc
        self.steps = []

    def emit(self, eng, fn, waits=(), inc=None, k=1):
        ws = tuple((h, v) for (h, v) in waits if v > 0)
        ih = None
        if inc is not None:
            inc.n += k
            ih = inc.h
        self.steps.append((eng, ws, fn, ih, k))

    def wait(self, eng, waits):
        self.emit(eng, None, waits=waits)

    def flush(self):
        steps = self.steps
        self.steps = []
        with self.nc.Block() as block:
            for name in ENGINES:
                def runner(e, name=name):
                    for (eng, ws, fn, ih, k) in steps:
                        if eng != name:
                            continue
                        for (h, v) in ws:
                            e.wait_ge(h, v)
                        ins = fn(e) if fn is not None else None
                        if ih is not None:
                            assert ins is not None
                            ins.then_inc(ih, k)
                getattr(block, name)(runner)


def build_nc():
    nc = bass.Bass("TRN2", target_bir_lowering=False)

    xT = nc.dram_tensor("xT", [D, T], F32, kind="ExternalInput").ap()
    Wq = nc.dram_tensor("Wq", [D, D], F32, kind="ExternalInput").ap()
    Wk = nc.dram_tensor("Wk", [D, KV * HD], F32, kind="ExternalInput").ap()
    Wv = nc.dram_tensor("Wv", [D, KV * HD], F32, kind="ExternalInput").ap()
    Wo = nc.dram_tensor("Wo", [D, D], F32, kind="ExternalInput").ap()
    Wg = nc.dram_tensor("Wg", [E, D, M], F32, kind="ExternalInput").ap()
    W1 = nc.dram_tensor("W1", [E, D, M], F32, kind="ExternalInput").ap()
    W2 = nc.dram_tensor("W2", [E, M, D], F32, kind="ExternalInput").ap()
    Wgate_img = nc.dram_tensor("Wgate_img", [128, DC * E], F32, kind="ExternalInput").ap()
    bgate_r = nc.dram_tensor("bgate_r", [1, E], F32, kind="ExternalInput").ap()
    bg_img = nc.dram_tensor("bg_img", [128, E * MC], F32, kind="ExternalInput").ap()
    b1_img = nc.dram_tensor("b1_img", [128, E * MC], F32, kind="ExternalInput").ap()
    b2_img = nc.dram_tensor("b2_img", [128, E * DC], F32, kind="ExternalInput").ap()
    cosq = nc.dram_tensor("cosq", [128, TPB], BF16, kind="ExternalInput").ap()
    sinq = nc.dram_tensor("sinq", [128, TPB], BF16, kind="ExternalInput").ap()
    cosk = nc.dram_tensor("cosk", [128, TPB], BF16, kind="ExternalInput").ap()
    sink = nc.dram_tensor("sink", [128, TPB], BF16, kind="ExternalInput").ap()
    alpha_img = nc.dram_tensor("alpha_img", [128, 64 * H], F32, kind="ExternalInput").ap()
    rot_b = nc.dram_tensor("rot_b", [128, 128], BF16, kind="ExternalInput").ap()
    ident_b = nc.dram_tensor("ident_b", [128, 128], BF16, kind="ExternalInput").ap()
    ident_f = nc.dram_tensor("ident_f", [128, 128], F32, kind="ExternalInput").ap()
    out_d = nc.dram_tensor("out", [D, T], F32, kind="ExternalOutput").ap()

    ccki = nc.dram_tensor("ccki", [KV * HD, T], BF16)
    ccko = nc.dram_tensor("ccko", [NCORE * KV * HD, T], BF16)
    ccvi = nc.dram_tensor("ccvi", [T, KV * HD], BF16)
    ccvo = nc.dram_tensor("ccvo", [NCORE * T, KV * HD], BF16)

    wg_t = Wg.rearrange("e (dc p) m -> e p dc m", p=128)
    w1_t = W1.rearrange("e (dc p) m -> e p dc m", p=128)
    w2_t = W2.rearrange("e (mc p) d -> e p mc d", p=128)
    wq_t = Wq.rearrange("(dc p) m -> p dc m", p=128)
    wk_t = Wk.rearrange("(dc p) m -> p dc m", p=128)
    wv_t = Wv.rearrange("(dc p) m -> p dc m", p=128)
    wo_t = Wo.rearrange("(dc p) m -> p dc m", p=128)
    out_t = out_d.rearrange("(dc p) t -> dc p t", p=128)

    st = contextlib.ExitStack()
    with st:
        E_ = st.enter_context
        x2 = E_(nc.sbuf_tensor("x2", [128, DC * T], F32))
        arena = E_(nc.sbuf_tensor("arena", [128, 43008], BF16))
        af32 = E_(nc.sbuf_tensor("af32", [128, 3072], F32))
        alpha_sb = E_(nc.sbuf_tensor("alpha_sb", [128, 64 * H], F32))
        wgbuf = E_(nc.sbuf_tensor("wgbuf", [128, 2 * 1024], BF16))
        w1buf = E_(nc.sbuf_tensor("w1buf", [128, 2 * 1024], BF16))
        w2buf = E_(nc.sbuf_tensor("w2buf", [128, 3 * 1024], BF16))
        ktb = E_(nc.sbuf_tensor("ktb", [128, 4 * 128], BF16))
        vtb = E_(nc.sbuf_tensor("vtb", [128, 4 * 64], BF16))
        vhb = E_(nc.sbuf_tensor("vhb", [128, 4 * 65], BF16))
        bg_sb = E_(nc.sbuf_tensor("bg_sb", [128, E * MC], F32))
        b1_sb = E_(nc.sbuf_tensor("b1_sb", [128, E * MC], F32))
        b2_sb = E_(nc.sbuf_tensor("b2_sb", [128, E * DC], F32))
        wgate_f = E_(nc.sbuf_tensor("wgate_f", [128, DC * E], F32))
        wgate_hi = E_(nc.sbuf_tensor("wgate_hi", [128, DC * E], BF16))
        wgate_lo = E_(nc.sbuf_tensor("wgate_lo", [128, DC * E], BF16))
        wgate_d = E_(nc.sbuf_tensor("wgate_d", [128, DC * E], F32))
        bgate_f = E_(nc.sbuf_tensor("bgate_f", [1, E], F32))
        bgate_hi = E_(nc.sbuf_tensor("bgate_hi", [1, E], BF16))
        bgate_lo = E_(nc.sbuf_tensor("bgate_lo", [1, E], BF16))
        bgate_d = E_(nc.sbuf_tensor("bgate_d", [1, E], F32))
        rot_sb = E_(nc.sbuf_tensor("rot_sb", [128, 128], BF16))
        idb_sb = E_(nc.sbuf_tensor("idb_sb", [128, 128], BF16))
        idf_sb = E_(nc.sbuf_tensor("idf_sb", [128, 128], F32))
        ones_c = E_(nc.sbuf_tensor("ones_c", [128, 1], F32))
        ones_r = E_(nc.sbuf_tensor("ones_r", [1, 128], F32))
        ones_rb = E_(nc.sbuf_tensor("ones_rb", [1, 128], BF16))
        ceps = E_(nc.sbuf_tensor("ceps", [1, 1], F32))
        cm05 = E_(nc.sbuf_tensor("cm05", [1, 1], F32))
        m_row = E_(nc.sbuf_tensor("m_row", [1, 1024], F32))
        v_row = E_(nc.sbuf_tensor("v_row", [1, 1024], F32))
        w_row = E_(nc.sbuf_tensor("w_row", [1, 2048], F32))
        zsb = E_(nc.sbuf_tensor("zsb", [128, 64], F32))
        t8 = E_(nc.sbuf_tensor("t8", [128, 64], F32))
        zm8 = E_(nc.sbuf_tensor("zm8", [128, 64], F32))
        msk1 = E_(nc.sbuf_tensor("msk1", [128, 64], F32))
        msk2 = E_(nc.sbuf_tensor("msk2", [128, 64], F32))
        m18 = E_(nc.sbuf_tensor("m18", [128, 8], F32))
        m28 = E_(nc.sbuf_tensor("m28", [128, 8], F32))
        d8 = E_(nc.sbuf_tensor("d8", [128, 8], F32))
        w1h = E_(nc.sbuf_tensor("w1h", [128, 8], F32))
        w2h = E_(nc.sbuf_tensor("w2h", [128, 8], F32))
        wvec = E_(nc.sbuf_tensor("wvec", [128, 64], F32))
        wT_sb = E_(nc.sbuf_tensor("wT_sb", [64, 128], F32))

        P0 = E_(nc.psum_tensor("P0", [128, 1024], F32))
        P1 = E_(nc.psum_tensor("P1", [128, 1024], F32))
        P2 = E_(nc.psum_tensor("P2", [128, 1024], F32))
        P3 = E_(nc.psum_tensor("P3", [128, 512], F32))
        PT = E_(nc.psum_tensor("PT", [128, 128], BF16))

        names = ("IN", "PE", "SC", "DV", "PL", "VB", "SB", "CC", "ST", "KVC",
                 "WGA", "WGB", "W1A", "W1B", "W2A", "W2B", "W2C",
                 "KVS0", "KVS1", "KVS2", "KVS3", "WRA", "WRB", "AWA", "AWB")
        S = {n: Sem(nc, st, "s" + n) for n in names}

        P = Prog(nc)

        A_H, A_XT, A_LO, A_T1 = 0, 32768, 40960, 41984
        A_Q, A_OC, A_PS, A_KC, A_VT, A_VF, A_KB = 0, 4096, 8192, 9216, 13312, 17408, 18432
        A_COS = 19456
        A_KH = 23552
        A_VH = 9216
        F_T, F_U, F_XSQ = 0, 1024, 2048

        def x2s(dc, c0, w):
            return x2[:, dc * T + c0: dc * T + c0 + w]

        def dma(eng, sem, outap, inap, waits=()):
            P.emit(eng, lambda e, o=outap, i=inap: e.dma_start(out=o, in_=i),
                   waits=waits, inc=sem, k=16)

        dma("sync", S["IN"], x2[:].rearrange("p (dc t) -> p dc t", dc=DC),
            xT.rearrange("(dc p) t -> p dc t", p=128))
        dma("sync", S["IN"], alpha_sb[:], alpha_img)
        dma("sync", S["IN"], rot_sb[:], rot_b)
        dma("sync", S["IN"], idb_sb[:], ident_b)
        dma("sync", S["IN"], idf_sb[:], ident_f)
        dma("sync", S["IN"], bg_sb[:], bg_img)
        dma("sync", S["IN"], b1_sb[:], b1_img)
        dma("sync", S["IN"], b2_sb[:], b2_img)
        dma("sync", S["IN"], wgate_f[:], Wgate_img)
        dma("sync", S["IN"], bgate_f[:], bgate_r)
        all_in = S["IN"].now()

        def setup_v(v):
            v.memset(ones_c[:], 1.0)
            v.memset(ones_r[:], 1.0)
            v.memset(ones_rb[:], 1.0)
            v.memset(ceps[:], EPS)
            return v.memset(cm05[:], -0.5)
        P.emit("vector", setup_v, waits=[all_in], inc=S["VB"])

        P.emit("gpsimd", lambda e: e.tensor_copy(out=wgate_hi[:], in_=wgate_f[:]),
               waits=[all_in], inc=S["PL"])
        P.emit("gpsimd", lambda e: e.tensor_copy(out=bgate_hi[:], in_=bgate_f[:]),
               inc=S["PL"])
        P.emit("vector", lambda e: e.tensor_tensor(
            out=wgate_d[:], in0=wgate_f[:], in1=wgate_hi[:], op=ALU.subtract),
            waits=[S["PL"].now(), S["VB"].now()], inc=S["VB"])
        P.emit("vector", lambda e: e.tensor_tensor(
            out=bgate_d[:], in0=bgate_f[:], in1=bgate_hi[:], op=ALU.subtract),
            waits=[S["VB"].now()], inc=S["VB"])
        P.emit("gpsimd", lambda e: e.tensor_copy(out=wgate_lo[:], in_=wgate_d[:]),
               waits=[S["VB"].now()], inc=S["PL"])
        P.emit("gpsimd", lambda e: e.tensor_copy(out=bgate_lo[:], in_=bgate_d[:]),
               inc=S["PL"])
        P.flush()

        def emit_ln(c0, width, gate=False):
            nh = width // 512
            for n in range(nh):
                for dc in range(DC):
                    xs = x2s(dc, c0 + n * 512, 512)
                    sq = af32[:, F_XSQ + (dc % 2) * 512: F_XSQ + (dc % 2) * 512 + 512]
                    P.emit("scalar", lambda e, o=sq, i=xs: e.activation(
                        out=o, in_=i, func=AF.Square),
                        waits=[S["PE"].now()], inc=S["SC"])
                    P.emit("tensor", lambda e, i=xs, n=n, dc=dc: e.matmul(
                        P0[0:1, n * 512:(n + 1) * 512], ones_c[:], i,
                        start=(dc == 0), stop=(dc == DC - 1), skip_group_check=True))
                    P.emit("tensor", lambda e, q=sq, n=n, dc=dc: e.matmul(
                        P1[0:1, n * 512:(n + 1) * 512], ones_c[:], q,
                        start=(dc == 0), stop=(dc == DC - 1), skip_group_check=True),
                        waits=[S["SC"].now()], inc=S["PE"])
            stats = S["PE"].now()
            mr = m_row[:, 0:width]
            vr = v_row[:, 0:width]
            qr = af32[0:1, F_XSQ:F_XSQ + width]
            P.emit("vector", lambda e: e.tensor_scalar(
                out=mr, in0=P0[0:1, 0:width], scalar1=1.0 / D, scalar2=None,
                op0=ALU.mult), waits=[stats, S["VB"].now()], inc=S["VB"])
            P.emit("vector", lambda e: e.tensor_scalar(
                out=vr, in0=P1[0:1, 0:width], scalar1=1.0 / D, scalar2=None,
                op0=ALU.mult), waits=[S["VB"].now()], inc=S["VB"])
            P.emit("vector", lambda e: e.scalar_tensor_tensor(
                out=qr, in0=mr, scalar=-1.0, in1=mr, op0=ALU.mult, op1=ALU.mult),
                waits=[S["VB"].now()], inc=S["VB"])
            P.emit("vector", lambda e: e.tensor_tensor(
                out=vr, in0=vr, in1=qr, op=ALU.add),
                waits=[S["VB"].now()], inc=S["VB"])
            row_dv = S["VB"].now()
            P.emit("scalar", lambda e: e.activation(
                out=vr, in_=vr, func=AF.Ln, bias=ceps[:]),
                waits=[row_dv, S["SB"].now()], inc=S["SB"])
            P.emit("scalar", lambda e: e.activation(
                out=vr, in_=vr, func=AF.Exp, scale=cm05[:]),
                waits=[S["SB"].now()], inc=S["SB"])
            P.emit("vector", lambda e: e.tensor_tensor(
                out=mr, in0=mr, in1=vr, op=ALU.mult),
                waits=[S["SB"].now(), S["VB"].now()], inc=S["VB"])
            rows = S["VB"].now()
            for n in range(nh):
                P.emit("tensor", lambda e, n=n: e.matmul(
                    P0[:, n * 512:(n + 1) * 512], ones_r[:],
                    v_row[:, n * 512:(n + 1) * 512], start=True, stop=True),
                    waits=[rows] if n == 0 else ())
                P.emit("tensor", lambda e, n=n: e.matmul(
                    P1[:, n * 512:(n + 1) * 512], ones_r[:],
                    m_row[:, n * 512:(n + 1) * 512], start=True, stop=True),
                    inc=S["PE"])
            bcast = S["PE"].now()
            for dc in range(DC):
                tbuf = af32[:, F_T:F_T + width]
                xtsl = arena[:, A_XT + dc * width: A_XT + (dc + 1) * width]
                losl = arena[:, A_LO: A_LO + width]
                P.emit("vector", lambda e, dc=dc, tb=tbuf: e.tensor_tensor(
                    out=tb, in0=x2s(dc, c0, width), in1=P0[:, 0:width], op=ALU.mult),
                    waits=[bcast] + ([S["PE"].now()] if gate else []), inc=S["DV"])
                P.emit("vector", lambda e, tb=tbuf: e.tensor_tensor(
                    out=tb, in0=tb, in1=P1[:, 0:width], op=ALU.subtract),
                    inc=S["DV"])
                P.emit("vector", lambda e, o=xtsl, tb=tbuf: e.tensor_copy(
                    out=o, in_=tb), inc=S["DV"])
                if gate:
                    P.emit("vector", lambda e, o=losl, tb=tbuf, x=xtsl: e.tensor_tensor(
                        out=o, in0=tb, in1=x, op=ALU.subtract), inc=S["DV"])
                    g_dv = S["DV"].now()
                    ntch = width // 128
                    for tch in range(ntch):
                        xsl = arena[:, A_XT + dc * width + tch * 128:
                                    A_XT + dc * width + tch * 128 + 128]
                        lsl = arena[:, A_LO + tch * 128: A_LO + tch * 128 + 128]
                        zr = P3[:, tch * 8:(tch + 1) * 8]
                        for pi, (lhs, rhs) in enumerate((
                                (xsl, wgate_hi), (xsl, wgate_lo), (lsl, wgate_hi))):
                            P.emit("tensor", lambda e, zr=zr, l=lhs, r=rhs, dc=dc,
                                   pi=pi, tch=tch: e.matmul(
                                zr, l, r[:, dc * 8:(dc + 1) * 8],
                                start=(dc == 0 and pi == 0 and tch == 0), stop=False,
                                skip_group_check=True),
                                waits=[g_dv] if (pi == 0) else (), inc=S["PE"])
            xt_ready = S["DV"].now()
            wT_ready = None
            if gate:
                ntch = width // 128
                for tch in range(ntch):
                    zr = P3[:, tch * 8:(tch + 1) * 8]
                    P.emit("tensor", lambda e, zr=zr: e.matmul(
                        zr, ones_rb[:], bgate_hi[:], start=False, stop=False,
                        skip_group_check=True))
                    P.emit("tensor", lambda e, zr=zr, tch=tch: e.matmul(
                        zr, ones_rb[:], bgate_lo[:], start=False, stop=True,
                        skip_group_check=True), inc=S["PE"])
                z_done = S["PE"].now()
                nt = ntch * 8

                def vb(fn, extra=()):
                    P.emit("vector", fn, waits=[S["VB"].now()] + list(extra),
                           inc=S["VB"])

                z3 = zsb[:, 0:nt].rearrange("p (t e) -> p t e", e=8)
                zm3 = zm8[:, 0:nt].rearrange("p (t e) -> p t e", e=8)
                mk13 = msk1[:, 0:nt].rearrange("p (t e) -> p t e", e=8)
                mk23 = msk2[:, 0:nt].rearrange("p (t e) -> p t e", e=8)
                m1b = m18[:, 0:ntch].rearrange("p (t e) -> p t e", e=1) \
                    .broadcast_to([128, ntch, 8])
                m2b = m28[:, 0:ntch].rearrange("p (t e) -> p t e", e=1) \
                    .broadcast_to([128, ntch, 8])
                w1b = w1h[:, 0:ntch].rearrange("p (t e) -> p t e", e=1) \
                    .broadcast_to([128, ntch, 8])
                w2b = w2h[:, 0:ntch].rearrange("p (t e) -> p t e", e=1) \
                    .broadcast_to([128, ntch, 8])
                vb(lambda e: e.tensor_copy(out=zsb[:, 0:nt], in_=P3[:, 0:nt]),
                   extra=[z_done])
                vb(lambda e: e.tensor_reduce(out=m18[:, 0:ntch], in_=z3, axis=AX.X,
                                             op=ALU.max))
                vb(lambda e: e.tensor_tensor(out=mk13, in0=z3, in1=m1b,
                                             op=ALU.is_equal))
                vb(lambda e: e.tensor_scalar(out=t8[:, 0:nt], in0=msk1[:, 0:nt],
                                             scalar1=-1e30, scalar2=None, op0=ALU.mult))
                vb(lambda e: e.tensor_tensor(out=zm8[:, 0:nt], in0=zsb[:, 0:nt],
                                             in1=t8[:, 0:nt], op=ALU.add))
                vb(lambda e: e.tensor_reduce(out=m28[:, 0:ntch], in_=zm3, axis=AX.X,
                                             op=ALU.max))
                vb(lambda e: e.tensor_tensor(out=mk23, in0=zm3, in1=m2b,
                                             op=ALU.is_equal))
                vb(lambda e: e.tensor_tensor(out=d8[:, 0:ntch], in0=m18[:, 0:ntch],
                                             in1=m28[:, 0:ntch], op=ALU.subtract))
                P.emit("scalar", lambda e: e.activation(
                    out=w1h[:, 0:ntch], in_=d8[:, 0:ntch], func=AF.Sigmoid),
                    waits=[S["VB"].now(), S["SB"].now()], inc=S["SB"])
                vb(lambda e: e.tensor_scalar(out=w1h[:, 0:ntch], in0=w1h[:, 0:ntch],
                                             scalar1=0.5, scalar2=None, op0=ALU.mult),
                   extra=[S["SB"].now()])
                vb(lambda e: e.tensor_scalar(out=w2h[:, 0:ntch], in0=w1h[:, 0:ntch],
                                             scalar1=-1.0, scalar2=0.5,
                                             op0=ALU.mult, op1=ALU.add))
                vb(lambda e: e.tensor_tensor(out=mk13, in0=mk13, in1=w1b, op=ALU.mult))
                vb(lambda e: e.tensor_tensor(out=mk23, in0=mk23, in1=w2b, op=ALU.mult))
                vb(lambda e: e.tensor_tensor(out=wvec[:, 0:nt], in0=msk1[:, 0:nt],
                                             in1=msk2[:, 0:nt], op=ALU.add))
                wv_done = S["VB"].now()
                P.emit("tensor", lambda e: e.transpose(
                    P3[0:nt, 0:128], wvec[:, 0:nt], idf_sb[:]),
                    waits=[wv_done], inc=S["PE"])
                P.emit("vector", lambda e: e.tensor_copy(
                    out=wT_sb[0:nt, :], in_=P3[0:nt, 0:128]),
                    waits=[S["PE"].now(), S["VB"].now()], inc=S["VB"])
                wT_ready = S["VB"].now()
            return xt_ready, wT_ready

        wr_state = [None, None]

        def emit_ffn(c0, nexp=E, sub='all'):
            xt_ready, wT_ready = emit_ln(c0, 1024, gate=True)
            wg_cons = [None, None]
            w1_cons = [None, None]
            w2_cons = [None, None, None]
            wg_fill = [None, None]
            w1_fill = [None, None]
            w2_fill = [None, None, None]
            wr_fill = [None, None]
            p3_free = S["VB"].now()
            silu_done = None
            h_done = None
            u_done = [None, None]
            sub_r = sub in ('all', 'pa', 'wbs') or 'r' in sub
            sub_d = sub in ('all', 'pa', 'wbs') or 'd' in sub
            sub_m = sub in ('all', 'pa', 'wbs') or 'm' in sub
            sub_s = sub in ('all', 'pa', 'wbs') or 's' in sub
            sub_h = sub in ('all', 'pa', 'wbs') or 'h' in sub
            for e in range(nexp):
                ws = e % 2
                sem = S["WRA" if ws == 0 else "WRB"]
                if sub_r:
                  for n in range(2):
                    for tq in range(4):
                        src = wT_sb[(4 * n + tq) * 8 + e:(4 * n + tq) * 8 + e + 1, :]
                        dst = w_row[0:1, ws * 1024 + n * 512 + tq * 128:
                                    ws * 1024 + n * 512 + tq * 128 + 128]
                        dma("sync", sem, dst, src,
                            waits=[wT_ready] + ([wr_state[ws]] if wr_state[ws] else []))
                wr_fill[ws] = sem.now()
                for mc in range(MC):
                    slot = mc % 2
                    gsem = S["WGA" if slot == 0 else "WGB"]
                    osem = S["W1A" if slot == 0 else "W1B"]
                    if not sub_d:
                        continue
                    dma("gpsimd", gsem,
                        wgbuf[:, slot * 1024:(slot + 1) * 1024]
                        .rearrange("p (dc j) -> p dc j", dc=DC),
                        wg_t[e, :, :, mc * 128:(mc + 1) * 128],
                        waits=[wg_cons[slot]] if wg_cons[slot] else ())
                    wg_fill[slot] = gsem.now()
                    dma("gpsimd", osem,
                        w1buf[:, slot * 1024:(slot + 1) * 1024]
                        .rearrange("p (dc j) -> p dc j", dc=DC),
                        w1_t[e, :, :, mc * 128:(mc + 1) * 128],
                        waits=[w1_cons[slot]] if w1_cons[slot] else ())
                    w1_fill[slot] = osem.now()
                    if not sub_m:
                        continue
                    wA = [wg_fill[slot], xt_ready]
                    if silu_done is not None:
                        wA.append(silu_done)
                    for dc in range(DC):
                        for n in range(2):
                            P.emit("tensor", lambda e_, dc=dc, n=n, slot=slot: e_.matmul(
                                P0[:, n * 512:(n + 1) * 512],
                                wgbuf[:, slot * 1024 + dc * 128:slot * 1024 + (dc + 1) * 128],
                                arena[:, A_XT + dc * 1024 + n * 512:
                                      A_XT + dc * 1024 + n * 512 + 512],
                                start=(dc == 0), stop=(dc == DC - 1),
                                skip_group_check=True),
                                waits=wA if (n == 0 and dc == 0) else (),
                                inc=S["PE"] if (n == 1 and dc == DC - 1) else None)
                    a_done = S["PE"].now()
                    wB = [w1_fill[slot]]
                    if h_done is not None:
                        wB.append(h_done)
                    for dc in range(DC):
                        for n in range(2):
                            P.emit("tensor", lambda e_, dc=dc, n=n, slot=slot: e_.matmul(
                                P1[:, n * 512:(n + 1) * 512],
                                w1buf[:, slot * 1024 + dc * 128:slot * 1024 + (dc + 1) * 128],
                                arena[:, A_XT + dc * 1024 + n * 512:
                                      A_XT + dc * 1024 + n * 512 + 512],
                                start=(dc == 0), stop=(dc == DC - 1),
                                skip_group_check=True),
                                waits=wB if (n == 0 and dc == 0) else (),
                                inc=S["PE"] if (n == 1 and dc == DC - 1) else None)
                    b_done = S["PE"].now()
                    wg_cons[slot] = a_done
                    w1_cons[slot] = b_done
                    if not sub_s:
                        silu_done = a_done
                        h_done = b_done
                        continue
                    bcol = bg_sb[:, e * MC + mc: e * MC + mc + 1]
                    P.emit("scalar", lambda e_, bc=bcol: e_.activation(
                        out=arena[:, A_T1:A_T1 + 1024], in_=P0[:, 0:1024],
                        func=AF.Silu, bias=bc),
                        waits=[a_done, S["DV"].now()], inc=S["SC"])
                    silu_done = S["SC"].now()
                    if not sub_h:
                        h_done = b_done
                        continue
                    b1col = b1_sb[:, e * MC + mc: e * MC + mc + 1]
                    P.emit("vector", lambda e_, mc=mc, bc=b1col: e_.scalar_tensor_tensor(
                        out=arena[:, A_H + mc * 1024:A_H + (mc + 1) * 1024],
                        in0=P1[:, 0:1024], scalar=bc,
                        in1=arena[:, A_T1:A_T1 + 1024], op0=ALU.add, op1=ALU.mult),
                        waits=[b_done, silu_done], inc=S["DV"])
                    h_done = S["DV"].now()
                if sub == 'pa' or sub.startswith('x'):
                    continue
                wbs_done = None
                for n in range(2):
                    P.emit("tensor", lambda e_, ws=ws, n=n: e_.matmul(
                        P3[:, 0:512], ones_r[:],
                        w_row[0:1, ws * 1024 + n * 512: ws * 1024 + n * 512 + 512],
                        start=True, stop=True),
                        waits=[wr_fill[ws], p3_free], inc=S["PE"])
                    P.emit("vector", lambda e_, n=n: e_.tensor_copy(
                        out=af32[:, F_T + n * 512:F_T + n * 512 + 512],
                        in_=P3[:, 0:512]),
                        waits=[S["PE"].now()], inc=S["DV"])
                    p3_free = S["DV"].now()
                wbs_done = S["DV"].now()
                wr_state[ws] = S["PE"].now()
                if sub == 'wbs':
                    continue
                w2_fill_by = {}

                def w2dma(qg):
                    s3 = qg % 3
                    qsem = S[("W2A", "W2B", "W2C")[s3]]
                    dma("gpsimd", qsem,
                        w2buf[:, s3 * 1024:(s3 + 1) * 1024]
                        .rearrange("p (mq j) -> p mq j", mq=8),
                        w2_t[e, :, (qg % 4) * 8:(qg % 4) * 8 + 8,
                             (qg // 4) * 128:(qg // 4) * 128 + 128],
                        waits=[w2_cons[s3]] if w2_cons[s3] else ())
                    w2_fill_by[qg] = qsem.now()

                for qg in range(3):
                    w2dma(qg)
                next_qg = 3
                for dout in range(DC):
                    for j in range(MC):
                        qg = dout * 4 + j // 8
                        s3 = qg % 3
                        for n in range(2):
                            wts = []
                            if n == 0 and j % 8 == 0:
                                wts.append(w2_fill_by[qg])
                            if n == 0 and j == 0:
                                wts.append(h_done)
                                if u_done[0] is not None:
                                    wts.append(u_done[1])
                            P.emit("tensor", lambda e_, j=j, s3=s3, n=n: e_.matmul(
                                P2[:, n * 512:(n + 1) * 512],
                                w2buf[:, s3 * 1024 + (j % 8) * 128:
                                      s3 * 1024 + (j % 8) * 128 + 128],
                                arena[:, A_H + j * 1024 + n * 512:
                                      A_H + j * 1024 + n * 512 + 512],
                                start=(j == 0), stop=(j == MC - 1),
                                skip_group_check=True),
                                waits=wts,
                                inc=S["PE"] if (n == 1 and j % 8 == 7) else None)
                        if j % 8 == 7:
                            w2_cons[s3] = S["PE"].now()
                            if next_qg < 32:
                                w2dma(next_qg)
                                next_qg += 1
                    y_done = S["PE"].now()
                    b2col = b2_sb[:, e * DC + dout: e * DC + dout + 1]
                    for n in range(2):
                        P.emit("vector", lambda e_, n=n, bc=b2col: e_.scalar_tensor_tensor(
                            out=af32[:, F_U:F_U + 512],
                            in0=P2[:, n * 512:(n + 1) * 512], scalar=bc,
                            in1=af32[:, F_T + n * 512:F_T + n * 512 + 512],
                            op0=ALU.add, op1=ALU.mult),
                            waits=[y_done, wbs_done], inc=S["DV"])
                        P.emit("vector", lambda e_, dout=dout, n=n: e_.tensor_tensor(
                            out=x2s(dout, c0 + n * 512, 512),
                            in0=x2s(dout, c0 + n * 512, 512),
                            in1=af32[:, F_U:F_U + 512], op=ALU.add),
                            inc=S["DV"])
                    u_done[0] = u_done[1] = S["DV"].now()

        def emit_kv():
            for ap_, off_ in ((cosq, 0), (sinq, TPB), (cosk, 2 * TPB), (sink, 3 * TPB)):
                dma("sync", S["IN"], arena[:, A_COS + off_:A_COS + off_ + TPB], ap_)
            cos_ready = S["IN"].now()
            for tc in range(2):
                xt_ready, _ = emit_ln(tc * 1024, 1024, gate=False)
                for vc in range(2):
                    asem = S["AWA"]
                    dma("gpsimd", asem,
                        wgbuf[:, 0:1024].rearrange("p (dc j) -> p dc j", dc=DC),
                        wk_t[:, :, vc * 128:(vc + 1) * 128],
                        waits=[S["PE"].now()])
                    kw = asem.now()
                    for dc in range(DC):
                        for n in range(2):
                            P.emit("tensor", lambda e_, dc=dc, n=n: e_.matmul(
                                P1[:, n * 512:(n + 1) * 512],
                                wgbuf[:, dc * 128:(dc + 1) * 128],
                                arena[:, A_XT + dc * 1024 + n * 512:
                                      A_XT + dc * 1024 + n * 512 + 512],
                                start=(dc == 0), stop=(dc == DC - 1),
                                skip_group_check=True),
                                waits=[kw, xt_ready, S["DV"].now()]
                                if (n == 0 and dc == 0) else (),
                                inc=S["PE"] if (n == 1 and dc == DC - 1) else None)
                    kproj = S["PE"].now()
                    P.emit("vector", lambda e_: e_.tensor_copy(
                        out=arena[:, A_KB:A_KB + 1024], in_=P1[:, 0:1024]),
                        waits=[kproj], inc=S["DV"])
                    kb_done = S["DV"].now()
                    for n in range(2):
                        P.emit("tensor", lambda e_, n=n: e_.matmul(
                            P2[:, n * 512:(n + 1) * 512], rot_sb[:],
                            arena[:, A_KB + n * 512:A_KB + n * 512 + 512],
                            start=True, stop=True),
                            waits=[kb_done, S["DV"].now()] if n == 0 else (),
                            inc=S["PE"] if n == 1 else None)
                    krot = S["PE"].now()
                    P.emit("vector", lambda e_: e_.tensor_tensor(
                        out=af32[:, F_T:F_T + 1024], in0=P1[:, 0:1024],
                        in1=arena[:, A_COS + 2 * TPB:A_COS + 3 * TPB], op=ALU.mult),
                        waits=[krot, cos_ready], inc=S["DV"])
                    P.emit("vector", lambda e_: e_.tensor_tensor(
                        out=af32[:, F_U:F_U + 1024], in0=P2[:, 0:1024],
                        in1=arena[:, A_COS + 3 * TPB:A_COS + 4 * TPB], op=ALU.mult),
                        inc=S["DV"])
                    P.emit("vector", lambda e_, tc=tc, vc=vc: e_.tensor_tensor(
                        out=arena[:, A_KC + vc * 2048 + tc * 1024:
                                  A_KC + vc * 2048 + tc * 1024 + 1024],
                        in0=af32[:, F_T:F_T + 1024], in1=af32[:, F_U:F_U + 1024],
                        op=ALU.add), inc=S["DV"])
                    kf_done = S["DV"].now()
                    dma("gpsimd", S["AWB"],
                        w1buf[:, 0:1024].rearrange("p (dc j) -> p dc j", dc=DC),
                        wv_t[:, :, vc * 128:(vc + 1) * 128],
                        waits=[S["PE"].now()])
                    vw = S["AWB"].now()
                    for dc in range(DC):
                        for n in range(2):
                            P.emit("tensor", lambda e_, dc=dc, n=n: e_.matmul(
                                P1[:, n * 512:(n + 1) * 512],
                                w1buf[:, dc * 128:(dc + 1) * 128],
                                arena[:, A_XT + dc * 1024 + n * 512:
                                      A_XT + dc * 1024 + n * 512 + 512],
                                start=(dc == 0), stop=(dc == DC - 1),
                                skip_group_check=True),
                                waits=[vw, kf_done] if (n == 0 and dc == 0) else (),
                                inc=S["PE"] if (n == 1 and dc == DC - 1) else None)
                    vproj = S["PE"].now()
                    P.emit("vector", lambda e_: e_.tensor_copy(
                        out=arena[:, A_VF:A_VF + 1024], in_=P1[:, 0:1024]),
                        waits=[vproj], inc=S["DV"])
                    vf_done = S["DV"].now()
                    for tch in range(8):
                        P.emit("tensor", lambda e_, tch=tch: e_.transpose(
                            PT[:, 0:128],
                            arena[:, A_VF + tch * 128:A_VF + tch * 128 + 128],
                            idb_sb[:]),
                            waits=[vf_done, S["DV"].now()], inc=S["PE"])
                        P.emit("vector", lambda e_, tc=tc, vc=vc, tch=tch: e_.tensor_copy(
                            out=arena[:, A_VT + (tc * 8 + tch) * 256 + vc * 128:
                                      A_VT + (tc * 8 + tch) * 256 + vc * 128 + 128],
                            in_=PT[:, 0:128]),
                            waits=[S["PE"].now()], inc=S["DV"])
            vt_done = S["DV"].now()
            dma("sync", S["KVC"],
                ccki.rearrange("(c p) t -> p c t", p=128),
                arena[:, A_KC:A_KC + 4096].rearrange("p (c t) -> p c t", c=2),
                waits=[vt_done])
            dma("sync", S["KVC"],
                ccvi.rearrange("(tc p) f -> p tc f", p=128),
                arena[:, A_VT:A_VT + 4096].rearrange("p (tc f) -> p tc f", tc=16),
                waits=[vt_done])
            kvc = S["KVC"].now()
            P.emit("gpsimd", lambda e_: e_.collective_compute(
                "AllGather", ALU.bypass, replica_groups=[list(range(NCORE))],
                ins=[ccki.ap().opt()], outs=[ccko.ap().opt()]),
                waits=[kvc], inc=S["CC"])
            P.emit("gpsimd", lambda e_: e_.collective_compute(
                "AllGather", ALU.bypass, replica_groups=[list(range(NCORE))],
                ins=[ccvi.ap().opt()], outs=[ccvo.ap().opt()]),
                inc=S["CC"])
            return S["CC"].now()

        def emit_qc(qc, cc_ready):
            batch, poff = qc // 2, (qc % 2) * 512
            c0 = qc * 512
            xt_ready, _ = emit_ln(c0, 512, gate=False)
            for ch in range(DC):
                asem = S["AWA" if ch % 2 == 0 else "AWB"]
                buf = wgbuf if ch % 2 == 0 else w1buf
                dma("gpsimd", asem,
                    buf[:, 0:1024].rearrange("p (dc j) -> p dc j", dc=DC),
                    wq_t[:, :, ch * 128:(ch + 1) * 128],
                    waits=[S["PE"].now()])
                qw_ = asem.now()
                for dc in range(DC):
                    P.emit("tensor", lambda e_, dc=dc, buf=buf: e_.matmul(
                        P1[:, 0:512], buf[:, dc * 128:(dc + 1) * 128],
                        arena[:, A_XT + dc * 512:A_XT + dc * 512 + 512],
                        start=(dc == 0), stop=(dc == DC - 1),
                        skip_group_check=True),
                        waits=[qw_, xt_ready, S["DV"].now()] if dc == 0 else (),
                        inc=S["PE"] if dc == DC - 1 else None)
                qproj = S["PE"].now()
                P.emit("vector", lambda e_: e_.tensor_copy(
                    out=arena[:, A_KB:A_KB + 512], in_=P1[:, 0:512]),
                    waits=[qproj], inc=S["DV"])
                qb_done = S["DV"].now()
                P.emit("tensor", lambda e_: e_.matmul(
                    P3[:, 0:512], rot_sb[:], arena[:, A_KB:A_KB + 512],
                    start=True, stop=True),
                    waits=[qb_done, S["DV"].now()], inc=S["PE"])
                qrot = S["PE"].now()
                P.emit("vector", lambda e_: e_.tensor_tensor(
                    out=af32[:, F_T:F_T + 512], in0=P1[:, 0:512],
                    in1=arena[:, A_COS + poff:A_COS + poff + 512], op=ALU.mult),
                    waits=[qrot], inc=S["DV"])
                P.emit("vector", lambda e_: e_.tensor_tensor(
                    out=af32[:, F_U:F_U + 512], in0=P3[:, 0:512],
                    in1=arena[:, A_COS + TPB + poff:A_COS + TPB + poff + 512], op=ALU.mult),
                    inc=S["DV"])
                P.emit("vector", lambda e_, ch=ch: e_.tensor_tensor(
                    out=arena[:, A_Q + ch * 512:A_Q + (ch + 1) * 512],
                    in0=af32[:, F_T:F_T + 512], in1=af32[:, F_U:F_U + 512],
                    op=ALU.add), inc=S["DV"])
            q_ready = S["DV"].now()
            p3_free = S["DV"].now()
            kv_fill = {}
            o_evac = None

            def dma_kv(h):
                par = h % 2
                base = par * 64
                kvh = h // 4
                kc0 = LIVE[h][0]
                ssem = S["KVS0" if par == 0 else "KVS1"]
                wts = [cc_ready]
                if kv_cons[par] is not None:
                    wts += kv_cons[par]
                r0 = kc0 // 8
                for r in range(r0, 8):
                    jlo = max(kc0 - 8 * r, 0)
                    dma("sync", ssem,
                        arena[base:base + 64,
                              A_KH + (8 * r + jlo - kc0) * 128:
                              A_KH + (8 * r + 8 - kc0) * 128],
                        ccko.ap()[256 * r + 128 * (kvh // 2) + 64 * (kvh % 2):
                                  256 * r + 128 * (kvh // 2) + 64 * (kvh % 2) + 64,
                                  batch * 1024 + 128 * jlo:(batch + 1) * 1024],
                        waits=wts)
                    dma("sync", ssem,
                        arena[:, A_VH + par * 4096 + (8 * r + jlo - kc0) * 64:
                              A_VH + par * 4096 + (8 * r + 8 - kc0) * 64]
                        .rearrange("p (j f) -> p j f", f=64),
                        ccvo.ap().rearrange("(rb j p) f -> rb p j f", p=128, j=16)
                        [r, :, 8 * batch + jlo:8 * batch + 8,
                         kvh * 64:(kvh + 1) * 64],
                        waits=wts)
                kv_fill[h] = ssem.now()

            kv_cons = [None, None]
            dma_kv(0)
            dma_kv(1)
            ps_guard = {}
            for h in range(H):
                ch, base, kvh = h // 2, (h % 2) * 64, h // 4
                par = h % 2
                live = LIVE[h]
                kc0 = live[0]
                qk_done = {}
                exp_done = {}
                if h > 0:
                    exp_done[-2] = prev_exps[0]
                    exp_done[-1] = prev_exps[1]
                vh_done = {}
                pv_done_i = {}
                for idx, kc in enumerate(live):
                    slot = idx % 4
                    acol = alpha_sb[:, kc * H + h: kc * H + h + 1]
                    vh_w = [kv_fill[h]]
                    if idx >= 4:
                        vh_w.append(pv_done_i[idx - 4])
                    elif o_evac is not None:
                        vh_w.append(prev_pv)
                    P.emit("vector", lambda e_, slot=slot, ac=acol, kc=kc, par=par,
                           kc0=kc0: e_.tensor_scalar(
                        out=vhb[:, slot * 65:slot * 65 + 64],
                        in0=arena[:, A_VH + par * 4096 + (kc - kc0) * 64:
                                  A_VH + par * 4096 + (kc - kc0) * 64 + 64],
                        scalar1=ac, scalar2=None, op0=ALU.mult),
                        waits=vh_w, inc=S["DV"])
                    P.emit("vector", lambda e_, slot=slot, ac=acol: e_.tensor_copy(
                        out=vhb[:, slot * 65 + 64:slot * 65 + 65], in_=ac),
                        inc=S["DV"])
                    vh_done[idx] = S["DV"].now()
                    half = idx % 2
                    wts = [kv_fill[h], q_ready]
                    if (idx - 2) in exp_done:
                        wts.append(exp_done[idx - 2])
                    P.emit("tensor", lambda e_, half=half, base=base, ch=ch, kc=kc,
                           kc0=kc0: e_.matmul(
                        P0[:, half * 512:(half + 1) * 512],
                        arena[base:base + 64, A_KH + (kc - kc0) * 128:
                              A_KH + (kc - kc0) * 128 + 128],
                        arena[base:base + 64, A_Q + ch * 512:A_Q + (ch + 1) * 512],
                        start=True, stop=True, skip_group_check=True),
                        waits=wts, inc=S["PE"])
                    qk_done[idx] = S["PE"].now()
                    wts = [qk_done[idx]]
                    if idx >= 2:
                        wts.append(S["PE"].now())
                    elif h > 0:
                        wts.append(prev_pv)
                    P.emit("scalar", lambda e_, half=half: e_.activation(
                        out=arena[:, A_PS + half * 512:A_PS + (half + 1) * 512],
                        in_=P0[:, half * 512:(half + 1) * 512], func=AF.Exp),
                        waits=wts, inc=S["SC"])
                    exp_done[idx] = S["SC"].now()
                    wts = [exp_done[idx], vh_done[idx]]
                    if idx == 0 and o_evac is not None:
                        wts.append(o_evac)
                    P.emit("tensor", lambda e_, slot=slot, half=half, idx=idx,
                           n_live=len(live): e_.matmul(
                        P2[0:65, 0:512], vhb[:, slot * 65:(slot + 1) * 65],
                        arena[:, A_PS + half * 512:A_PS + (half + 1) * 512],
                        start=(idx == 0), stop=(idx == n_live - 1),
                        skip_group_check=True),
                        waits=wts, inc=S["PE"])
                    pv_done_i[idx] = S["PE"].now()
                kv_cons[par] = [qk_done[len(live) - 1], vh_done[len(live) - 1]]
                if h + 2 < H:
                    dma_kv(h + 2)
                pv_done = S["PE"].now()
                prev_pv = pv_done
                nl = len(live)
                prev_exps = (exp_done[nl - 2] if nl >= 2 else exp_done[nl - 1],
                             exp_done[nl - 1])
                P.emit("vector", lambda e_: e_.reciprocal(
                    out=af32[0:1, F_U:F_U + 512], in_=P2[64:65, 0:512]),
                    waits=[pv_done], inc=S["DV"])
                P.emit("tensor", lambda e_: e_.matmul(
                    P3[0:64, 0:512], ones_r[:, 0:64], af32[0:1, F_U:F_U + 512],
                    start=True, stop=True),
                    waits=[S["DV"].now(), p3_free], inc=S["PE"])
                P.emit("vector", lambda e_: e_.tensor_copy(
                    out=af32[0:64, F_T:F_T + 512], in_=P3[0:64, 0:512]),
                    waits=[S["PE"].now()], inc=S["DV"])
                p3_free = S["DV"].now()
                P.emit("vector", lambda e_, base=base, ch=ch: e_.tensor_tensor(
                    out=arena[base:base + 64, A_OC + ch * 512:A_OC + (ch + 1) * 512],
                    in0=P2[0:64, 0:512], in1=af32[0:64, F_T:F_T + 512], op=ALU.mult),
                    inc=S["DV"])
                o_evac = S["DV"].now()
            x2add = [None, None]
            for dout in range(DC):
                asem = S["AWA" if dout % 2 == 0 else "AWB"]
                buf = wgbuf if dout % 2 == 0 else w1buf
                dma("gpsimd", asem,
                    buf[:, 0:1024].rearrange("p (dc j) -> p dc j", dc=DC),
                    wo_t[:, :, dout * 128:(dout + 1) * 128],
                    waits=[S["PE"].now()])
                ww = asem.now()
                half = dout % 2
                for fc in range(DC):
                    wts = [ww, o_evac] if fc == 0 else []
                    if fc == 0 and x2add[half] is not None:
                        wts.append(x2add[half])
                    P.emit("tensor", lambda e_, fc=fc, buf=buf, half=half: e_.matmul(
                        P1[:, half * 512:(half + 1) * 512],
                        buf[:, fc * 128:(fc + 1) * 128],
                        arena[:, A_OC + fc * 512:A_OC + (fc + 1) * 512],
                        start=(fc == 0), stop=(fc == DC - 1),
                        skip_group_check=True),
                        waits=wts,
                        inc=S["PE"] if fc == DC - 1 else None)
                wo_done = S["PE"].now()
                P.emit("vector", lambda e_, dout=dout, half=half: e_.tensor_tensor(
                    out=x2s(dout, c0, 512), in0=x2s(dout, c0, 512),
                    in1=P1[:, half * 512:(half + 1) * 512], op=ALU.add),
                    waits=[wo_done], inc=S["DV"])
                x2add[half] = S["DV"].now()

        stage = globals().get("_STAGE", "full")
        if stage == "xrdw":
            emit_ffn(0, nexp=1, sub="xrd")
            for nm in ("WGA", "WGB", "W1A", "W1B", "WRA"):
                P.wait("gpsimd", [S[nm].now()])
            P.flush()
        if stage == "s0":
            pass
        elif stage == "xrdw":
            pass
        elif stage in ("pa", "wbs") or stage.startswith("x"):
            emit_ffn(0, nexp=1, sub=stage)
            P.flush()
        elif stage.startswith("e"):
            emit_ffn(0, nexp=int(stage[1:]))
            P.flush()
        else:
            emit_ffn(0)
            P.flush()
            if stage != "t0":
                emit_ffn(1024)
                P.flush()
        if stage in ("f1a", "full"):
            cc_ready = emit_kv()
            P.flush()
            for qc in range(4):
                emit_qc(qc, cc_ready)
                P.flush()
        if stage == "full":
            emit_ffn(0)
            P.flush()
            emit_ffn(1024)
            P.flush()
        for dc in range(DC):
            dma("sync", S["ST"], out_t[dc], x2[:, dc * T:(dc + 1) * T])
        P.wait("sync", [S["ST"].now()])
        P.flush()

    return nc


_NC = None


def _get_nc():
    global _NC
    if _NC is None:
        _NC = build_nc()
    return _NC


def kernel(L0_attn_x, L0_attn_Wq, L0_attn_Wk, L0_attn_Wv, L0_attn_Wo,
           L0_ffn_Wgate, L0_ffn_bgate, L0_ffn_Wg, L0_ffn_bg,
           L0_ffn_W1, L0_ffn_b1, L0_ffn_W2, L0_ffn_b2):
    x = np.asarray(L0_attn_x, dtype=np.float32)
    f32c = lambda a: np.ascontiguousarray(np.asarray(a, dtype=np.float32))
    shared = {
        "Wq": f32c(L0_attn_Wq), "Wk": f32c(L0_attn_Wk), "Wv": f32c(L0_attn_Wv),
        "Wo": f32c(L0_attn_Wo), "Wg": f32c(L0_ffn_Wg), "W1": f32c(L0_ffn_W1),
        "W2": f32c(L0_ffn_W2),
        "Wgate_img": np.ascontiguousarray(
            f32c(L0_ffn_Wgate).reshape(DC, 128, E).transpose(1, 0, 2).reshape(128, DC * E)),
        "bgate_r": f32c(L0_ffn_bgate).reshape(1, E),
        "bg_img": np.ascontiguousarray(
            f32c(L0_ffn_bg).reshape(E, MC, 128).transpose(2, 0, 1).reshape(128, E * MC)),
        "b1_img": np.ascontiguousarray(
            f32c(L0_ffn_b1).reshape(E, MC, 128).transpose(2, 0, 1).reshape(128, E * MC)),
        "b2_img": np.ascontiguousarray(
            f32c(L0_ffn_b2).reshape(E, DC, 128).transpose(2, 0, 1).reshape(128, E * DC)),
    }
    in_maps = []
    for c in range(NCORE):
        sl = dict(shared)
        sl.update(_host_consts(c))
        xc = np.concatenate([x[0, TPB * c:TPB * (c + 1)],
                             x[1, TPB * c:TPB * (c + 1)]], axis=0)
        sl["xT"] = np.ascontiguousarray(xc.T)
        in_maps.append(sl)
    nc = _get_nc()
    res = run_bass_kernel_spmd(nc, in_maps, core_ids=list(range(NCORE)))
    out = np.empty((B, L, D), dtype=np.float32)
    for c, r in enumerate(res.results):
        oc = np.asarray(r["out"])
        out[0, TPB * c:TPB * (c + 1)] = oc[:, 0:TPB].T
        out[1, TPB * c:TPB * (c + 1)] = oc[:, TPB:T].T
    return out


# version 08
# speedup vs baseline: 1.0712x; 1.0686x over previous
import contextlib
import numpy as np
import ml_dtypes

import concourse.bass as bass
import concourse.mybir as mybir
from concourse.bass_utils import run_bass_kernel_spmd

F32 = mybir.dt.float32
BF16 = mybir.dt.bfloat16
AF = mybir.ActivationFunctionType
ALU = mybir.AluOpType
AX = mybir.AxisListType
bfl = ml_dtypes.bfloat16

B, L, D = 2, 8192, 1024
H, KV, HD = 16, 4, 64
E, M = 8, 4096
NCORE = 8
TPB = L // NCORE
T = 2 * TPB
DC = D // 128
MC = M // 128
EPS = 1e-5

ENGINES = ("sync", "tensor", "scalar", "vector", "gpsimd")


def _alpha_full():
    pos = np.arange(L, dtype=np.float64)
    s = np.asarray([2.0 ** (-0.5 * (i + 1)) for i in range(H)], dtype=np.float64)
    return np.exp(s[None, :] * (pos[:, None] - (L - 1))).astype(np.float32)


_ALPHA = _alpha_full()
LIVE = [
    [kc for kc in range(L // 128) if np.any(_ALPHA[kc * 128:(kc + 1) * 128, h] != 0.0)]
    for h in range(H)
]


def _host_consts(core):
    pos = np.arange(TPB * core, TPB * (core + 1), dtype=np.float64)
    d = np.arange(128)
    f = 1.0 / (10000.0 ** (2.0 * ((d % 64) % 32) / 64.0))
    ang = pos[None, :] * f[:, None]
    cos, sin = np.cos(ang), np.sin(ang)
    a = _ALPHA.reshape(64, 128, H).transpose(1, 0, 2).reshape(128, 64 * H)
    R = np.zeros((128, 128), dtype=np.float32)
    for o in (0, 64):
        for m in range(32):
            R[o + m + 32, o + m] = -1.0
            R[o + m, o + m + 32] = 1.0
    return {
        "cosq": (cos / 8.0).astype(bfl),
        "sinq": (sin / 8.0).astype(bfl),
        "cosk": cos.astype(bfl),
        "sink": sin.astype(bfl),
        "alpha_img": np.ascontiguousarray(a),
        "rot_b": R.astype(bfl),
        "ident_b": np.eye(128, dtype=np.float32).astype(bfl),
        "ident_f": np.eye(128, dtype=np.float32),
    }


class Sem:
    def __init__(self, nc, stack, name):
        self.h = stack.enter_context(nc.semaphore(name))
        self.n = 0

    def now(self):
        return (self.h, self.n)


class Prog:

    def __init__(self, nc):
        self.nc = nc
        self.steps = []

    def emit(self, eng, fn, waits=(), inc=None, k=1):
        ws = tuple((h, v) for (h, v) in waits if v > 0)
        ih = None
        if inc is not None:
            inc.n += k
            ih = inc.h
        self.steps.append((eng, ws, fn, ih, k))

    def wait(self, eng, waits):
        self.emit(eng, None, waits=waits)

    def flush(self):
        steps = self.steps
        self.steps = []
        with self.nc.Block() as block:
            for name in ENGINES:
                def runner(e, name=name):
                    for (eng, ws, fn, ih, k) in steps:
                        if eng != name:
                            continue
                        for (h, v) in ws:
                            e.wait_ge(h, v)
                        ins = fn(e) if fn is not None else None
                        if ih is not None:
                            assert ins is not None
                            ins.then_inc(ih, k)
                getattr(block, name)(runner)


def build_nc():
    nc = bass.Bass("TRN2", target_bir_lowering=False)

    xT = nc.dram_tensor("xT", [D, T], F32, kind="ExternalInput").ap()
    Wq = nc.dram_tensor("Wq", [D, D], F32, kind="ExternalInput").ap()
    Wk = nc.dram_tensor("Wk", [D, KV * HD], F32, kind="ExternalInput").ap()
    Wv = nc.dram_tensor("Wv", [D, KV * HD], F32, kind="ExternalInput").ap()
    Wo = nc.dram_tensor("Wo", [D, D], F32, kind="ExternalInput").ap()
    Wg = nc.dram_tensor("Wg", [E, D, M], F32, kind="ExternalInput").ap()
    W1 = nc.dram_tensor("W1", [E, D, M], F32, kind="ExternalInput").ap()
    W2 = nc.dram_tensor("W2", [E, M, D], F32, kind="ExternalInput").ap()
    Wgate_img = nc.dram_tensor("Wgate_img", [128, DC * E], F32, kind="ExternalInput").ap()
    bgate_r = nc.dram_tensor("bgate_r", [1, E], F32, kind="ExternalInput").ap()
    bg_img = nc.dram_tensor("bg_img", [128, E * MC], F32, kind="ExternalInput").ap()
    b1_img = nc.dram_tensor("b1_img", [128, E * MC], F32, kind="ExternalInput").ap()
    b2_img = nc.dram_tensor("b2_img", [128, E * DC], F32, kind="ExternalInput").ap()
    cosq = nc.dram_tensor("cosq", [128, TPB], BF16, kind="ExternalInput").ap()
    sinq = nc.dram_tensor("sinq", [128, TPB], BF16, kind="ExternalInput").ap()
    cosk = nc.dram_tensor("cosk", [128, TPB], BF16, kind="ExternalInput").ap()
    sink = nc.dram_tensor("sink", [128, TPB], BF16, kind="ExternalInput").ap()
    alpha_img = nc.dram_tensor("alpha_img", [128, 64 * H], F32, kind="ExternalInput").ap()
    rot_b = nc.dram_tensor("rot_b", [128, 128], BF16, kind="ExternalInput").ap()
    ident_b = nc.dram_tensor("ident_b", [128, 128], BF16, kind="ExternalInput").ap()
    ident_f = nc.dram_tensor("ident_f", [128, 128], F32, kind="ExternalInput").ap()
    out_d = nc.dram_tensor("out", [D, T], F32, kind="ExternalOutput").ap()

    ccki = nc.dram_tensor("ccki", [KV * HD, T], BF16)
    ccko = nc.dram_tensor("ccko", [NCORE * KV * HD, T], BF16)
    ccvi = nc.dram_tensor("ccvi", [T, KV * HD], BF16)
    ccvo = nc.dram_tensor("ccvo", [NCORE * T, KV * HD], BF16)

    wg_t = Wg.rearrange("e (dc p) m -> e p dc m", p=128)
    w1_t = W1.rearrange("e (dc p) m -> e p dc m", p=128)
    w2_t = W2.rearrange("e (mc p) d -> e p mc d", p=128)
    wq_t = Wq.rearrange("(dc p) m -> p dc m", p=128)
    wk_t = Wk.rearrange("(dc p) m -> p dc m", p=128)
    wv_t = Wv.rearrange("(dc p) m -> p dc m", p=128)
    wo_t = Wo.rearrange("(dc p) m -> p dc m", p=128)
    out_t = out_d.rearrange("(dc p) t -> dc p t", p=128)

    st = contextlib.ExitStack()
    with st:
        E_ = st.enter_context
        x2 = E_(nc.sbuf_tensor("x2", [128, DC * T], F32))
        arena = E_(nc.sbuf_tensor("arena", [128, 43008], BF16))
        af32 = E_(nc.sbuf_tensor("af32", [128, 3072], F32))
        alpha_sb = E_(nc.sbuf_tensor("alpha_sb", [128, 64 * H], F32))
        wgbuf = E_(nc.sbuf_tensor("wgbuf", [128, 2 * 1024], BF16))
        w1buf = E_(nc.sbuf_tensor("w1buf", [128, 2 * 1024], BF16))
        w2buf = E_(nc.sbuf_tensor("w2buf", [128, 3 * 1024], BF16))
        ktb = E_(nc.sbuf_tensor("ktb", [128, 4 * 128], BF16))
        vtb = E_(nc.sbuf_tensor("vtb", [128, 4 * 64], BF16))
        vhb = E_(nc.sbuf_tensor("vhb", [128, 8 * 65], BF16))
        bg_sb = E_(nc.sbuf_tensor("bg_sb", [128, E * MC], F32))
        b1_sb = E_(nc.sbuf_tensor("b1_sb", [128, E * MC], F32))
        b2_sb = E_(nc.sbuf_tensor("b2_sb", [128, E * DC], F32))
        wgate_f = E_(nc.sbuf_tensor("wgate_f", [128, DC * E], F32))
        wgate_hi = E_(nc.sbuf_tensor("wgate_hi", [128, DC * E], BF16))
        wgate_lo = E_(nc.sbuf_tensor("wgate_lo", [128, DC * E], BF16))
        wgate_d = E_(nc.sbuf_tensor("wgate_d", [128, DC * E], F32))
        bgate_f = E_(nc.sbuf_tensor("bgate_f", [1, E], F32))
        bgate_hi = E_(nc.sbuf_tensor("bgate_hi", [1, E], BF16))
        bgate_lo = E_(nc.sbuf_tensor("bgate_lo", [1, E], BF16))
        bgate_d = E_(nc.sbuf_tensor("bgate_d", [1, E], F32))
        rot_sb = E_(nc.sbuf_tensor("rot_sb", [128, 128], BF16))
        idb_sb = E_(nc.sbuf_tensor("idb_sb", [128, 128], BF16))
        idf_sb = E_(nc.sbuf_tensor("idf_sb", [128, 128], F32))
        ones_c = E_(nc.sbuf_tensor("ones_c", [128, 1], F32))
        ones_r = E_(nc.sbuf_tensor("ones_r", [1, 128], F32))
        ones_rb = E_(nc.sbuf_tensor("ones_rb", [1, 128], BF16))
        ceps = E_(nc.sbuf_tensor("ceps", [1, 1], F32))
        cm05 = E_(nc.sbuf_tensor("cm05", [1, 1], F32))
        m_row = E_(nc.sbuf_tensor("m_row", [1, 1024], F32))
        v_row = E_(nc.sbuf_tensor("v_row", [1, 1024], F32))
        w_row = E_(nc.sbuf_tensor("w_row", [1, 2048], F32))
        zsb = E_(nc.sbuf_tensor("zsb", [128, 64], F32))
        t8 = E_(nc.sbuf_tensor("t8", [128, 64], F32))
        zm8 = E_(nc.sbuf_tensor("zm8", [128, 64], F32))
        msk1 = E_(nc.sbuf_tensor("msk1", [128, 64], F32))
        msk2 = E_(nc.sbuf_tensor("msk2", [128, 64], F32))
        m18 = E_(nc.sbuf_tensor("m18", [128, 8], F32))
        m28 = E_(nc.sbuf_tensor("m28", [128, 8], F32))
        d8 = E_(nc.sbuf_tensor("d8", [128, 8], F32))
        w1h = E_(nc.sbuf_tensor("w1h", [128, 8], F32))
        w2h = E_(nc.sbuf_tensor("w2h", [128, 8], F32))
        wvec = E_(nc.sbuf_tensor("wvec", [128, 64], F32))
        wT_sb = E_(nc.sbuf_tensor("wT_sb", [64, 128], F32))

        P0 = E_(nc.psum_tensor("P0", [128, 1024], F32))
        P1 = E_(nc.psum_tensor("P1", [128, 1024], F32))
        P2 = E_(nc.psum_tensor("P2", [128, 1024], F32))
        P3 = E_(nc.psum_tensor("P3", [128, 512], F32))
        PT = E_(nc.psum_tensor("PT", [128, 128], BF16))

        names = ("IN", "PE", "SC", "DV", "PL", "VB", "SB", "CC", "ST", "KVC",
                 "WGA", "WGB", "W1A", "W1B", "W2A", "W2B", "W2C",
                 "KVS0", "KVS1", "KVS2", "KVS3", "WRA", "WRB", "AWA", "AWB")
        S = {n: Sem(nc, st, "s" + n) for n in names}

        P = Prog(nc)

        A_H, A_XT, A_LO, A_T1 = 0, 32768, 40960, 41984
        A_Q, A_OC, A_PS, A_KC, A_VT, A_VF, A_KB = 0, 4096, 8192, 10240, 14336, 18432, 19456
        A_COS = 20480
        A_KH = 24576
        A_VH = 10240
        F_T, F_U, F_XSQ = 0, 1024, 2048

        def x2s(dc, c0, w):
            return x2[:, dc * T + c0: dc * T + c0 + w]

        def dma(eng, sem, outap, inap, waits=()):
            P.emit(eng, lambda e, o=outap, i=inap: e.dma_start(out=o, in_=i),
                   waits=waits, inc=sem, k=16)

        dma("sync", S["IN"], x2[:].rearrange("p (dc t) -> p dc t", dc=DC),
            xT.rearrange("(dc p) t -> p dc t", p=128))
        dma("sync", S["IN"], alpha_sb[:], alpha_img)
        dma("sync", S["IN"], rot_sb[:], rot_b)
        dma("sync", S["IN"], idb_sb[:], ident_b)
        dma("sync", S["IN"], idf_sb[:], ident_f)
        dma("sync", S["IN"], bg_sb[:], bg_img)
        dma("sync", S["IN"], b1_sb[:], b1_img)
        dma("sync", S["IN"], b2_sb[:], b2_img)
        dma("sync", S["IN"], wgate_f[:], Wgate_img)
        dma("sync", S["IN"], bgate_f[:], bgate_r)
        all_in = S["IN"].now()

        def setup_v(v):
            v.memset(ones_c[:], 1.0)
            v.memset(ones_r[:], 1.0)
            v.memset(ones_rb[:], 1.0)
            v.memset(ceps[:], EPS)
            return v.memset(cm05[:], -0.5)
        P.emit("vector", setup_v, waits=[all_in], inc=S["VB"])

        P.emit("gpsimd", lambda e: e.tensor_copy(out=wgate_hi[:], in_=wgate_f[:]),
               waits=[all_in], inc=S["PL"])
        P.emit("gpsimd", lambda e: e.tensor_copy(out=bgate_hi[:], in_=bgate_f[:]),
               inc=S["PL"])
        P.emit("vector", lambda e: e.tensor_tensor(
            out=wgate_d[:], in0=wgate_f[:], in1=wgate_hi[:], op=ALU.subtract),
            waits=[S["PL"].now(), S["VB"].now()], inc=S["VB"])
        P.emit("vector", lambda e: e.tensor_tensor(
            out=bgate_d[:], in0=bgate_f[:], in1=bgate_hi[:], op=ALU.subtract),
            waits=[S["VB"].now()], inc=S["VB"])
        P.emit("gpsimd", lambda e: e.tensor_copy(out=wgate_lo[:], in_=wgate_d[:]),
               waits=[S["VB"].now()], inc=S["PL"])
        P.emit("gpsimd", lambda e: e.tensor_copy(out=bgate_lo[:], in_=bgate_d[:]),
               inc=S["PL"])
        P.flush()

        def emit_ln(c0, width, gate=False):
            nh = width // 512
            for n in range(nh):
                for dc in range(DC):
                    xs = x2s(dc, c0 + n * 512, 512)
                    sq = af32[:, F_XSQ + (dc % 2) * 512: F_XSQ + (dc % 2) * 512 + 512]
                    P.emit("scalar", lambda e, o=sq, i=xs: e.activation(
                        out=o, in_=i, func=AF.Square),
                        waits=[S["PE"].now()], inc=S["SC"])
                    P.emit("tensor", lambda e, i=xs, n=n, dc=dc: e.matmul(
                        P0[0:1, n * 512:(n + 1) * 512], ones_c[:], i,
                        start=(dc == 0), stop=(dc == DC - 1), skip_group_check=True))
                    P.emit("tensor", lambda e, q=sq, n=n, dc=dc: e.matmul(
                        P1[0:1, n * 512:(n + 1) * 512], ones_c[:], q,
                        start=(dc == 0), stop=(dc == DC - 1), skip_group_check=True),
                        waits=[S["SC"].now()], inc=S["PE"])
            stats = S["PE"].now()
            mr = m_row[:, 0:width]
            vr = v_row[:, 0:width]
            qr = af32[0:1, F_XSQ:F_XSQ + width]
            P.emit("vector", lambda e: e.tensor_scalar(
                out=mr, in0=P0[0:1, 0:width], scalar1=1.0 / D, scalar2=None,
                op0=ALU.mult), waits=[stats, S["VB"].now()], inc=S["VB"])
            P.emit("vector", lambda e: e.tensor_scalar(
                out=vr, in0=P1[0:1, 0:width], scalar1=1.0 / D, scalar2=None,
                op0=ALU.mult), waits=[S["VB"].now()], inc=S["VB"])
            P.emit("vector", lambda e: e.scalar_tensor_tensor(
                out=qr, in0=mr, scalar=-1.0, in1=mr, op0=ALU.mult, op1=ALU.mult),
                waits=[S["VB"].now()], inc=S["VB"])
            P.emit("vector", lambda e: e.tensor_tensor(
                out=vr, in0=vr, in1=qr, op=ALU.add),
                waits=[S["VB"].now()], inc=S["VB"])
            row_dv = S["VB"].now()
            P.emit("scalar", lambda e: e.activation(
                out=vr, in_=vr, func=AF.Ln, bias=ceps[:]),
                waits=[row_dv, S["SB"].now()], inc=S["SB"])
            P.emit("scalar", lambda e: e.activation(
                out=vr, in_=vr, func=AF.Exp, scale=cm05[:]),
                waits=[S["SB"].now()], inc=S["SB"])
            P.emit("vector", lambda e: e.tensor_tensor(
                out=mr, in0=mr, in1=vr, op=ALU.mult),
                waits=[S["SB"].now(), S["VB"].now()], inc=S["VB"])
            rows = S["VB"].now()
            for n in range(nh):
                P.emit("tensor", lambda e, n=n: e.matmul(
                    P0[:, n * 512:(n + 1) * 512], ones_r[:],
                    v_row[:, n * 512:(n + 1) * 512], start=True, stop=True),
                    waits=[rows] if n == 0 else ())
                P.emit("tensor", lambda e, n=n: e.matmul(
                    P1[:, n * 512:(n + 1) * 512], ones_r[:],
                    m_row[:, n * 512:(n + 1) * 512], start=True, stop=True),
                    inc=S["PE"])
            bcast = S["PE"].now()
            for dc in range(DC):
                tbuf = af32[:, F_T:F_T + width]
                xtsl = arena[:, A_XT + dc * width: A_XT + (dc + 1) * width]
                losl = arena[:, A_LO: A_LO + width]
                P.emit("vector", lambda e, dc=dc, tb=tbuf: e.tensor_tensor(
                    out=tb, in0=x2s(dc, c0, width), in1=P0[:, 0:width], op=ALU.mult),
                    waits=[bcast] + ([S["PE"].now()] if gate else []), inc=S["DV"])
                P.emit("vector", lambda e, tb=tbuf: e.tensor_tensor(
                    out=tb, in0=tb, in1=P1[:, 0:width], op=ALU.subtract),
                    inc=S["DV"])
                P.emit("vector", lambda e, o=xtsl, tb=tbuf: e.tensor_copy(
                    out=o, in_=tb), inc=S["DV"])
                if gate:
                    P.emit("vector", lambda e, o=losl, tb=tbuf, x=xtsl: e.tensor_tensor(
                        out=o, in0=tb, in1=x, op=ALU.subtract), inc=S["DV"])
                    g_dv = S["DV"].now()
                    ntch = width // 128
                    for tch in range(ntch):
                        xsl = arena[:, A_XT + dc * width + tch * 128:
                                    A_XT + dc * width + tch * 128 + 128]
                        lsl = arena[:, A_LO + tch * 128: A_LO + tch * 128 + 128]
                        zr = P3[:, tch * 8:(tch + 1) * 8]
                        for pi, (lhs, rhs) in enumerate((
                                (xsl, wgate_hi), (xsl, wgate_lo), (lsl, wgate_hi))):
                            P.emit("tensor", lambda e, zr=zr, l=lhs, r=rhs, dc=dc,
                                   pi=pi, tch=tch: e.matmul(
                                zr, l, r[:, dc * 8:(dc + 1) * 8],
                                start=(dc == 0 and pi == 0 and tch == 0), stop=False,
                                skip_group_check=True),
                                waits=[g_dv] if (pi == 0) else (), inc=S["PE"])
            xt_ready = S["DV"].now()
            wT_ready = None
            if gate:
                ntch = width // 128
                for tch in range(ntch):
                    zr = P3[:, tch * 8:(tch + 1) * 8]
                    P.emit("tensor", lambda e, zr=zr: e.matmul(
                        zr, ones_rb[:], bgate_hi[:], start=False, stop=False,
                        skip_group_check=True))
                    P.emit("tensor", lambda e, zr=zr, tch=tch: e.matmul(
                        zr, ones_rb[:], bgate_lo[:], start=False, stop=True,
                        skip_group_check=True), inc=S["PE"])
                z_done = S["PE"].now()
                nt = ntch * 8

                def vb(fn, extra=()):
                    P.emit("vector", fn, waits=[S["VB"].now()] + list(extra),
                           inc=S["VB"])

                z3 = zsb[:, 0:nt].rearrange("p (t e) -> p t e", e=8)
                zm3 = zm8[:, 0:nt].rearrange("p (t e) -> p t e", e=8)
                mk13 = msk1[:, 0:nt].rearrange("p (t e) -> p t e", e=8)
                mk23 = msk2[:, 0:nt].rearrange("p (t e) -> p t e", e=8)
                m1b = m18[:, 0:ntch].rearrange("p (t e) -> p t e", e=1) \
                    .broadcast_to([128, ntch, 8])
                m2b = m28[:, 0:ntch].rearrange("p (t e) -> p t e", e=1) \
                    .broadcast_to([128, ntch, 8])
                w1b = w1h[:, 0:ntch].rearrange("p (t e) -> p t e", e=1) \
                    .broadcast_to([128, ntch, 8])
                w2b = w2h[:, 0:ntch].rearrange("p (t e) -> p t e", e=1) \
                    .broadcast_to([128, ntch, 8])
                vb(lambda e: e.tensor_copy(out=zsb[:, 0:nt], in_=P3[:, 0:nt]),
                   extra=[z_done])
                vb(lambda e: e.tensor_reduce(out=m18[:, 0:ntch], in_=z3, axis=AX.X,
                                             op=ALU.max))
                vb(lambda e: e.tensor_tensor(out=mk13, in0=z3, in1=m1b,
                                             op=ALU.is_equal))
                vb(lambda e: e.tensor_scalar(out=t8[:, 0:nt], in0=msk1[:, 0:nt],
                                             scalar1=-1e30, scalar2=None, op0=ALU.mult))
                vb(lambda e: e.tensor_tensor(out=zm8[:, 0:nt], in0=zsb[:, 0:nt],
                                             in1=t8[:, 0:nt], op=ALU.add))
                vb(lambda e: e.tensor_reduce(out=m28[:, 0:ntch], in_=zm3, axis=AX.X,
                                             op=ALU.max))
                vb(lambda e: e.tensor_tensor(out=mk23, in0=zm3, in1=m2b,
                                             op=ALU.is_equal))
                vb(lambda e: e.tensor_tensor(out=d8[:, 0:ntch], in0=m18[:, 0:ntch],
                                             in1=m28[:, 0:ntch], op=ALU.subtract))
                P.emit("scalar", lambda e: e.activation(
                    out=w1h[:, 0:ntch], in_=d8[:, 0:ntch], func=AF.Sigmoid),
                    waits=[S["VB"].now(), S["SB"].now()], inc=S["SB"])
                vb(lambda e: e.tensor_scalar(out=w1h[:, 0:ntch], in0=w1h[:, 0:ntch],
                                             scalar1=0.5, scalar2=None, op0=ALU.mult),
                   extra=[S["SB"].now()])
                vb(lambda e: e.tensor_scalar(out=w2h[:, 0:ntch], in0=w1h[:, 0:ntch],
                                             scalar1=-1.0, scalar2=0.5,
                                             op0=ALU.mult, op1=ALU.add))
                vb(lambda e: e.tensor_tensor(out=mk13, in0=mk13, in1=w1b, op=ALU.mult))
                vb(lambda e: e.tensor_tensor(out=mk23, in0=mk23, in1=w2b, op=ALU.mult))
                vb(lambda e: e.tensor_tensor(out=wvec[:, 0:nt], in0=msk1[:, 0:nt],
                                             in1=msk2[:, 0:nt], op=ALU.add))
                wv_done = S["VB"].now()
                P.emit("tensor", lambda e: e.transpose(
                    P3[0:nt, 0:128], wvec[:, 0:nt], idf_sb[:]),
                    waits=[wv_done], inc=S["PE"])
                P.emit("vector", lambda e: e.tensor_copy(
                    out=wT_sb[0:nt, :], in_=P3[0:nt, 0:128]),
                    waits=[S["PE"].now(), S["VB"].now()], inc=S["VB"])
                wT_ready = S["VB"].now()
            return xt_ready, wT_ready

        wr_state = [None, None]

        def emit_ffn(c0, nexp=E, sub='all'):
            xt_ready, wT_ready = emit_ln(c0, 1024, gate=True)
            wg_cons = [None, None]
            w1_cons = [None, None]
            w2_cons = [None, None, None]
            wg_fill = [None, None]
            w1_fill = [None, None]
            w2_fill = [None, None, None]
            wr_fill = [None, None]
            p3_free = S["VB"].now()
            silu_done = None
            h_done = None
            u_done = [None, None]
            sub_r = sub in ('all', 'pa', 'wbs') or 'r' in sub
            sub_d = sub in ('all', 'pa', 'wbs') or 'd' in sub
            sub_m = sub in ('all', 'pa', 'wbs') or 'm' in sub
            sub_s = sub in ('all', 'pa', 'wbs') or 's' in sub
            sub_h = sub in ('all', 'pa', 'wbs') or 'h' in sub
            for e in range(nexp):
                ws = e % 2
                sem = S["WRA" if ws == 0 else "WRB"]
                if sub_r:
                  for n in range(2):
                    for tq in range(4):
                        src = wT_sb[(4 * n + tq) * 8 + e:(4 * n + tq) * 8 + e + 1, :]
                        dst = w_row[0:1, ws * 1024 + n * 512 + tq * 128:
                                    ws * 1024 + n * 512 + tq * 128 + 128]
                        dma("sync", sem, dst, src,
                            waits=[wT_ready] + ([wr_state[ws]] if wr_state[ws] else []))
                wr_fill[ws] = sem.now()
                for mc in range(MC):
                    slot = mc % 2
                    gsem = S["WGA" if slot == 0 else "WGB"]
                    osem = S["W1A" if slot == 0 else "W1B"]
                    if not sub_d:
                        continue
                    dma("gpsimd", gsem,
                        wgbuf[:, slot * 1024:(slot + 1) * 1024]
                        .rearrange("p (dc j) -> p dc j", dc=DC),
                        wg_t[e, :, :, mc * 128:(mc + 1) * 128],
                        waits=[wg_cons[slot]] if wg_cons[slot] else ())
                    wg_fill[slot] = gsem.now()
                    dma("gpsimd", osem,
                        w1buf[:, slot * 1024:(slot + 1) * 1024]
                        .rearrange("p (dc j) -> p dc j", dc=DC),
                        w1_t[e, :, :, mc * 128:(mc + 1) * 128],
                        waits=[w1_cons[slot]] if w1_cons[slot] else ())
                    w1_fill[slot] = osem.now()
                    if not sub_m:
                        continue
                    wA = [wg_fill[slot], xt_ready]
                    if silu_done is not None:
                        wA.append(silu_done)
                    for dc in range(DC):
                        for n in range(2):
                            P.emit("tensor", lambda e_, dc=dc, n=n, slot=slot: e_.matmul(
                                P0[:, n * 512:(n + 1) * 512],
                                wgbuf[:, slot * 1024 + dc * 128:slot * 1024 + (dc + 1) * 128],
                                arena[:, A_XT + dc * 1024 + n * 512:
                                      A_XT + dc * 1024 + n * 512 + 512],
                                start=(dc == 0), stop=(dc == DC - 1),
                                skip_group_check=True),
                                waits=wA if (n == 0 and dc == 0) else (),
                                inc=S["PE"] if (n == 1 and dc == DC - 1) else None)
                    a_done = S["PE"].now()
                    wB = [w1_fill[slot]]
                    if h_done is not None:
                        wB.append(h_done)
                    for dc in range(DC):
                        for n in range(2):
                            P.emit("tensor", lambda e_, dc=dc, n=n, slot=slot: e_.matmul(
                                P1[:, n * 512:(n + 1) * 512],
                                w1buf[:, slot * 1024 + dc * 128:slot * 1024 + (dc + 1) * 128],
                                arena[:, A_XT + dc * 1024 + n * 512:
                                      A_XT + dc * 1024 + n * 512 + 512],
                                start=(dc == 0), stop=(dc == DC - 1),
                                skip_group_check=True),
                                waits=wB if (n == 0 and dc == 0) else (),
                                inc=S["PE"] if (n == 1 and dc == DC - 1) else None)
                    b_done = S["PE"].now()
                    wg_cons[slot] = a_done
                    w1_cons[slot] = b_done
                    if not sub_s:
                        silu_done = a_done
                        h_done = b_done
                        continue
                    bcol = bg_sb[:, e * MC + mc: e * MC + mc + 1]
                    P.emit("scalar", lambda e_, bc=bcol: e_.activation(
                        out=arena[:, A_T1:A_T1 + 1024], in_=P0[:, 0:1024],
                        func=AF.Silu, bias=bc),
                        waits=[a_done, S["DV"].now()], inc=S["SC"])
                    silu_done = S["SC"].now()
                    if not sub_h:
                        h_done = b_done
                        continue
                    b1col = b1_sb[:, e * MC + mc: e * MC + mc + 1]
                    P.emit("vector", lambda e_, mc=mc, bc=b1col: e_.scalar_tensor_tensor(
                        out=arena[:, A_H + mc * 1024:A_H + (mc + 1) * 1024],
                        in0=P1[:, 0:1024], scalar=bc,
                        in1=arena[:, A_T1:A_T1 + 1024], op0=ALU.add, op1=ALU.mult),
                        waits=[b_done, silu_done], inc=S["DV"])
                    h_done = S["DV"].now()
                if sub == 'pa' or sub.startswith('x'):
                    continue
                wbs_done = None
                for n in range(2):
                    P.emit("tensor", lambda e_, ws=ws, n=n: e_.matmul(
                        P3[:, 0:512], ones_r[:],
                        w_row[0:1, ws * 1024 + n * 512: ws * 1024 + n * 512 + 512],
                        start=True, stop=True),
                        waits=[wr_fill[ws], p3_free], inc=S["PE"])
                    P.emit("vector", lambda e_, n=n: e_.tensor_copy(
                        out=af32[:, F_T + n * 512:F_T + n * 512 + 512],
                        in_=P3[:, 0:512]),
                        waits=[S["PE"].now()], inc=S["DV"])
                    p3_free = S["DV"].now()
                wbs_done = S["DV"].now()
                wr_state[ws] = S["PE"].now()
                if sub == 'wbs':
                    continue
                w2_fill_by = {}

                def w2dma(qg):
                    s3 = qg % 3
                    qsem = S[("W2A", "W2B", "W2C")[s3]]
                    dma("gpsimd", qsem,
                        w2buf[:, s3 * 1024:(s3 + 1) * 1024]
                        .rearrange("p (mq j) -> p mq j", mq=8),
                        w2_t[e, :, (qg % 4) * 8:(qg % 4) * 8 + 8,
                             (qg // 4) * 128:(qg // 4) * 128 + 128],
                        waits=[w2_cons[s3]] if w2_cons[s3] else ())
                    w2_fill_by[qg] = qsem.now()

                for qg in range(3):
                    w2dma(qg)
                next_qg = 3
                for dout in range(DC):
                    for j in range(MC):
                        qg = dout * 4 + j // 8
                        s3 = qg % 3
                        for n in range(2):
                            wts = []
                            if n == 0 and j % 8 == 0:
                                wts.append(w2_fill_by[qg])
                            if n == 0 and j == 0:
                                wts.append(h_done)
                                if u_done[0] is not None:
                                    wts.append(u_done[1])
                            P.emit("tensor", lambda e_, j=j, s3=s3, n=n: e_.matmul(
                                P2[:, n * 512:(n + 1) * 512],
                                w2buf[:, s3 * 1024 + (j % 8) * 128:
                                      s3 * 1024 + (j % 8) * 128 + 128],
                                arena[:, A_H + j * 1024 + n * 512:
                                      A_H + j * 1024 + n * 512 + 512],
                                start=(j == 0), stop=(j == MC - 1),
                                skip_group_check=True),
                                waits=wts,
                                inc=S["PE"] if (n == 1 and j % 8 == 7) else None)
                        if j % 8 == 7:
                            w2_cons[s3] = S["PE"].now()
                            if next_qg < 32:
                                w2dma(next_qg)
                                next_qg += 1
                    y_done = S["PE"].now()
                    b2col = b2_sb[:, e * DC + dout: e * DC + dout + 1]
                    for n in range(2):
                        P.emit("vector", lambda e_, n=n, bc=b2col: e_.scalar_tensor_tensor(
                            out=af32[:, F_U:F_U + 512],
                            in0=P2[:, n * 512:(n + 1) * 512], scalar=bc,
                            in1=af32[:, F_T + n * 512:F_T + n * 512 + 512],
                            op0=ALU.add, op1=ALU.mult),
                            waits=[y_done, wbs_done], inc=S["DV"])
                        P.emit("vector", lambda e_, dout=dout, n=n: e_.tensor_tensor(
                            out=x2s(dout, c0 + n * 512, 512),
                            in0=x2s(dout, c0 + n * 512, 512),
                            in1=af32[:, F_U:F_U + 512], op=ALU.add),
                            inc=S["DV"])
                    u_done[0] = u_done[1] = S["DV"].now()

        def emit_kv():
            for ap_, off_ in ((cosq, 0), (sinq, TPB), (cosk, 2 * TPB), (sink, 3 * TPB)):
                dma("sync", S["IN"], arena[:, A_COS + off_:A_COS + off_ + TPB], ap_)
            cos_ready = S["IN"].now()
            for tc in range(2):
                xt_ready, _ = emit_ln(tc * 1024, 1024, gate=False)
                for vc in range(2):
                    asem = S["AWA"]
                    dma("gpsimd", asem,
                        wgbuf[:, 0:1024].rearrange("p (dc j) -> p dc j", dc=DC),
                        wk_t[:, :, vc * 128:(vc + 1) * 128],
                        waits=[S["PE"].now()])
                    kw = asem.now()
                    for dc in range(DC):
                        for n in range(2):
                            P.emit("tensor", lambda e_, dc=dc, n=n: e_.matmul(
                                P1[:, n * 512:(n + 1) * 512],
                                wgbuf[:, dc * 128:(dc + 1) * 128],
                                arena[:, A_XT + dc * 1024 + n * 512:
                                      A_XT + dc * 1024 + n * 512 + 512],
                                start=(dc == 0), stop=(dc == DC - 1),
                                skip_group_check=True),
                                waits=[kw, xt_ready, S["DV"].now()]
                                if (n == 0 and dc == 0) else (),
                                inc=S["PE"] if (n == 1 and dc == DC - 1) else None)
                    kproj = S["PE"].now()
                    P.emit("vector", lambda e_: e_.tensor_copy(
                        out=arena[:, A_KB:A_KB + 1024], in_=P1[:, 0:1024]),
                        waits=[kproj], inc=S["DV"])
                    kb_done = S["DV"].now()
                    for n in range(2):
                        P.emit("tensor", lambda e_, n=n: e_.matmul(
                            P2[:, n * 512:(n + 1) * 512], rot_sb[:],
                            arena[:, A_KB + n * 512:A_KB + n * 512 + 512],
                            start=True, stop=True),
                            waits=[kb_done, S["DV"].now()] if n == 0 else (),
                            inc=S["PE"] if n == 1 else None)
                    krot = S["PE"].now()
                    P.emit("vector", lambda e_: e_.tensor_tensor(
                        out=af32[:, F_T:F_T + 1024], in0=P1[:, 0:1024],
                        in1=arena[:, A_COS + 2 * TPB:A_COS + 3 * TPB], op=ALU.mult),
                        waits=[krot, cos_ready], inc=S["DV"])
                    P.emit("vector", lambda e_: e_.tensor_tensor(
                        out=af32[:, F_U:F_U + 1024], in0=P2[:, 0:1024],
                        in1=arena[:, A_COS + 3 * TPB:A_COS + 4 * TPB], op=ALU.mult),
                        inc=S["DV"])
                    P.emit("vector", lambda e_, tc=tc, vc=vc: e_.tensor_tensor(
                        out=arena[:, A_KC + vc * 2048 + tc * 1024:
                                  A_KC + vc * 2048 + tc * 1024 + 1024],
                        in0=af32[:, F_T:F_T + 1024], in1=af32[:, F_U:F_U + 1024],
                        op=ALU.add), inc=S["DV"])
                    kf_done = S["DV"].now()
                    dma("gpsimd", S["AWB"],
                        w1buf[:, 0:1024].rearrange("p (dc j) -> p dc j", dc=DC),
                        wv_t[:, :, vc * 128:(vc + 1) * 128],
                        waits=[S["PE"].now()])
                    vw = S["AWB"].now()
                    for dc in range(DC):
                        for n in range(2):
                            P.emit("tensor", lambda e_, dc=dc, n=n: e_.matmul(
                                P1[:, n * 512:(n + 1) * 512],
                                w1buf[:, dc * 128:(dc + 1) * 128],
                                arena[:, A_XT + dc * 1024 + n * 512:
                                      A_XT + dc * 1024 + n * 512 + 512],
                                start=(dc == 0), stop=(dc == DC - 1),
                                skip_group_check=True),
                                waits=[vw, kf_done] if (n == 0 and dc == 0) else (),
                                inc=S["PE"] if (n == 1 and dc == DC - 1) else None)
                    vproj = S["PE"].now()
                    P.emit("vector", lambda e_: e_.tensor_copy(
                        out=arena[:, A_VF:A_VF + 1024], in_=P1[:, 0:1024]),
                        waits=[vproj], inc=S["DV"])
                    vf_done = S["DV"].now()
                    for tch in range(8):
                        P.emit("tensor", lambda e_, tch=tch: e_.transpose(
                            PT[:, 0:128],
                            arena[:, A_VF + tch * 128:A_VF + tch * 128 + 128],
                            idb_sb[:]),
                            waits=[vf_done, S["DV"].now()], inc=S["PE"])
                        P.emit("vector", lambda e_, tc=tc, vc=vc, tch=tch: e_.tensor_copy(
                            out=arena[:, A_VT + (tc * 8 + tch) * 256 + vc * 128:
                                      A_VT + (tc * 8 + tch) * 256 + vc * 128 + 128],
                            in_=PT[:, 0:128]),
                            waits=[S["PE"].now()], inc=S["DV"])
            vt_done = S["DV"].now()
            dma("sync", S["KVC"],
                ccki.rearrange("(c p) t -> p c t", p=128),
                arena[:, A_KC:A_KC + 4096].rearrange("p (c t) -> p c t", c=2),
                waits=[vt_done])
            dma("sync", S["KVC"],
                ccvi.rearrange("(tc p) f -> p tc f", p=128),
                arena[:, A_VT:A_VT + 4096].rearrange("p (tc f) -> p tc f", tc=16),
                waits=[vt_done])
            kvc = S["KVC"].now()
            P.emit("gpsimd", lambda e_: e_.collective_compute(
                "AllGather", ALU.bypass, replica_groups=[list(range(NCORE))],
                ins=[ccki.ap().opt()], outs=[ccko.ap().opt()]),
                waits=[kvc], inc=S["CC"])
            P.emit("gpsimd", lambda e_: e_.collective_compute(
                "AllGather", ALU.bypass, replica_groups=[list(range(NCORE))],
                ins=[ccvi.ap().opt()], outs=[ccvo.ap().opt()]),
                inc=S["CC"])
            return S["CC"].now()

        def emit_qc(qc, cc_ready):
            batch, poff = qc // 2, (qc % 2) * 512
            c0 = qc * 512
            xt_ready, _ = emit_ln(c0, 512, gate=False)
            for ch in range(DC):
                asem = S["AWA" if ch % 2 == 0 else "AWB"]
                buf = wgbuf if ch % 2 == 0 else w1buf
                dma("gpsimd", asem,
                    buf[:, 0:1024].rearrange("p (dc j) -> p dc j", dc=DC),
                    wq_t[:, :, ch * 128:(ch + 1) * 128],
                    waits=[S["PE"].now()])
                qw_ = asem.now()
                for dc in range(DC):
                    P.emit("tensor", lambda e_, dc=dc, buf=buf: e_.matmul(
                        P1[:, 0:512], buf[:, dc * 128:(dc + 1) * 128],
                        arena[:, A_XT + dc * 512:A_XT + dc * 512 + 512],
                        start=(dc == 0), stop=(dc == DC - 1),
                        skip_group_check=True),
                        waits=[qw_, xt_ready, S["DV"].now()] if dc == 0 else (),
                        inc=S["PE"] if dc == DC - 1 else None)
                qproj = S["PE"].now()
                P.emit("vector", lambda e_: e_.tensor_copy(
                    out=arena[:, A_KB:A_KB + 512], in_=P1[:, 0:512]),
                    waits=[qproj], inc=S["DV"])
                qb_done = S["DV"].now()
                P.emit("tensor", lambda e_: e_.matmul(
                    P3[:, 0:512], rot_sb[:], arena[:, A_KB:A_KB + 512],
                    start=True, stop=True),
                    waits=[qb_done, S["DV"].now()], inc=S["PE"])
                qrot = S["PE"].now()
                P.emit("vector", lambda e_: e_.tensor_tensor(
                    out=af32[:, F_T:F_T + 512], in0=P1[:, 0:512],
                    in1=arena[:, A_COS + poff:A_COS + poff + 512], op=ALU.mult),
                    waits=[qrot], inc=S["DV"])
                P.emit("vector", lambda e_: e_.tensor_tensor(
                    out=af32[:, F_U:F_U + 512], in0=P3[:, 0:512],
                    in1=arena[:, A_COS + TPB + poff:A_COS + TPB + poff + 512], op=ALU.mult),
                    inc=S["DV"])
                P.emit("vector", lambda e_, ch=ch: e_.tensor_tensor(
                    out=arena[:, A_Q + ch * 512:A_Q + (ch + 1) * 512],
                    in0=af32[:, F_T:F_T + 512], in1=af32[:, F_U:F_U + 512],
                    op=ALU.add), inc=S["DV"])
            q_ready = S["DV"].now()
            p3_free = S["DV"].now()
            kv_fill = {}
            o_evac = None

            def dma_kv(h):
                par = h % 2
                base = par * 64
                kvh = h // 4
                kc0 = LIVE[h][0]
                ssem = S["KVS0" if par == 0 else "KVS1"]
                wts = [cc_ready]
                if kv_cons[par] is not None:
                    wts += kv_cons[par]
                r0 = kc0 // 8
                for r in range(r0, 8):
                    jlo = max(kc0 - 8 * r, 0)
                    dma("sync", ssem,
                        arena[base:base + 64,
                              A_KH + (8 * r + jlo - kc0) * 128:
                              A_KH + (8 * r + 8 - kc0) * 128],
                        ccko.ap()[256 * r + 128 * (kvh // 2) + 64 * (kvh % 2):
                                  256 * r + 128 * (kvh // 2) + 64 * (kvh % 2) + 64,
                                  batch * 1024 + 128 * jlo:(batch + 1) * 1024],
                        waits=wts)
                    dma("sync", ssem,
                        arena[:, A_VH + par * 4096 + (8 * r + jlo - kc0) * 64:
                              A_VH + par * 4096 + (8 * r + 8 - kc0) * 64]
                        .rearrange("p (j f) -> p j f", f=64),
                        ccvo.ap().rearrange("(rb j p) f -> rb p j f", p=128, j=16)
                        [r, :, 8 * batch + jlo:8 * batch + 8,
                         kvh * 64:(kvh + 1) * 64],
                        waits=wts)
                kv_fill[h] = ssem.now()

            kv_cons = [None, None]
            dma_kv(0)
            dma_kv(1)
            ps_guard = {}
            for h in range(H):
                ch, base, kvh = h // 2, (h % 2) * 64, h // 4
                par = h % 2
                live = LIVE[h]
                kc0 = live[0]
                vh_done = {}
                nl = len(live)
                ngrp = (nl + 3) // 4
                expg = {}
                pvg = {}
                if h > 0:
                    expg[-1] = prev_expg
                    pvg[-1] = prev_pvg
                for g in range(ngrp):
                    idxs = list(range(4 * g, min(4 * g + 4, nl)))
                    cnt = len(idxs)
                    for idx in idxs:
                        kc = live[idx]
                        slot = idx % 8
                        acol = alpha_sb[:, kc * H + h: kc * H + h + 1]
                        vh_w = [kv_fill[h]]
                        if g >= 2:
                            vh_w.append(pvg[g - 2])
                        elif h > 0:
                            vh_w.append(prev_pvg)
                        P.emit("vector", lambda e_, slot=slot, ac=acol, kc=kc,
                               par=par, kc0=kc0: e_.tensor_scalar(
                            out=vhb[:, slot * 65:slot * 65 + 64],
                            in0=arena[:, A_VH + par * 4096 + (kc - kc0) * 64:
                                      A_VH + par * 4096 + (kc - kc0) * 64 + 64],
                            scalar1=ac, scalar2=None, op0=ALU.mult),
                            waits=vh_w, inc=S["DV"])
                        P.emit("vector", lambda e_, slot=slot, ac=acol: e_.tensor_copy(
                            out=vhb[:, slot * 65 + 64:slot * 65 + 65], in_=ac),
                            inc=S["DV"])
                        vh_done[idx] = S["DV"].now()
                    for i, idx in enumerate(idxs):
                        kc = live[idx]
                        PS_T = P0 if i < 2 else P1
                        off = (i % 2) * 512
                        wts = []
                        if i == 0:
                            wts = [kv_fill[h], q_ready]
                            if (g - 1) in expg:
                                wts.append(expg[g - 1])
                        P.emit("tensor", lambda e_, PS_T=PS_T, off=off, base=base,
                               ch=ch, kc=kc, kc0=kc0: e_.matmul(
                            PS_T[:, off:off + 512],
                            arena[base:base + 64, A_KH + (kc - kc0) * 128:
                                  A_KH + (kc - kc0) * 128 + 128],
                            arena[base:base + 64, A_Q + ch * 512:A_Q + (ch + 1) * 512],
                            start=True, stop=True, skip_group_check=True),
                            waits=wts, inc=S["PE"] if i == cnt - 1 else None)
                    qkg = S["PE"].now()
                    e_w = [qkg]
                    if (g - 1) in pvg:
                        e_w.append(pvg[g - 1])
                    n0 = min(cnt, 2) * 512
                    P.emit("scalar", lambda e_, n0=n0: e_.activation(
                        out=arena[:, A_PS:A_PS + n0], in_=P0[:, 0:n0], func=AF.Exp),
                        waits=e_w, inc=S["SC"] if cnt <= 2 else None)
                    if cnt > 2:
                        n1 = (cnt - 2) * 512
                        P.emit("scalar", lambda e_, n1=n1: e_.activation(
                            out=arena[:, A_PS + 1024:A_PS + 1024 + n1],
                            in_=P1[:, 0:n1], func=AF.Exp),
                            inc=S["SC"])
                    expg[g] = S["SC"].now()
                    for i, idx in enumerate(idxs):
                        slot = idx % 8
                        wts = [vh_done[idx]]
                        if i == 0:
                            wts.append(expg[g])
                            if idx == 0 and o_evac is not None:
                                wts.append(o_evac)
                        P.emit("tensor", lambda e_, slot=slot, i=i, idx=idx,
                               nl=nl: e_.matmul(
                            P2[0:65, 0:512], vhb[:, slot * 65:(slot + 1) * 65],
                            arena[:, A_PS + i * 512:A_PS + (i + 1) * 512],
                            start=(idx == 0), stop=(idx == nl - 1),
                            skip_group_check=True),
                            waits=wts, inc=S["PE"] if i == cnt - 1 else None)
                    pvg[g] = S["PE"].now()
                kv_cons[par] = [qkg, vh_done[nl - 1]]
                if h + 2 < H:
                    dma_kv(h + 2)
                pv_done = S["PE"].now()
                prev_pvg = pvg[ngrp - 1]
                prev_expg = expg[ngrp - 1]
                P.emit("vector", lambda e_: e_.reciprocal(
                    out=af32[0:1, F_U:F_U + 512], in_=P2[64:65, 0:512]),
                    waits=[pv_done], inc=S["DV"])
                P.emit("tensor", lambda e_: e_.matmul(
                    P3[0:64, 0:512], ones_r[:, 0:64], af32[0:1, F_U:F_U + 512],
                    start=True, stop=True),
                    waits=[S["DV"].now(), p3_free], inc=S["PE"])
                P.emit("vector", lambda e_: e_.tensor_copy(
                    out=af32[0:64, F_T:F_T + 512], in_=P3[0:64, 0:512]),
                    waits=[S["PE"].now()], inc=S["DV"])
                p3_free = S["DV"].now()
                P.emit("vector", lambda e_, base=base, ch=ch: e_.tensor_tensor(
                    out=arena[base:base + 64, A_OC + ch * 512:A_OC + (ch + 1) * 512],
                    in0=P2[0:64, 0:512], in1=af32[0:64, F_T:F_T + 512], op=ALU.mult),
                    inc=S["DV"])
                o_evac = S["DV"].now()
            x2add = [None, None]
            for dout in range(DC):
                asem = S["AWA" if dout % 2 == 0 else "AWB"]
                buf = wgbuf if dout % 2 == 0 else w1buf
                dma("gpsimd", asem,
                    buf[:, 0:1024].rearrange("p (dc j) -> p dc j", dc=DC),
                    wo_t[:, :, dout * 128:(dout + 1) * 128],
                    waits=[S["PE"].now()])
                ww = asem.now()
                half = dout % 2
                for fc in range(DC):
                    wts = [ww, o_evac] if fc == 0 else []
                    if fc == 0 and x2add[half] is not None:
                        wts.append(x2add[half])
                    P.emit("tensor", lambda e_, fc=fc, buf=buf, half=half: e_.matmul(
                        P1[:, half * 512:(half + 1) * 512],
                        buf[:, fc * 128:(fc + 1) * 128],
                        arena[:, A_OC + fc * 512:A_OC + (fc + 1) * 512],
                        start=(fc == 0), stop=(fc == DC - 1),
                        skip_group_check=True),
                        waits=wts,
                        inc=S["PE"] if fc == DC - 1 else None)
                wo_done = S["PE"].now()
                P.emit("vector", lambda e_, dout=dout, half=half: e_.tensor_tensor(
                    out=x2s(dout, c0, 512), in0=x2s(dout, c0, 512),
                    in1=P1[:, half * 512:(half + 1) * 512], op=ALU.add),
                    waits=[wo_done], inc=S["DV"])
                x2add[half] = S["DV"].now()

        stage = globals().get("_STAGE", "full")
        if stage == "xrdw":
            emit_ffn(0, nexp=1, sub="xrd")
            for nm in ("WGA", "WGB", "W1A", "W1B", "WRA"):
                P.wait("gpsimd", [S[nm].now()])
            P.flush()
        if stage == "s0":
            pass
        elif stage == "xrdw":
            pass
        elif stage in ("pa", "wbs") or stage.startswith("x"):
            emit_ffn(0, nexp=1, sub=stage)
            P.flush()
        elif stage.startswith("e"):
            emit_ffn(0, nexp=int(stage[1:]))
            P.flush()
        else:
            emit_ffn(0)
            P.flush()
            if stage != "t0":
                emit_ffn(1024)
                P.flush()
        if stage in ("f1a", "full"):
            cc_ready = emit_kv()
            P.flush()
            for qc in range(4):
                emit_qc(qc, cc_ready)
                P.flush()
        if stage == "full":
            emit_ffn(0)
            P.flush()
            emit_ffn(1024)
            P.flush()
        for dc in range(DC):
            dma("sync", S["ST"], out_t[dc], x2[:, dc * T:(dc + 1) * T])
        P.wait("sync", [S["ST"].now()])
        P.flush()

    return nc


_NC = None


def _get_nc():
    global _NC
    if _NC is None:
        _NC = build_nc()
    return _NC


def kernel(L0_attn_x, L0_attn_Wq, L0_attn_Wk, L0_attn_Wv, L0_attn_Wo,
           L0_ffn_Wgate, L0_ffn_bgate, L0_ffn_Wg, L0_ffn_bg,
           L0_ffn_W1, L0_ffn_b1, L0_ffn_W2, L0_ffn_b2):
    x = np.asarray(L0_attn_x, dtype=np.float32)
    f32c = lambda a: np.ascontiguousarray(np.asarray(a, dtype=np.float32))
    shared = {
        "Wq": f32c(L0_attn_Wq), "Wk": f32c(L0_attn_Wk), "Wv": f32c(L0_attn_Wv),
        "Wo": f32c(L0_attn_Wo), "Wg": f32c(L0_ffn_Wg), "W1": f32c(L0_ffn_W1),
        "W2": f32c(L0_ffn_W2),
        "Wgate_img": np.ascontiguousarray(
            f32c(L0_ffn_Wgate).reshape(DC, 128, E).transpose(1, 0, 2).reshape(128, DC * E)),
        "bgate_r": f32c(L0_ffn_bgate).reshape(1, E),
        "bg_img": np.ascontiguousarray(
            f32c(L0_ffn_bg).reshape(E, MC, 128).transpose(2, 0, 1).reshape(128, E * MC)),
        "b1_img": np.ascontiguousarray(
            f32c(L0_ffn_b1).reshape(E, MC, 128).transpose(2, 0, 1).reshape(128, E * MC)),
        "b2_img": np.ascontiguousarray(
            f32c(L0_ffn_b2).reshape(E, DC, 128).transpose(2, 0, 1).reshape(128, E * DC)),
    }
    in_maps = []
    for c in range(NCORE):
        sl = dict(shared)
        sl.update(_host_consts(c))
        xc = np.concatenate([x[0, TPB * c:TPB * (c + 1)],
                             x[1, TPB * c:TPB * (c + 1)]], axis=0)
        sl["xT"] = np.ascontiguousarray(xc.T)
        in_maps.append(sl)
    nc = _get_nc()
    res = run_bass_kernel_spmd(nc, in_maps, core_ids=list(range(NCORE)))
    out = np.empty((B, L, D), dtype=np.float32)
    for c, r in enumerate(res.results):
        oc = np.asarray(r["out"])
        out[0, TPB * c:TPB * (c + 1)] = oc[:, 0:TPB].T
        out[1, TPB * c:TPB * (c + 1)] = oc[:, TPB:T].T
    return out


# version 09
# speedup vs baseline: 1.1141x; 1.0401x over previous
import contextlib
import numpy as np
import ml_dtypes

import concourse.bass as bass
import concourse.mybir as mybir
from concourse.bass_utils import run_bass_kernel_spmd

F32 = mybir.dt.float32
BF16 = mybir.dt.bfloat16
AF = mybir.ActivationFunctionType
ALU = mybir.AluOpType
AX = mybir.AxisListType
bfl = ml_dtypes.bfloat16

B, L, D = 2, 8192, 1024
H, KV, HD = 16, 4, 64
E, M = 8, 4096
NCORE = 8
TPB = L // NCORE
T = 2 * TPB
DC = D // 128
MC = M // 128
EPS = 1e-5

ENGINES = ("sync", "tensor", "scalar", "vector", "gpsimd")


def _alpha_full():
    pos = np.arange(L, dtype=np.float64)
    s = np.asarray([2.0 ** (-0.5 * (i + 1)) for i in range(H)], dtype=np.float64)
    return np.exp(s[None, :] * (pos[:, None] - (L - 1))).astype(np.float32)


_ALPHA = _alpha_full()
LIVE = [
    [kc for kc in range(L // 128) if np.any(_ALPHA[kc * 128:(kc + 1) * 128, h] != 0.0)]
    for h in range(H)
]


def _host_consts(core):
    pos = np.arange(TPB * core, TPB * (core + 1), dtype=np.float64)
    d = np.arange(128)
    f = 1.0 / (10000.0 ** (2.0 * ((d % 64) % 32) / 64.0))
    ang = pos[None, :] * f[:, None]
    cos, sin = np.cos(ang), np.sin(ang)
    a = _ALPHA.reshape(64, 128, H).transpose(1, 0, 2).reshape(128, 64 * H)
    R = np.zeros((128, 128), dtype=np.float32)
    for o in (0, 64):
        for m in range(32):
            R[o + m + 32, o + m] = -1.0
            R[o + m, o + m + 32] = 1.0
    return {
        "cosq": (cos / 8.0).astype(bfl),
        "sinq": (sin / 8.0).astype(bfl),
        "cosk": cos.astype(bfl),
        "sink": sin.astype(bfl),
        "alpha_img": np.ascontiguousarray(a),
        "rot_b": R.astype(bfl),
        "ident_b": np.eye(128, dtype=np.float32).astype(bfl),
        "ident_f": np.eye(128, dtype=np.float32),
    }


class Sem:
    def __init__(self, nc, stack, name):
        self.h = stack.enter_context(nc.semaphore(name))
        self.n = 0

    def now(self):
        return (self.h, self.n)


class Prog:

    def __init__(self, nc):
        self.nc = nc
        self.steps = []

    def emit(self, eng, fn, waits=(), inc=None, k=1):
        ws = tuple((h, v) for (h, v) in waits if v > 0)
        ih = None
        if inc is not None:
            inc.n += k
            ih = inc.h
        self.steps.append((eng, ws, fn, ih, k))

    def wait(self, eng, waits):
        self.emit(eng, None, waits=waits)

    def flush(self):
        steps = self.steps
        self.steps = []
        with self.nc.Block() as block:
            for name in ENGINES:
                def runner(e, name=name):
                    for (eng, ws, fn, ih, k) in steps:
                        if eng != name:
                            continue
                        for (h, v) in ws:
                            e.wait_ge(h, v)
                        ins = fn(e) if fn is not None else None
                        if ih is not None:
                            assert ins is not None
                            ins.then_inc(ih, k)
                getattr(block, name)(runner)


def build_nc():
    nc = bass.Bass("TRN2", target_bir_lowering=False)

    xT = nc.dram_tensor("xT", [D, T], F32, kind="ExternalInput").ap()
    Wq = nc.dram_tensor("Wq", [D, D], F32, kind="ExternalInput").ap()
    Wk = nc.dram_tensor("Wk", [D, KV * HD], F32, kind="ExternalInput").ap()
    Wv = nc.dram_tensor("Wv", [D, KV * HD], F32, kind="ExternalInput").ap()
    Wo = nc.dram_tensor("Wo", [D, D], F32, kind="ExternalInput").ap()
    Wg = nc.dram_tensor("Wg", [E, D, M], F32, kind="ExternalInput").ap()
    W1 = nc.dram_tensor("W1", [E, D, M], F32, kind="ExternalInput").ap()
    W2 = nc.dram_tensor("W2", [E, M, D], F32, kind="ExternalInput").ap()
    Wgate_img = nc.dram_tensor("Wgate_img", [128, DC * E], F32, kind="ExternalInput").ap()
    bgate_r = nc.dram_tensor("bgate_r", [1, E], F32, kind="ExternalInput").ap()
    bg_img = nc.dram_tensor("bg_img", [128, E * MC], F32, kind="ExternalInput").ap()
    b1_img = nc.dram_tensor("b1_img", [128, E * MC], F32, kind="ExternalInput").ap()
    b2_img = nc.dram_tensor("b2_img", [128, E * DC], F32, kind="ExternalInput").ap()
    cosq = nc.dram_tensor("cosq", [128, TPB], BF16, kind="ExternalInput").ap()
    sinq = nc.dram_tensor("sinq", [128, TPB], BF16, kind="ExternalInput").ap()
    cosk = nc.dram_tensor("cosk", [128, TPB], BF16, kind="ExternalInput").ap()
    sink = nc.dram_tensor("sink", [128, TPB], BF16, kind="ExternalInput").ap()
    alpha_img = nc.dram_tensor("alpha_img", [128, 64 * H], F32, kind="ExternalInput").ap()
    rot_b = nc.dram_tensor("rot_b", [128, 128], BF16, kind="ExternalInput").ap()
    ident_b = nc.dram_tensor("ident_b", [128, 128], BF16, kind="ExternalInput").ap()
    ident_f = nc.dram_tensor("ident_f", [128, 128], F32, kind="ExternalInput").ap()
    out_d = nc.dram_tensor("out", [D, T], F32, kind="ExternalOutput").ap()

    ccki = nc.dram_tensor("ccki", [KV * HD, T], BF16)
    ccko = nc.dram_tensor("ccko", [NCORE * KV * HD, T], BF16)
    ccvi = nc.dram_tensor("ccvi", [T, KV * HD], BF16)
    ccvo = nc.dram_tensor("ccvo", [NCORE * T, KV * HD], BF16)

    wg_t = Wg.rearrange("e (dc p) m -> e p dc m", p=128)
    w1_t = W1.rearrange("e (dc p) m -> e p dc m", p=128)
    w2_t = W2.rearrange("e (mc p) d -> e p mc d", p=128)
    wq_t = Wq.rearrange("(dc p) m -> p dc m", p=128)
    wk_t = Wk.rearrange("(dc p) m -> p dc m", p=128)
    wv_t = Wv.rearrange("(dc p) m -> p dc m", p=128)
    wo_t = Wo.rearrange("(dc p) m -> p dc m", p=128)
    out_t = out_d.rearrange("(dc p) t -> dc p t", p=128)

    st = contextlib.ExitStack()
    with st:
        E_ = st.enter_context
        x2 = E_(nc.sbuf_tensor("x2", [128, DC * T], F32))
        arena = E_(nc.sbuf_tensor("arena", [128, 43008], BF16))
        af32 = E_(nc.sbuf_tensor("af32", [128, 3072], F32))
        alpha_sb = E_(nc.sbuf_tensor("alpha_sb", [128, 64 * H], F32))
        wgbuf = E_(nc.sbuf_tensor("wgbuf", [128, 2 * 1024], BF16))
        w1buf = E_(nc.sbuf_tensor("w1buf", [128, 2 * 1024], BF16))
        w2buf = E_(nc.sbuf_tensor("w2buf", [128, 3 * 1024], BF16))
        ktb = E_(nc.sbuf_tensor("ktb", [128, 4 * 128], BF16))
        vtb = E_(nc.sbuf_tensor("vtb", [128, 4 * 64], BF16))
        vhb = E_(nc.sbuf_tensor("vhb", [128, 8 * 65], BF16))
        bg_sb = E_(nc.sbuf_tensor("bg_sb", [128, E * MC], F32))
        b1_sb = E_(nc.sbuf_tensor("b1_sb", [128, E * MC], F32))
        b2_sb = E_(nc.sbuf_tensor("b2_sb", [128, E * DC], F32))
        wgate_f = E_(nc.sbuf_tensor("wgate_f", [128, DC * E], F32))
        wgate_hi = E_(nc.sbuf_tensor("wgate_hi", [128, DC * E], BF16))
        wgate_lo = E_(nc.sbuf_tensor("wgate_lo", [128, DC * E], BF16))
        wgate_d = E_(nc.sbuf_tensor("wgate_d", [128, DC * E], F32))
        bgate_f = E_(nc.sbuf_tensor("bgate_f", [1, E], F32))
        bgate_hi = E_(nc.sbuf_tensor("bgate_hi", [1, E], BF16))
        bgate_lo = E_(nc.sbuf_tensor("bgate_lo", [1, E], BF16))
        bgate_d = E_(nc.sbuf_tensor("bgate_d", [1, E], F32))
        rot_sb = E_(nc.sbuf_tensor("rot_sb", [128, 128], BF16))
        idb_sb = E_(nc.sbuf_tensor("idb_sb", [128, 128], BF16))
        idf_sb = E_(nc.sbuf_tensor("idf_sb", [128, 128], F32))
        ones_c = E_(nc.sbuf_tensor("ones_c", [128, 1], F32))
        ones_r = E_(nc.sbuf_tensor("ones_r", [1, 128], F32))
        ones_rb = E_(nc.sbuf_tensor("ones_rb", [1, 128], BF16))
        ceps = E_(nc.sbuf_tensor("ceps", [1, 1], F32))
        cm05 = E_(nc.sbuf_tensor("cm05", [1, 1], F32))
        m_row = E_(nc.sbuf_tensor("m_row", [1, 1024], F32))
        v_row = E_(nc.sbuf_tensor("v_row", [1, 1024], F32))
        w_row = E_(nc.sbuf_tensor("w_row", [1, 2048], F32))
        zsb = E_(nc.sbuf_tensor("zsb", [128, 64], F32))
        t8 = E_(nc.sbuf_tensor("t8", [128, 64], F32))
        zm8 = E_(nc.sbuf_tensor("zm8", [128, 64], F32))
        msk1 = E_(nc.sbuf_tensor("msk1", [128, 64], F32))
        msk2 = E_(nc.sbuf_tensor("msk2", [128, 64], F32))
        m18 = E_(nc.sbuf_tensor("m18", [128, 8], F32))
        m28 = E_(nc.sbuf_tensor("m28", [128, 8], F32))
        d8 = E_(nc.sbuf_tensor("d8", [128, 8], F32))
        w1h = E_(nc.sbuf_tensor("w1h", [128, 8], F32))
        w2h = E_(nc.sbuf_tensor("w2h", [128, 8], F32))
        wvec = E_(nc.sbuf_tensor("wvec", [128, 64], F32))
        wT_sb = E_(nc.sbuf_tensor("wT_sb", [64, 128], F32))

        P0 = E_(nc.psum_tensor("P0", [128, 1024], F32))
        P1 = E_(nc.psum_tensor("P1", [128, 1024], F32))
        P2 = E_(nc.psum_tensor("P2", [128, 1024], F32))
        P3 = E_(nc.psum_tensor("P3", [128, 512], F32))
        PT = E_(nc.psum_tensor("PT", [128, 128], BF16))

        names = ("IN", "PE", "SC", "DV", "PL", "VB", "SB", "CC", "ST", "KVC",
                 "WGA", "WGB", "W1A", "W1B", "W2A", "W2B", "W2C",
                 "KVS0", "KVS1", "KVS2", "KVS3", "WRA", "WRB", "AWA", "AWB")
        S = {n: Sem(nc, st, "s" + n) for n in names}

        P = Prog(nc)

        A_H, A_XT, A_LO, A_T1 = 0, 32768, 40960, 41984
        A_Q, A_OC, A_PS, A_KC, A_VT, A_VF, A_KB = 0, 4096, 8192, 8192, 12288, 16384, 20480
        A_COS = 21504
        A_KH = 25600
        A_VH = 12288
        A_XTA = 38912
        F_T, F_U, F_XSQ = 0, 1024, 2048

        def x2s(dc, c0, w):
            return x2[:, dc * T + c0: dc * T + c0 + w]

        def dma(eng, sem, outap, inap, waits=()):
            P.emit(eng, lambda e, o=outap, i=inap: e.dma_start(out=o, in_=i),
                   waits=waits, inc=sem, k=16)

        dma("sync", S["IN"], x2[:].rearrange("p (dc t) -> p dc t", dc=DC),
            xT.rearrange("(dc p) t -> p dc t", p=128))
        dma("sync", S["IN"], alpha_sb[:], alpha_img)
        dma("sync", S["IN"], rot_sb[:], rot_b)
        dma("sync", S["IN"], idb_sb[:], ident_b)
        dma("sync", S["IN"], idf_sb[:], ident_f)
        dma("sync", S["IN"], bg_sb[:], bg_img)
        dma("sync", S["IN"], b1_sb[:], b1_img)
        dma("sync", S["IN"], b2_sb[:], b2_img)
        dma("sync", S["IN"], wgate_f[:], Wgate_img)
        dma("sync", S["IN"], bgate_f[:], bgate_r)
        all_in = S["IN"].now()

        def setup_v(v):
            v.memset(ones_c[:], 1.0)
            v.memset(ones_r[:], 1.0)
            v.memset(ones_rb[:], 1.0)
            v.memset(ceps[:], EPS)
            return v.memset(cm05[:], -0.5)
        P.emit("vector", setup_v, waits=[all_in], inc=S["VB"])

        P.emit("gpsimd", lambda e: e.tensor_copy(out=wgate_hi[:], in_=wgate_f[:]),
               waits=[all_in], inc=S["PL"])
        P.emit("gpsimd", lambda e: e.tensor_copy(out=bgate_hi[:], in_=bgate_f[:]),
               inc=S["PL"])
        P.emit("vector", lambda e: e.tensor_tensor(
            out=wgate_d[:], in0=wgate_f[:], in1=wgate_hi[:], op=ALU.subtract),
            waits=[S["PL"].now(), S["VB"].now()], inc=S["VB"])
        P.emit("vector", lambda e: e.tensor_tensor(
            out=bgate_d[:], in0=bgate_f[:], in1=bgate_hi[:], op=ALU.subtract),
            waits=[S["VB"].now()], inc=S["VB"])
        P.emit("gpsimd", lambda e: e.tensor_copy(out=wgate_lo[:], in_=wgate_d[:]),
               waits=[S["VB"].now()], inc=S["PL"])
        P.emit("gpsimd", lambda e: e.tensor_copy(out=bgate_lo[:], in_=bgate_d[:]),
               inc=S["PL"])
        P.flush()

        def emit_ln(c0, width, gate=False, xt_base=None):
            nh = width // 512
            if xt_base is None:
                xt_base = A_XT
            for n in range(nh):
                for dc in range(DC):
                    xs = x2s(dc, c0 + n * 512, 512)
                    sq = af32[:, F_XSQ + (dc % 2) * 512: F_XSQ + (dc % 2) * 512 + 512]
                    P.emit("scalar", lambda e, o=sq, i=xs: e.activation(
                        out=o, in_=i, func=AF.Square),
                        waits=[S["PE"].now()], inc=S["SC"])
                    P.emit("tensor", lambda e, i=xs, n=n, dc=dc: e.matmul(
                        P0[0:1, n * 512:(n + 1) * 512], ones_c[:], i,
                        start=(dc == 0), stop=(dc == DC - 1), skip_group_check=True))
                    P.emit("tensor", lambda e, q=sq, n=n, dc=dc: e.matmul(
                        P1[0:1, n * 512:(n + 1) * 512], ones_c[:], q,
                        start=(dc == 0), stop=(dc == DC - 1), skip_group_check=True),
                        waits=[S["SC"].now()], inc=S["PE"])
            stats = S["PE"].now()
            mr = m_row[:, 0:width]
            vr = v_row[:, 0:width]
            qr = af32[0:1, F_XSQ:F_XSQ + width]
            P.emit("vector", lambda e: e.tensor_scalar(
                out=mr, in0=P0[0:1, 0:width], scalar1=1.0 / D, scalar2=None,
                op0=ALU.mult), waits=[stats, S["VB"].now()], inc=S["VB"])
            P.emit("vector", lambda e: e.tensor_scalar(
                out=vr, in0=P1[0:1, 0:width], scalar1=1.0 / D, scalar2=None,
                op0=ALU.mult), waits=[S["VB"].now()], inc=S["VB"])
            P.emit("vector", lambda e: e.scalar_tensor_tensor(
                out=qr, in0=mr, scalar=-1.0, in1=mr, op0=ALU.mult, op1=ALU.mult),
                waits=[S["VB"].now()], inc=S["VB"])
            P.emit("vector", lambda e: e.tensor_tensor(
                out=vr, in0=vr, in1=qr, op=ALU.add),
                waits=[S["VB"].now()], inc=S["VB"])
            row_dv = S["VB"].now()
            P.emit("scalar", lambda e: e.activation(
                out=vr, in_=vr, func=AF.Ln, bias=ceps[:]),
                waits=[row_dv, S["SB"].now()], inc=S["SB"])
            P.emit("scalar", lambda e: e.activation(
                out=vr, in_=vr, func=AF.Exp, scale=cm05[:]),
                waits=[S["SB"].now()], inc=S["SB"])
            P.emit("vector", lambda e: e.tensor_tensor(
                out=mr, in0=mr, in1=vr, op=ALU.mult),
                waits=[S["SB"].now(), S["VB"].now()], inc=S["VB"])
            rows = S["VB"].now()
            for n in range(nh):
                P.emit("tensor", lambda e, n=n: e.matmul(
                    P0[:, n * 512:(n + 1) * 512], ones_r[:],
                    v_row[:, n * 512:(n + 1) * 512], start=True, stop=True),
                    waits=[rows] if n == 0 else ())
                P.emit("tensor", lambda e, n=n: e.matmul(
                    P1[:, n * 512:(n + 1) * 512], ones_r[:],
                    m_row[:, n * 512:(n + 1) * 512], start=True, stop=True),
                    inc=S["PE"])
            bcast = S["PE"].now()
            for dc in range(DC):
                tbuf = af32[:, F_T:F_T + width]
                xtsl = arena[:, xt_base + dc * width: xt_base + (dc + 1) * width]
                losl = arena[:, A_LO: A_LO + width]
                P.emit("vector", lambda e, dc=dc, tb=tbuf: e.tensor_tensor(
                    out=tb, in0=x2s(dc, c0, width), in1=P0[:, 0:width], op=ALU.mult),
                    waits=[bcast] + ([S["PE"].now()] if gate else []), inc=S["DV"])
                P.emit("vector", lambda e, tb=tbuf: e.tensor_tensor(
                    out=tb, in0=tb, in1=P1[:, 0:width], op=ALU.subtract),
                    inc=S["DV"])
                P.emit("vector", lambda e, o=xtsl, tb=tbuf: e.tensor_copy(
                    out=o, in_=tb), inc=S["DV"])
                if gate:
                    P.emit("vector", lambda e, o=losl, tb=tbuf, x=xtsl: e.tensor_tensor(
                        out=o, in0=tb, in1=x, op=ALU.subtract), inc=S["DV"])
                    g_dv = S["DV"].now()
                    ntch = width // 128
                    for tch in range(ntch):
                        xsl = arena[:, xt_base + dc * width + tch * 128:
                                    xt_base + dc * width + tch * 128 + 128]
                        lsl = arena[:, A_LO + tch * 128: A_LO + tch * 128 + 128]
                        zr = P3[:, tch * 8:(tch + 1) * 8]
                        for pi, (lhs, rhs) in enumerate((
                                (xsl, wgate_hi), (xsl, wgate_lo), (lsl, wgate_hi))):
                            P.emit("tensor", lambda e, zr=zr, l=lhs, r=rhs, dc=dc,
                                   pi=pi, tch=tch: e.matmul(
                                zr, l, r[:, dc * 8:(dc + 1) * 8],
                                start=(dc == 0 and pi == 0 and tch == 0), stop=False,
                                skip_group_check=True),
                                waits=[g_dv] if (pi == 0) else (), inc=S["PE"])
            xt_ready = S["DV"].now()
            wT_ready = None
            if gate:
                ntch = width // 128
                for tch in range(ntch):
                    zr = P3[:, tch * 8:(tch + 1) * 8]
                    P.emit("tensor", lambda e, zr=zr: e.matmul(
                        zr, ones_rb[:], bgate_hi[:], start=False, stop=False,
                        skip_group_check=True))
                    P.emit("tensor", lambda e, zr=zr, tch=tch: e.matmul(
                        zr, ones_rb[:], bgate_lo[:], start=False, stop=True,
                        skip_group_check=True), inc=S["PE"])
                z_done = S["PE"].now()
                nt = ntch * 8

                def vb(fn, extra=()):
                    P.emit("vector", fn, waits=[S["VB"].now()] + list(extra),
                           inc=S["VB"])

                z3 = zsb[:, 0:nt].rearrange("p (t e) -> p t e", e=8)
                zm3 = zm8[:, 0:nt].rearrange("p (t e) -> p t e", e=8)
                mk13 = msk1[:, 0:nt].rearrange("p (t e) -> p t e", e=8)
                mk23 = msk2[:, 0:nt].rearrange("p (t e) -> p t e", e=8)
                m1b = m18[:, 0:ntch].rearrange("p (t e) -> p t e", e=1) \
                    .broadcast_to([128, ntch, 8])
                m2b = m28[:, 0:ntch].rearrange("p (t e) -> p t e", e=1) \
                    .broadcast_to([128, ntch, 8])
                w1b = w1h[:, 0:ntch].rearrange("p (t e) -> p t e", e=1) \
                    .broadcast_to([128, ntch, 8])
                w2b = w2h[:, 0:ntch].rearrange("p (t e) -> p t e", e=1) \
                    .broadcast_to([128, ntch, 8])
                vb(lambda e: e.tensor_copy(out=zsb[:, 0:nt], in_=P3[:, 0:nt]),
                   extra=[z_done])
                vb(lambda e: e.tensor_reduce(out=m18[:, 0:ntch], in_=z3, axis=AX.X,
                                             op=ALU.max))
                vb(lambda e: e.tensor_tensor(out=mk13, in0=z3, in1=m1b,
                                             op=ALU.is_equal))
                vb(lambda e: e.tensor_scalar(out=t8[:, 0:nt], in0=msk1[:, 0:nt],
                                             scalar1=-1e30, scalar2=None, op0=ALU.mult))
                vb(lambda e: e.tensor_tensor(out=zm8[:, 0:nt], in0=zsb[:, 0:nt],
                                             in1=t8[:, 0:nt], op=ALU.add))
                vb(lambda e: e.tensor_reduce(out=m28[:, 0:ntch], in_=zm3, axis=AX.X,
                                             op=ALU.max))
                vb(lambda e: e.tensor_tensor(out=mk23, in0=zm3, in1=m2b,
                                             op=ALU.is_equal))
                vb(lambda e: e.tensor_tensor(out=d8[:, 0:ntch], in0=m18[:, 0:ntch],
                                             in1=m28[:, 0:ntch], op=ALU.subtract))
                P.emit("scalar", lambda e: e.activation(
                    out=w1h[:, 0:ntch], in_=d8[:, 0:ntch], func=AF.Sigmoid),
                    waits=[S["VB"].now(), S["SB"].now()], inc=S["SB"])
                vb(lambda e: e.tensor_scalar(out=w1h[:, 0:ntch], in0=w1h[:, 0:ntch],
                                             scalar1=0.5, scalar2=None, op0=ALU.mult),
                   extra=[S["SB"].now()])
                vb(lambda e: e.tensor_scalar(out=w2h[:, 0:ntch], in0=w1h[:, 0:ntch],
                                             scalar1=-1.0, scalar2=0.5,
                                             op0=ALU.mult, op1=ALU.add))
                vb(lambda e: e.tensor_tensor(out=mk13, in0=mk13, in1=w1b, op=ALU.mult))
                vb(lambda e: e.tensor_tensor(out=mk23, in0=mk23, in1=w2b, op=ALU.mult))
                vb(lambda e: e.tensor_tensor(out=wvec[:, 0:nt], in0=msk1[:, 0:nt],
                                             in1=msk2[:, 0:nt], op=ALU.add))
                wv_done = S["VB"].now()
                P.emit("tensor", lambda e: e.transpose(
                    P3[0:nt, 0:128], wvec[:, 0:nt], idf_sb[:]),
                    waits=[wv_done], inc=S["PE"])
                P.emit("vector", lambda e: e.tensor_copy(
                    out=wT_sb[0:nt, :], in_=P3[0:nt, 0:128]),
                    waits=[S["PE"].now(), S["VB"].now()], inc=S["VB"])
                wT_ready = S["VB"].now()
            return xt_ready, wT_ready

        wr_state = [None, None]

        def emit_ffn(c0, nexp=E, sub='all'):
            xt_ready, wT_ready = emit_ln(c0, 1024, gate=True)
            wg_cons = [None, None]
            w1_cons = [None, None]
            w2_cons = [None, None, None]
            wg_fill = [None, None]
            w1_fill = [None, None]
            w2_fill = [None, None, None]
            wr_fill = [None, None]
            p3_free = S["VB"].now()
            silu_done = None
            h_done = None
            u_done = [None, None]
            sub_r = sub in ('all', 'pa', 'wbs') or 'r' in sub
            sub_d = sub in ('all', 'pa', 'wbs') or 'd' in sub
            sub_m = sub in ('all', 'pa', 'wbs') or 'm' in sub
            sub_s = sub in ('all', 'pa', 'wbs') or 's' in sub
            sub_h = sub in ('all', 'pa', 'wbs') or 'h' in sub
            for e in range(nexp):
                ws = e % 2
                sem = S["WRA" if ws == 0 else "WRB"]
                if sub_r:
                  for n in range(2):
                    for tq in range(4):
                        src = wT_sb[(4 * n + tq) * 8 + e:(4 * n + tq) * 8 + e + 1, :]
                        dst = w_row[0:1, ws * 1024 + n * 512 + tq * 128:
                                    ws * 1024 + n * 512 + tq * 128 + 128]
                        dma("sync", sem, dst, src,
                            waits=[wT_ready] + ([wr_state[ws]] if wr_state[ws] else []))
                wr_fill[ws] = sem.now()
                for mc in range(MC):
                    slot = mc % 2
                    gsem = S["WGA" if slot == 0 else "WGB"]
                    osem = S["W1A" if slot == 0 else "W1B"]
                    if not sub_d:
                        continue
                    dma("gpsimd", gsem,
                        wgbuf[:, slot * 1024:(slot + 1) * 1024]
                        .rearrange("p (dc j) -> p dc j", dc=DC),
                        wg_t[e, :, :, mc * 128:(mc + 1) * 128],
                        waits=[wg_cons[slot]] if wg_cons[slot] else ())
                    wg_fill[slot] = gsem.now()
                    dma("gpsimd", osem,
                        w1buf[:, slot * 1024:(slot + 1) * 1024]
                        .rearrange("p (dc j) -> p dc j", dc=DC),
                        w1_t[e, :, :, mc * 128:(mc + 1) * 128],
                        waits=[w1_cons[slot]] if w1_cons[slot] else ())
                    w1_fill[slot] = osem.now()
                    if not sub_m:
                        continue
                    wA = [wg_fill[slot], xt_ready]
                    if silu_done is not None:
                        wA.append(silu_done)
                    for dc in range(DC):
                        for n in range(2):
                            P.emit("tensor", lambda e_, dc=dc, n=n, slot=slot: e_.matmul(
                                P0[:, n * 512:(n + 1) * 512],
                                wgbuf[:, slot * 1024 + dc * 128:slot * 1024 + (dc + 1) * 128],
                                arena[:, A_XT + dc * 1024 + n * 512:
                                      A_XT + dc * 1024 + n * 512 + 512],
                                start=(dc == 0), stop=(dc == DC - 1),
                                skip_group_check=True),
                                waits=wA if (n == 0 and dc == 0) else (),
                                inc=S["PE"] if (n == 1 and dc == DC - 1) else None)
                    a_done = S["PE"].now()
                    wB = [w1_fill[slot]]
                    if h_done is not None:
                        wB.append(h_done)
                    for dc in range(DC):
                        for n in range(2):
                            P.emit("tensor", lambda e_, dc=dc, n=n, slot=slot: e_.matmul(
                                P1[:, n * 512:(n + 1) * 512],
                                w1buf[:, slot * 1024 + dc * 128:slot * 1024 + (dc + 1) * 128],
                                arena[:, A_XT + dc * 1024 + n * 512:
                                      A_XT + dc * 1024 + n * 512 + 512],
                                start=(dc == 0), stop=(dc == DC - 1),
                                skip_group_check=True),
                                waits=wB if (n == 0 and dc == 0) else (),
                                inc=S["PE"] if (n == 1 and dc == DC - 1) else None)
                    b_done = S["PE"].now()
                    wg_cons[slot] = a_done
                    w1_cons[slot] = b_done
                    if not sub_s:
                        silu_done = a_done
                        h_done = b_done
                        continue
                    bcol = bg_sb[:, e * MC + mc: e * MC + mc + 1]
                    P.emit("scalar", lambda e_, bc=bcol: e_.activation(
                        out=arena[:, A_T1:A_T1 + 1024], in_=P0[:, 0:1024],
                        func=AF.Silu, bias=bc),
                        waits=[a_done, S["DV"].now()], inc=S["SC"])
                    silu_done = S["SC"].now()
                    if not sub_h:
                        h_done = b_done
                        continue
                    b1col = b1_sb[:, e * MC + mc: e * MC + mc + 1]
                    P.emit("vector", lambda e_, mc=mc, bc=b1col: e_.scalar_tensor_tensor(
                        out=arena[:, A_H + mc * 1024:A_H + (mc + 1) * 1024],
                        in0=P1[:, 0:1024], scalar=bc,
                        in1=arena[:, A_T1:A_T1 + 1024], op0=ALU.add, op1=ALU.mult),
                        waits=[b_done, silu_done], inc=S["DV"])
                    h_done = S["DV"].now()
                if sub == 'pa' or sub.startswith('x'):
                    continue
                wbs_done = None
                for n in range(2):
                    P.emit("tensor", lambda e_, ws=ws, n=n: e_.matmul(
                        P3[:, 0:512], ones_r[:],
                        w_row[0:1, ws * 1024 + n * 512: ws * 1024 + n * 512 + 512],
                        start=True, stop=True),
                        waits=[wr_fill[ws], p3_free], inc=S["PE"])
                    P.emit("vector", lambda e_, n=n: e_.tensor_copy(
                        out=af32[:, F_T + n * 512:F_T + n * 512 + 512],
                        in_=P3[:, 0:512]),
                        waits=[S["PE"].now()], inc=S["DV"])
                    p3_free = S["DV"].now()
                wbs_done = S["DV"].now()
                wr_state[ws] = S["PE"].now()
                if sub == 'wbs':
                    continue
                w2_fill_by = {}

                def w2dma(qg):
                    s3 = qg % 3
                    qsem = S[("W2A", "W2B", "W2C")[s3]]
                    dma("gpsimd", qsem,
                        w2buf[:, s3 * 1024:(s3 + 1) * 1024]
                        .rearrange("p (mq j) -> p mq j", mq=8),
                        w2_t[e, :, (qg % 4) * 8:(qg % 4) * 8 + 8,
                             (qg // 4) * 128:(qg // 4) * 128 + 128],
                        waits=[w2_cons[s3]] if w2_cons[s3] else ())
                    w2_fill_by[qg] = qsem.now()

                for qg in range(3):
                    w2dma(qg)
                next_qg = 3
                for dout in range(DC):
                    for j in range(MC):
                        qg = dout * 4 + j // 8
                        s3 = qg % 3
                        for n in range(2):
                            wts = []
                            if n == 0 and j % 8 == 0:
                                wts.append(w2_fill_by[qg])
                            if n == 0 and j == 0:
                                wts.append(h_done)
                                if u_done[0] is not None:
                                    wts.append(u_done[1])
                            P.emit("tensor", lambda e_, j=j, s3=s3, n=n: e_.matmul(
                                P2[:, n * 512:(n + 1) * 512],
                                w2buf[:, s3 * 1024 + (j % 8) * 128:
                                      s3 * 1024 + (j % 8) * 128 + 128],
                                arena[:, A_H + j * 1024 + n * 512:
                                      A_H + j * 1024 + n * 512 + 512],
                                start=(j == 0), stop=(j == MC - 1),
                                skip_group_check=True),
                                waits=wts,
                                inc=S["PE"] if (n == 1 and j % 8 == 7) else None)
                        if j % 8 == 7:
                            w2_cons[s3] = S["PE"].now()
                            if next_qg < 32:
                                w2dma(next_qg)
                                next_qg += 1
                    y_done = S["PE"].now()
                    b2col = b2_sb[:, e * DC + dout: e * DC + dout + 1]
                    for n in range(2):
                        P.emit("vector", lambda e_, n=n, bc=b2col: e_.scalar_tensor_tensor(
                            out=af32[:, F_U:F_U + 512],
                            in0=P2[:, n * 512:(n + 1) * 512], scalar=bc,
                            in1=af32[:, F_T + n * 512:F_T + n * 512 + 512],
                            op0=ALU.add, op1=ALU.mult),
                            waits=[y_done, wbs_done], inc=S["DV"])
                        P.emit("vector", lambda e_, dout=dout, n=n: e_.tensor_tensor(
                            out=x2s(dout, c0 + n * 512, 512),
                            in0=x2s(dout, c0 + n * 512, 512),
                            in1=af32[:, F_U:F_U + 512], op=ALU.add),
                            inc=S["DV"])
                    u_done[0] = u_done[1] = S["DV"].now()

        def emit_kv():
            for ap_, off_ in ((cosq, 0), (sinq, TPB), (cosk, 2 * TPB), (sink, 3 * TPB)):
                dma("sync", S["IN"], arena[:, A_COS + off_:A_COS + off_ + TPB], ap_)
            cos_ready = S["IN"].now()
            for tc in range(2):
                xt_ready, _ = emit_ln(tc * 1024, 1024, gate=False)
                for vc in range(2):
                    asem = S["AWA"]
                    dma("gpsimd", asem,
                        wgbuf[:, 0:1024].rearrange("p (dc j) -> p dc j", dc=DC),
                        wk_t[:, :, vc * 128:(vc + 1) * 128],
                        waits=[S["PE"].now()])
                    kw = asem.now()
                    for dc in range(DC):
                        for n in range(2):
                            P.emit("tensor", lambda e_, dc=dc, n=n: e_.matmul(
                                P1[:, n * 512:(n + 1) * 512],
                                wgbuf[:, dc * 128:(dc + 1) * 128],
                                arena[:, A_XT + dc * 1024 + n * 512:
                                      A_XT + dc * 1024 + n * 512 + 512],
                                start=(dc == 0), stop=(dc == DC - 1),
                                skip_group_check=True),
                                waits=[kw, xt_ready, S["DV"].now()]
                                if (n == 0 and dc == 0) else (),
                                inc=S["PE"] if (n == 1 and dc == DC - 1) else None)
                    kproj = S["PE"].now()
                    P.emit("vector", lambda e_: e_.tensor_copy(
                        out=arena[:, A_KB:A_KB + 1024], in_=P1[:, 0:1024]),
                        waits=[kproj], inc=S["DV"])
                    kb_done = S["DV"].now()
                    for n in range(2):
                        P.emit("tensor", lambda e_, n=n: e_.matmul(
                            P2[:, n * 512:(n + 1) * 512], rot_sb[:],
                            arena[:, A_KB + n * 512:A_KB + n * 512 + 512],
                            start=True, stop=True),
                            waits=[kb_done, S["DV"].now()] if n == 0 else (),
                            inc=S["PE"] if n == 1 else None)
                    krot = S["PE"].now()
                    P.emit("vector", lambda e_: e_.tensor_tensor(
                        out=af32[:, F_T:F_T + 1024], in0=P1[:, 0:1024],
                        in1=arena[:, A_COS + 2 * TPB:A_COS + 3 * TPB], op=ALU.mult),
                        waits=[krot, cos_ready], inc=S["DV"])
                    P.emit("vector", lambda e_: e_.tensor_tensor(
                        out=af32[:, F_U:F_U + 1024], in0=P2[:, 0:1024],
                        in1=arena[:, A_COS + 3 * TPB:A_COS + 4 * TPB], op=ALU.mult),
                        inc=S["DV"])
                    P.emit("vector", lambda e_, tc=tc, vc=vc: e_.tensor_tensor(
                        out=arena[:, A_KC + vc * 2048 + tc * 1024:
                                  A_KC + vc * 2048 + tc * 1024 + 1024],
                        in0=af32[:, F_T:F_T + 1024], in1=af32[:, F_U:F_U + 1024],
                        op=ALU.add), inc=S["DV"])
                    kf_done = S["DV"].now()
                    dma("gpsimd", S["AWB"],
                        w1buf[:, 0:1024].rearrange("p (dc j) -> p dc j", dc=DC),
                        wv_t[:, :, vc * 128:(vc + 1) * 128],
                        waits=[S["PE"].now()])
                    vw = S["AWB"].now()
                    for dc in range(DC):
                        for n in range(2):
                            P.emit("tensor", lambda e_, dc=dc, n=n: e_.matmul(
                                P1[:, n * 512:(n + 1) * 512],
                                w1buf[:, dc * 128:(dc + 1) * 128],
                                arena[:, A_XT + dc * 1024 + n * 512:
                                      A_XT + dc * 1024 + n * 512 + 512],
                                start=(dc == 0), stop=(dc == DC - 1),
                                skip_group_check=True),
                                waits=[vw, kf_done] if (n == 0 and dc == 0) else (),
                                inc=S["PE"] if (n == 1 and dc == DC - 1) else None)
                    vproj = S["PE"].now()
                    P.emit("vector", lambda e_: e_.tensor_copy(
                        out=arena[:, A_VF:A_VF + 1024], in_=P1[:, 0:1024]),
                        waits=[vproj], inc=S["DV"])
                    vf_done = S["DV"].now()
                    for tch in range(8):
                        P.emit("tensor", lambda e_, tch=tch: e_.transpose(
                            PT[:, 0:128],
                            arena[:, A_VF + tch * 128:A_VF + tch * 128 + 128],
                            idb_sb[:]),
                            waits=[vf_done, S["DV"].now()], inc=S["PE"])
                        P.emit("vector", lambda e_, tc=tc, vc=vc, tch=tch: e_.tensor_copy(
                            out=arena[:, A_VT + (tc * 8 + tch) * 256 + vc * 128:
                                      A_VT + (tc * 8 + tch) * 256 + vc * 128 + 128],
                            in_=PT[:, 0:128]),
                            waits=[S["PE"].now()], inc=S["DV"])
            vt_done = S["DV"].now()
            dma("sync", S["KVC"],
                ccki.rearrange("(c p) t -> p c t", p=128),
                arena[:, A_KC:A_KC + 4096].rearrange("p (c t) -> p c t", c=2),
                waits=[vt_done])
            dma("sync", S["KVC"],
                ccvi.rearrange("(tc p) f -> p tc f", p=128),
                arena[:, A_VT:A_VT + 4096].rearrange("p (tc f) -> p tc f", tc=16),
                waits=[vt_done])
            kvc = S["KVC"].now()
            P.emit("gpsimd", lambda e_: e_.collective_compute(
                "AllGather", ALU.bypass, replica_groups=[list(range(NCORE))],
                ins=[ccki.ap().opt()], outs=[ccko.ap().opt()]),
                waits=[kvc], inc=S["CC"])
            P.emit("gpsimd", lambda e_: e_.collective_compute(
                "AllGather", ALU.bypass, replica_groups=[list(range(NCORE))],
                ins=[ccvi.ap().opt()], outs=[ccvo.ap().opt()]),
                inc=S["CC"])
            return S["CC"].now()

        def emit_qc(qc, cc_ready):
            batch, poff = qc // 2, (qc % 2) * 512
            c0 = qc * 512
            xt_ready, _ = emit_ln(c0, 512, gate=False, xt_base=A_XTA)
            for ch in range(DC):
                asem = S["AWA" if ch % 2 == 0 else "AWB"]
                buf = wgbuf if ch % 2 == 0 else w1buf
                dma("gpsimd", asem,
                    buf[:, 0:1024].rearrange("p (dc j) -> p dc j", dc=DC),
                    wq_t[:, :, ch * 128:(ch + 1) * 128],
                    waits=[S["PE"].now()])
                qw_ = asem.now()
                for dc in range(DC):
                    P.emit("tensor", lambda e_, dc=dc, buf=buf: e_.matmul(
                        P1[:, 0:512], buf[:, dc * 128:(dc + 1) * 128],
                        arena[:, A_XTA + dc * 512:A_XTA + dc * 512 + 512],
                        start=(dc == 0), stop=(dc == DC - 1),
                        skip_group_check=True),
                        waits=[qw_, xt_ready, S["DV"].now()] if dc == 0 else (),
                        inc=S["PE"] if dc == DC - 1 else None)
                qproj = S["PE"].now()
                P.emit("vector", lambda e_: e_.tensor_copy(
                    out=arena[:, A_KB:A_KB + 512], in_=P1[:, 0:512]),
                    waits=[qproj], inc=S["DV"])
                qb_done = S["DV"].now()
                P.emit("tensor", lambda e_: e_.matmul(
                    P3[:, 0:512], rot_sb[:], arena[:, A_KB:A_KB + 512],
                    start=True, stop=True),
                    waits=[qb_done, S["DV"].now()], inc=S["PE"])
                qrot = S["PE"].now()
                P.emit("vector", lambda e_: e_.tensor_tensor(
                    out=af32[:, F_T:F_T + 512], in0=P1[:, 0:512],
                    in1=arena[:, A_COS + poff:A_COS + poff + 512], op=ALU.mult),
                    waits=[qrot], inc=S["DV"])
                P.emit("vector", lambda e_: e_.tensor_tensor(
                    out=af32[:, F_U:F_U + 512], in0=P3[:, 0:512],
                    in1=arena[:, A_COS + TPB + poff:A_COS + TPB + poff + 512], op=ALU.mult),
                    inc=S["DV"])
                P.emit("vector", lambda e_, ch=ch: e_.tensor_tensor(
                    out=arena[:, A_Q + ch * 512:A_Q + (ch + 1) * 512],
                    in0=af32[:, F_T:F_T + 512], in1=af32[:, F_U:F_U + 512],
                    op=ALU.add), inc=S["DV"])
            q_ready = S["DV"].now()
            p3_free = S["DV"].now()
            kv_fill = {}
            o_evac = None

            def dma_kv(h):
                par = h % 2
                base = par * 64
                kvh = h // 4
                kc0 = LIVE[h][0]
                ssem = S["KVS0" if par == 0 else "KVS1"]
                wts = [cc_ready]
                if kv_cons[par] is not None:
                    wts += kv_cons[par]
                r0 = kc0 // 8
                for r in range(r0, 8):
                    jlo = max(kc0 - 8 * r, 0)
                    dma("sync", ssem,
                        arena[base:base + 64,
                              A_KH + (8 * r + jlo - kc0) * 128:
                              A_KH + (8 * r + 8 - kc0) * 128],
                        ccko.ap()[256 * r + 128 * (kvh // 2) + 64 * (kvh % 2):
                                  256 * r + 128 * (kvh // 2) + 64 * (kvh % 2) + 64,
                                  batch * 1024 + 128 * jlo:(batch + 1) * 1024],
                        waits=wts)
                    dma("sync", ssem,
                        arena[:, A_VH + par * 4096 + (8 * r + jlo - kc0) * 64:
                              A_VH + par * 4096 + (8 * r + 8 - kc0) * 64]
                        .rearrange("p (j f) -> p j f", f=64),
                        ccvo.ap().rearrange("(rb j p) f -> rb p j f", p=128, j=16)
                        [r, :, 8 * batch + jlo:8 * batch + 8,
                         kvh * 64:(kvh + 1) * 64],
                        waits=wts)
                kv_fill[h] = ssem.now()

            kv_cons = [None, None]
            dma_kv(0)
            dma_kv(1)
            ps_guard = {}
            for h in range(H):
                ch, base, kvh = h // 2, (h % 2) * 64, h // 4
                par = h % 2
                live = LIVE[h]
                kc0 = live[0]
                vh_done = {}
                nl = len(live)
                ngrp = (nl + 3) // 4
                expg = {}
                pvg = {}
                if h > 0:
                    expg[-1] = prev_expg
                    pvg[-1] = prev_pvg
                    pvg[-2] = prev_pvg

                def emit_pv(g):
                    idxs = list(range(4 * g, min(4 * g + 4, nl)))
                    gp = (g % 2) * 2048
                    for i, idx in enumerate(idxs):
                        slot = idx % 8
                        wts = [vh_done[idx]]
                        if i == 0:
                            wts.append(expg[g])
                            if idx == 0 and o_evac is not None:
                                wts.append(o_evac)
                        P.emit("tensor", lambda e_, slot=slot, i=i, idx=idx, gp=gp,
                               nl=nl: e_.matmul(
                            P2[0:65, 0:512], vhb[:, slot * 65:(slot + 1) * 65],
                            arena[:, A_PS + gp + i * 512:A_PS + gp + (i + 1) * 512],
                            start=(idx == 0), stop=(idx == nl - 1),
                            skip_group_check=True),
                            waits=wts, inc=S["PE"] if i == len(idxs) - 1 else None)
                    pvg[g] = S["PE"].now()

                for g in range(ngrp):
                    idxs = list(range(4 * g, min(4 * g + 4, nl)))
                    cnt = len(idxs)
                    gp = (g % 2) * 2048
                    for idx in idxs:
                        kc = live[idx]
                        slot = idx % 8
                        acol = alpha_sb[:, kc * H + h: kc * H + h + 1]
                        vh_w = [kv_fill[h]]
                        if g >= 2:
                            vh_w.append(pvg[g - 2])
                        elif h > 0:
                            vh_w.append(prev_pvg)
                        P.emit("vector", lambda e_, slot=slot, ac=acol, kc=kc,
                               par=par, kc0=kc0: e_.tensor_scalar(
                            out=vhb[:, slot * 65:slot * 65 + 64],
                            in0=arena[:, A_VH + par * 4096 + (kc - kc0) * 64:
                                      A_VH + par * 4096 + (kc - kc0) * 64 + 64],
                            scalar1=ac, scalar2=None, op0=ALU.mult),
                            waits=vh_w, inc=S["DV"])
                        P.emit("vector", lambda e_, slot=slot, ac=acol: e_.tensor_copy(
                            out=vhb[:, slot * 65 + 64:slot * 65 + 65], in_=ac),
                            inc=S["DV"])
                        vh_done[idx] = S["DV"].now()
                    for i, idx in enumerate(idxs):
                        kc = live[idx]
                        PS_T = P0 if i < 2 else P1
                        off = (i % 2) * 512
                        wts = []
                        if i == 0:
                            wts = [kv_fill[h], q_ready]
                            if (g - 1) in expg:
                                wts.append(expg[g - 1])
                        P.emit("tensor", lambda e_, PS_T=PS_T, off=off, base=base,
                               ch=ch, kc=kc, kc0=kc0: e_.matmul(
                            PS_T[:, off:off + 512],
                            arena[base:base + 64, A_KH + (kc - kc0) * 128:
                                  A_KH + (kc - kc0) * 128 + 128],
                            arena[base:base + 64, A_Q + ch * 512:A_Q + (ch + 1) * 512],
                            start=True, stop=True, skip_group_check=True),
                            waits=wts, inc=S["PE"] if i == cnt - 1 else None)
                    qkg = S["PE"].now()
                    e_w = [qkg]
                    if (g - 2) in pvg:
                        e_w.append(pvg[g - 2])
                    n0 = min(cnt, 2) * 512
                    P.emit("scalar", lambda e_, n0=n0, gp=gp: e_.activation(
                        out=arena[:, A_PS + gp:A_PS + gp + n0], in_=P0[:, 0:n0],
                        func=AF.Exp),
                        waits=e_w, inc=S["SC"] if cnt <= 2 else None)
                    if cnt > 2:
                        n1 = (cnt - 2) * 512
                        P.emit("scalar", lambda e_, n1=n1, gp=gp: e_.activation(
                            out=arena[:, A_PS + gp + 1024:A_PS + gp + 1024 + n1],
                            in_=P1[:, 0:n1], func=AF.Exp),
                            inc=S["SC"])
                    expg[g] = S["SC"].now()
                    if g >= 1:
                        emit_pv(g - 1)
                emit_pv(ngrp - 1)
                kv_cons[par] = [qkg, vh_done[nl - 1]]
                if h + 2 < H:
                    dma_kv(h + 2)
                pv_done = S["PE"].now()
                prev_pvg = pvg[ngrp - 1]
                prev_expg = expg[ngrp - 1]
                P.emit("vector", lambda e_: e_.reciprocal(
                    out=af32[0:1, F_U:F_U + 512], in_=P2[64:65, 0:512]),
                    waits=[pv_done], inc=S["DV"])
                P.emit("tensor", lambda e_: e_.matmul(
                    P3[0:64, 0:512], ones_r[:, 0:64], af32[0:1, F_U:F_U + 512],
                    start=True, stop=True),
                    waits=[S["DV"].now(), p3_free], inc=S["PE"])
                P.emit("vector", lambda e_: e_.tensor_copy(
                    out=af32[0:64, F_T:F_T + 512], in_=P3[0:64, 0:512]),
                    waits=[S["PE"].now()], inc=S["DV"])
                p3_free = S["DV"].now()
                P.emit("vector", lambda e_, base=base, ch=ch: e_.tensor_tensor(
                    out=arena[base:base + 64, A_OC + ch * 512:A_OC + (ch + 1) * 512],
                    in0=P2[0:64, 0:512], in1=af32[0:64, F_T:F_T + 512], op=ALU.mult),
                    inc=S["DV"])
                o_evac = S["DV"].now()
            x2add = [None, None]
            for dout in range(DC):
                asem = S["AWA" if dout % 2 == 0 else "AWB"]
                buf = wgbuf if dout % 2 == 0 else w1buf
                dma("gpsimd", asem,
                    buf[:, 0:1024].rearrange("p (dc j) -> p dc j", dc=DC),
                    wo_t[:, :, dout * 128:(dout + 1) * 128],
                    waits=[S["PE"].now()])
                ww = asem.now()
                half = dout % 2
                for fc in range(DC):
                    wts = [ww, o_evac] if fc == 0 else []
                    if fc == 0 and x2add[half] is not None:
                        wts.append(x2add[half])
                    P.emit("tensor", lambda e_, fc=fc, buf=buf, half=half: e_.matmul(
                        P1[:, half * 512:(half + 1) * 512],
                        buf[:, fc * 128:(fc + 1) * 128],
                        arena[:, A_OC + fc * 512:A_OC + (fc + 1) * 512],
                        start=(fc == 0), stop=(fc == DC - 1),
                        skip_group_check=True),
                        waits=wts,
                        inc=S["PE"] if fc == DC - 1 else None)
                wo_done = S["PE"].now()
                P.emit("vector", lambda e_, dout=dout, half=half: e_.tensor_tensor(
                    out=x2s(dout, c0, 512), in0=x2s(dout, c0, 512),
                    in1=P1[:, half * 512:(half + 1) * 512], op=ALU.add),
                    waits=[wo_done], inc=S["DV"])
                x2add[half] = S["DV"].now()

        stage = globals().get("_STAGE", "full")
        if stage == "xrdw":
            emit_ffn(0, nexp=1, sub="xrd")
            for nm in ("WGA", "WGB", "W1A", "W1B", "WRA"):
                P.wait("gpsimd", [S[nm].now()])
            P.flush()
        if stage == "s0":
            pass
        elif stage == "xrdw":
            pass
        elif stage in ("pa", "wbs") or stage.startswith("x"):
            emit_ffn(0, nexp=1, sub=stage)
            P.flush()
        elif stage.startswith("e"):
            emit_ffn(0, nexp=int(stage[1:]))
            P.flush()
        else:
            emit_ffn(0)
            P.flush()
            if stage != "t0":
                emit_ffn(1024)
                P.flush()
        if stage in ("f1a", "full"):
            cc_ready = emit_kv()
            P.flush()
            for qc in range(4):
                emit_qc(qc, cc_ready)
                P.flush()
        if stage == "full":
            emit_ffn(0)
            P.flush()
            emit_ffn(1024)
            P.flush()
        for dc in range(DC):
            dma("sync", S["ST"], out_t[dc], x2[:, dc * T:(dc + 1) * T])
        P.wait("sync", [S["ST"].now()])
        P.flush()

    return nc


_NC = None


def _get_nc():
    global _NC
    if _NC is None:
        _NC = build_nc()
    return _NC


def kernel(L0_attn_x, L0_attn_Wq, L0_attn_Wk, L0_attn_Wv, L0_attn_Wo,
           L0_ffn_Wgate, L0_ffn_bgate, L0_ffn_Wg, L0_ffn_bg,
           L0_ffn_W1, L0_ffn_b1, L0_ffn_W2, L0_ffn_b2):
    x = np.asarray(L0_attn_x, dtype=np.float32)
    f32c = lambda a: np.ascontiguousarray(np.asarray(a, dtype=np.float32))
    shared = {
        "Wq": f32c(L0_attn_Wq), "Wk": f32c(L0_attn_Wk), "Wv": f32c(L0_attn_Wv),
        "Wo": f32c(L0_attn_Wo), "Wg": f32c(L0_ffn_Wg), "W1": f32c(L0_ffn_W1),
        "W2": f32c(L0_ffn_W2),
        "Wgate_img": np.ascontiguousarray(
            f32c(L0_ffn_Wgate).reshape(DC, 128, E).transpose(1, 0, 2).reshape(128, DC * E)),
        "bgate_r": f32c(L0_ffn_bgate).reshape(1, E),
        "bg_img": np.ascontiguousarray(
            f32c(L0_ffn_bg).reshape(E, MC, 128).transpose(2, 0, 1).reshape(128, E * MC)),
        "b1_img": np.ascontiguousarray(
            f32c(L0_ffn_b1).reshape(E, MC, 128).transpose(2, 0, 1).reshape(128, E * MC)),
        "b2_img": np.ascontiguousarray(
            f32c(L0_ffn_b2).reshape(E, DC, 128).transpose(2, 0, 1).reshape(128, E * DC)),
    }
    in_maps = []
    for c in range(NCORE):
        sl = dict(shared)
        sl.update(_host_consts(c))
        xc = np.concatenate([x[0, TPB * c:TPB * (c + 1)],
                             x[1, TPB * c:TPB * (c + 1)]], axis=0)
        sl["xT"] = np.ascontiguousarray(xc.T)
        in_maps.append(sl)
    nc = _get_nc()
    res = run_bass_kernel_spmd(nc, in_maps, core_ids=list(range(NCORE)))
    out = np.empty((B, L, D), dtype=np.float32)
    for c, r in enumerate(res.results):
        oc = np.asarray(r["out"])
        out[0, TPB * c:TPB * (c + 1)] = oc[:, 0:TPB].T
        out[1, TPB * c:TPB * (c + 1)] = oc[:, TPB:T].T
    return out


# version 10 (final)
# speedup vs baseline: 1.1340x; 1.0179x over previous
import contextlib
import numpy as np
import ml_dtypes

import concourse.bass as bass
import concourse.mybir as mybir
from concourse.bass_utils import run_bass_kernel_spmd

F32 = mybir.dt.float32
BF16 = mybir.dt.bfloat16
AF = mybir.ActivationFunctionType
ALU = mybir.AluOpType
AX = mybir.AxisListType
bfl = ml_dtypes.bfloat16

B, L, D = 2, 8192, 1024
H, KV, HD = 16, 4, 64
E, M = 8, 4096
NCORE = 8
TPB = L // NCORE
T = 2 * TPB
DC = D // 128
MC = M // 128
EPS = 1e-5

ENGINES = ("sync", "tensor", "scalar", "vector", "gpsimd")


def _alpha_full():
    pos = np.arange(L, dtype=np.float64)
    s = np.asarray([2.0 ** (-0.5 * (i + 1)) for i in range(H)], dtype=np.float64)
    return np.exp(s[None, :] * (pos[:, None] - (L - 1))).astype(np.float32)


_ALPHA = _alpha_full()
LIVE = [
    [kc for kc in range(L // 128) if np.any(_ALPHA[kc * 128:(kc + 1) * 128, h] != 0.0)]
    for h in range(H)
]


def _host_consts(core):
    pos = np.arange(TPB * core, TPB * (core + 1), dtype=np.float64)
    d = np.arange(128)
    f = 1.0 / (10000.0 ** (2.0 * ((d % 64) % 32) / 64.0))
    ang = pos[None, :] * f[:, None]
    cos, sin = np.cos(ang), np.sin(ang)
    a = _ALPHA.reshape(64, 128, H).transpose(1, 0, 2).reshape(128, 64 * H)
    R = np.zeros((128, 128), dtype=np.float32)
    for o in (0, 64):
        for m in range(32):
            R[o + m + 32, o + m] = -1.0
            R[o + m, o + m + 32] = 1.0
    return {
        "cosq": (cos / 8.0).astype(bfl),
        "sinq": (sin / 8.0).astype(bfl),
        "cosk": cos.astype(bfl),
        "sink": sin.astype(bfl),
        "alpha_img": np.ascontiguousarray(a),
        "rot_b": R.astype(bfl),
        "ident_b": np.eye(128, dtype=np.float32).astype(bfl),
        "ident_f": np.eye(128, dtype=np.float32),
    }


class Sem:
    def __init__(self, nc, stack, name):
        self.h = stack.enter_context(nc.semaphore(name))
        self.n = 0

    def now(self):
        return (self.h, self.n)


class Prog:

    def __init__(self, nc):
        self.nc = nc
        self.steps = []

    def emit(self, eng, fn, waits=(), inc=None, k=1):
        ws = tuple((h, v) for (h, v) in waits if v > 0)
        ih = None
        if inc is not None:
            inc.n += k
            ih = inc.h
        self.steps.append((eng, ws, fn, ih, k))

    def wait(self, eng, waits):
        self.emit(eng, None, waits=waits)

    def flush(self):
        steps = self.steps
        self.steps = []
        with self.nc.Block() as block:
            for name in ENGINES:
                def runner(e, name=name):
                    for (eng, ws, fn, ih, k) in steps:
                        if eng != name:
                            continue
                        for (h, v) in ws:
                            e.wait_ge(h, v)
                        ins = fn(e) if fn is not None else None
                        if ih is not None:
                            assert ins is not None
                            ins.then_inc(ih, k)
                getattr(block, name)(runner)


def build_nc():
    nc = bass.Bass("TRN2", target_bir_lowering=False)

    xT = nc.dram_tensor("xT", [D, T], F32, kind="ExternalInput").ap()
    Wq = nc.dram_tensor("Wq", [D, D], F32, kind="ExternalInput").ap()
    Wk = nc.dram_tensor("Wk", [D, KV * HD], F32, kind="ExternalInput").ap()
    Wv = nc.dram_tensor("Wv", [D, KV * HD], F32, kind="ExternalInput").ap()
    Wo = nc.dram_tensor("Wo", [D, D], F32, kind="ExternalInput").ap()
    Wg = nc.dram_tensor("Wg", [E, D, M], F32, kind="ExternalInput").ap()
    W1 = nc.dram_tensor("W1", [E, D, M], F32, kind="ExternalInput").ap()
    W2 = nc.dram_tensor("W2", [E, M, D], F32, kind="ExternalInput").ap()
    Wgate_img = nc.dram_tensor("Wgate_img", [128, DC * E], F32, kind="ExternalInput").ap()
    bgate_r = nc.dram_tensor("bgate_r", [1, E], F32, kind="ExternalInput").ap()
    bg_img = nc.dram_tensor("bg_img", [128, E * MC], F32, kind="ExternalInput").ap()
    b1_img = nc.dram_tensor("b1_img", [128, E * MC], F32, kind="ExternalInput").ap()
    b2_img = nc.dram_tensor("b2_img", [128, E * DC], F32, kind="ExternalInput").ap()
    cosq = nc.dram_tensor("cosq", [128, TPB], BF16, kind="ExternalInput").ap()
    sinq = nc.dram_tensor("sinq", [128, TPB], BF16, kind="ExternalInput").ap()
    cosk = nc.dram_tensor("cosk", [128, TPB], BF16, kind="ExternalInput").ap()
    sink = nc.dram_tensor("sink", [128, TPB], BF16, kind="ExternalInput").ap()
    alpha_img = nc.dram_tensor("alpha_img", [128, 64 * H], F32, kind="ExternalInput").ap()
    rot_b = nc.dram_tensor("rot_b", [128, 128], BF16, kind="ExternalInput").ap()
    ident_b = nc.dram_tensor("ident_b", [128, 128], BF16, kind="ExternalInput").ap()
    ident_f = nc.dram_tensor("ident_f", [128, 128], F32, kind="ExternalInput").ap()
    out_d = nc.dram_tensor("out", [D, T], F32, kind="ExternalOutput").ap()

    ccki = nc.dram_tensor("ccki", [KV * HD, T], BF16)
    ccko = nc.dram_tensor("ccko", [NCORE * KV * HD, T], BF16)
    ccvi = nc.dram_tensor("ccvi", [T, KV * HD], BF16)
    ccvo = nc.dram_tensor("ccvo", [NCORE * T, KV * HD], BF16)

    wg_t = Wg.rearrange("e (dc p) m -> e p dc m", p=128)
    w1_t = W1.rearrange("e (dc p) m -> e p dc m", p=128)
    w2_t = W2.rearrange("e (mc p) d -> e p mc d", p=128)
    wq_t = Wq.rearrange("(dc p) m -> p dc m", p=128)
    wk_t = Wk.rearrange("(dc p) m -> p dc m", p=128)
    wv_t = Wv.rearrange("(dc p) m -> p dc m", p=128)
    wo_t = Wo.rearrange("(dc p) m -> p dc m", p=128)
    out_t = out_d.rearrange("(dc p) t -> dc p t", p=128)

    st = contextlib.ExitStack()
    with st:
        E_ = st.enter_context
        x2 = E_(nc.sbuf_tensor("x2", [128, DC * T], F32))
        arena = E_(nc.sbuf_tensor("arena", [128, 43008], BF16))
        af32 = E_(nc.sbuf_tensor("af32", [128, 3072], F32))
        alpha_sb = E_(nc.sbuf_tensor("alpha_sb", [128, 64 * H], F32))
        wgbuf = E_(nc.sbuf_tensor("wgbuf", [128, 2 * 1024], BF16))
        w1buf = E_(nc.sbuf_tensor("w1buf", [128, 2 * 1024], BF16))
        w2buf = E_(nc.sbuf_tensor("w2buf", [128, 3 * 1024], BF16))
        ktb = E_(nc.sbuf_tensor("ktb", [128, 4 * 128], BF16))
        vtb = E_(nc.sbuf_tensor("vtb", [128, 4 * 64], BF16))
        vhb = E_(nc.sbuf_tensor("vhb", [128, 8 * 65], BF16))
        bg_sb = E_(nc.sbuf_tensor("bg_sb", [128, E * MC], F32))
        b1_sb = E_(nc.sbuf_tensor("b1_sb", [128, E * MC], F32))
        b2_sb = E_(nc.sbuf_tensor("b2_sb", [128, E * DC], F32))
        wgate_f = E_(nc.sbuf_tensor("wgate_f", [128, DC * E], F32))
        wgate_hi = E_(nc.sbuf_tensor("wgate_hi", [128, DC * E], BF16))
        wgate_lo = E_(nc.sbuf_tensor("wgate_lo", [128, DC * E], BF16))
        wgate_d = E_(nc.sbuf_tensor("wgate_d", [128, DC * E], F32))
        bgate_f = E_(nc.sbuf_tensor("bgate_f", [1, E], F32))
        bgate_hi = E_(nc.sbuf_tensor("bgate_hi", [1, E], BF16))
        bgate_lo = E_(nc.sbuf_tensor("bgate_lo", [1, E], BF16))
        bgate_d = E_(nc.sbuf_tensor("bgate_d", [1, E], F32))
        rot_sb = E_(nc.sbuf_tensor("rot_sb", [128, 128], BF16))
        idb_sb = E_(nc.sbuf_tensor("idb_sb", [128, 128], BF16))
        idf_sb = E_(nc.sbuf_tensor("idf_sb", [128, 128], F32))
        ones_c = E_(nc.sbuf_tensor("ones_c", [128, 1], F32))
        ones_r = E_(nc.sbuf_tensor("ones_r", [1, 128], F32))
        ones_rb = E_(nc.sbuf_tensor("ones_rb", [1, 128], BF16))
        ceps = E_(nc.sbuf_tensor("ceps", [1, 1], F32))
        cm05 = E_(nc.sbuf_tensor("cm05", [1, 1], F32))
        m_row = E_(nc.sbuf_tensor("m_row", [1, 1024], F32))
        v_row = E_(nc.sbuf_tensor("v_row", [1, 1024], F32))
        w_row = E_(nc.sbuf_tensor("w_row", [1, 2048], F32))
        zsb = E_(nc.sbuf_tensor("zsb", [128, 64], F32))
        t8 = E_(nc.sbuf_tensor("t8", [128, 64], F32))
        zm8 = E_(nc.sbuf_tensor("zm8", [128, 64], F32))
        msk1 = E_(nc.sbuf_tensor("msk1", [128, 64], F32))
        msk2 = E_(nc.sbuf_tensor("msk2", [128, 64], F32))
        m18 = E_(nc.sbuf_tensor("m18", [128, 8], F32))
        m28 = E_(nc.sbuf_tensor("m28", [128, 8], F32))
        d8 = E_(nc.sbuf_tensor("d8", [128, 8], F32))
        w1h = E_(nc.sbuf_tensor("w1h", [128, 8], F32))
        w2h = E_(nc.sbuf_tensor("w2h", [128, 8], F32))
        wvec = E_(nc.sbuf_tensor("wvec", [128, 64], F32))
        wT_sb = E_(nc.sbuf_tensor("wT_sb", [64, 128], F32))

        P0 = E_(nc.psum_tensor("P0", [128, 1024], F32))
        P1 = E_(nc.psum_tensor("P1", [128, 1024], F32))
        P2 = E_(nc.psum_tensor("P2", [128, 1024], F32))
        P3 = E_(nc.psum_tensor("P3", [128, 512], F32))
        PT = E_(nc.psum_tensor("PT", [128, 128], BF16))

        names = ("IN", "PE", "SC", "DV", "PL", "VB", "SB", "CC", "ST", "KVC",
                 "WGA", "WGB", "W1A", "W1B", "W2A", "W2B", "W2C",
                 "KVS0", "KVS1", "KVS2", "KVS3", "WRA", "WRB", "AWA", "AWB")
        S = {n: Sem(nc, st, "s" + n) for n in names}

        P = Prog(nc)

        A_H, A_XT, A_LO, A_T1 = 0, 32768, 40960, 41984
        A_Q, A_OC, A_PS, A_KC, A_VT, A_VF, A_KB = 0, 4096, 8192, 8192, 12288, 16384, 20480
        A_COS = 21504
        A_KH = 25600
        A_VH = 12288
        A_XTA = 38912
        F_T, F_U, F_XSQ = 0, 1024, 2048

        def x2s(dc, c0, w):
            return x2[:, dc * T + c0: dc * T + c0 + w]

        def dma(eng, sem, outap, inap, waits=()):
            P.emit(eng, lambda e, o=outap, i=inap: e.dma_start(out=o, in_=i),
                   waits=waits, inc=sem, k=16)

        dma("sync", S["IN"], x2[:].rearrange("p (dc t) -> p dc t", dc=DC),
            xT.rearrange("(dc p) t -> p dc t", p=128))
        dma("sync", S["IN"], alpha_sb[:], alpha_img)
        dma("sync", S["IN"], rot_sb[:], rot_b)
        dma("sync", S["IN"], idb_sb[:], ident_b)
        dma("sync", S["IN"], idf_sb[:], ident_f)
        dma("sync", S["IN"], bg_sb[:], bg_img)
        dma("sync", S["IN"], b1_sb[:], b1_img)
        dma("sync", S["IN"], b2_sb[:], b2_img)
        dma("sync", S["IN"], wgate_f[:], Wgate_img)
        dma("sync", S["IN"], bgate_f[:], bgate_r)
        all_in = S["IN"].now()

        def setup_v(v):
            v.memset(ones_c[:], 1.0)
            v.memset(ones_r[:], 1.0)
            v.memset(ones_rb[:], 1.0)
            v.memset(ceps[:], EPS)
            return v.memset(cm05[:], -0.5)
        P.emit("vector", setup_v, waits=[all_in], inc=S["VB"])

        P.emit("gpsimd", lambda e: e.tensor_copy(out=wgate_hi[:], in_=wgate_f[:]),
               waits=[all_in], inc=S["PL"])
        P.emit("gpsimd", lambda e: e.tensor_copy(out=bgate_hi[:], in_=bgate_f[:]),
               inc=S["PL"])
        P.emit("vector", lambda e: e.tensor_tensor(
            out=wgate_d[:], in0=wgate_f[:], in1=wgate_hi[:], op=ALU.subtract),
            waits=[S["PL"].now(), S["VB"].now()], inc=S["VB"])
        P.emit("vector", lambda e: e.tensor_tensor(
            out=bgate_d[:], in0=bgate_f[:], in1=bgate_hi[:], op=ALU.subtract),
            waits=[S["VB"].now()], inc=S["VB"])
        P.emit("gpsimd", lambda e: e.tensor_copy(out=wgate_lo[:], in_=wgate_d[:]),
               waits=[S["VB"].now()], inc=S["PL"])
        P.emit("gpsimd", lambda e: e.tensor_copy(out=bgate_lo[:], in_=bgate_d[:]),
               inc=S["PL"])
        P.flush()

        def emit_ln(c0, width, gate=False, xt_base=None):
            nh = width // 512
            if xt_base is None:
                xt_base = A_XT
            for n in range(nh):
                for dc in range(DC):
                    xs = x2s(dc, c0 + n * 512, 512)
                    sq = af32[:, F_XSQ + (dc % 2) * 512: F_XSQ + (dc % 2) * 512 + 512]
                    P.emit("scalar", lambda e, o=sq, i=xs: e.activation(
                        out=o, in_=i, func=AF.Square),
                        waits=[S["PE"].now()], inc=S["SC"])
                    P.emit("tensor", lambda e, i=xs, n=n, dc=dc: e.matmul(
                        P0[0:1, n * 512:(n + 1) * 512], ones_c[:], i,
                        start=(dc == 0), stop=(dc == DC - 1), skip_group_check=True))
                    P.emit("tensor", lambda e, q=sq, n=n, dc=dc: e.matmul(
                        P1[0:1, n * 512:(n + 1) * 512], ones_c[:], q,
                        start=(dc == 0), stop=(dc == DC - 1), skip_group_check=True),
                        waits=[S["SC"].now()], inc=S["PE"])
            stats = S["PE"].now()
            mr = m_row[:, 0:width]
            vr = v_row[:, 0:width]
            qr = af32[0:1, F_XSQ:F_XSQ + width]
            P.emit("vector", lambda e: e.tensor_scalar(
                out=mr, in0=P0[0:1, 0:width], scalar1=1.0 / D, scalar2=None,
                op0=ALU.mult), waits=[stats, S["VB"].now()], inc=S["VB"])
            P.emit("vector", lambda e: e.tensor_scalar(
                out=vr, in0=P1[0:1, 0:width], scalar1=1.0 / D, scalar2=None,
                op0=ALU.mult), waits=[S["VB"].now()], inc=S["VB"])
            P.emit("vector", lambda e: e.scalar_tensor_tensor(
                out=qr, in0=mr, scalar=-1.0, in1=mr, op0=ALU.mult, op1=ALU.mult),
                waits=[S["VB"].now()], inc=S["VB"])
            P.emit("vector", lambda e: e.tensor_tensor(
                out=vr, in0=vr, in1=qr, op=ALU.add),
                waits=[S["VB"].now()], inc=S["VB"])
            row_dv = S["VB"].now()
            P.emit("scalar", lambda e: e.activation(
                out=vr, in_=vr, func=AF.Ln, bias=ceps[:]),
                waits=[row_dv, S["SB"].now()], inc=S["SB"])
            P.emit("scalar", lambda e: e.activation(
                out=vr, in_=vr, func=AF.Exp, scale=cm05[:]),
                waits=[S["SB"].now()], inc=S["SB"])
            P.emit("vector", lambda e: e.tensor_tensor(
                out=mr, in0=mr, in1=vr, op=ALU.mult),
                waits=[S["SB"].now(), S["VB"].now()], inc=S["VB"])
            rows = S["VB"].now()
            for n in range(nh):
                P.emit("tensor", lambda e, n=n: e.matmul(
                    P0[:, n * 512:(n + 1) * 512], ones_r[:],
                    v_row[:, n * 512:(n + 1) * 512], start=True, stop=True),
                    waits=[rows] if n == 0 else ())
                P.emit("tensor", lambda e, n=n: e.matmul(
                    P1[:, n * 512:(n + 1) * 512], ones_r[:],
                    m_row[:, n * 512:(n + 1) * 512], start=True, stop=True),
                    inc=S["PE"])
            bcast = S["PE"].now()
            for dc in range(DC):
                tbuf = af32[:, F_T:F_T + width]
                xtsl = arena[:, xt_base + dc * width: xt_base + (dc + 1) * width]
                losl = arena[:, A_LO: A_LO + width]
                P.emit("vector", lambda e, dc=dc, tb=tbuf: e.tensor_tensor(
                    out=tb, in0=x2s(dc, c0, width), in1=P0[:, 0:width], op=ALU.mult),
                    waits=[bcast] + ([S["PE"].now()] if gate else []), inc=S["DV"])
                P.emit("vector", lambda e, tb=tbuf: e.tensor_tensor(
                    out=tb, in0=tb, in1=P1[:, 0:width], op=ALU.subtract),
                    inc=S["DV"])
                P.emit("vector", lambda e, o=xtsl, tb=tbuf: e.tensor_copy(
                    out=o, in_=tb), inc=S["DV"])
                if gate:
                    P.emit("vector", lambda e, o=losl, tb=tbuf, x=xtsl: e.tensor_tensor(
                        out=o, in0=tb, in1=x, op=ALU.subtract), inc=S["DV"])
                    g_dv = S["DV"].now()
                    ntch = width // 128
                    for tch in range(ntch):
                        xsl = arena[:, xt_base + dc * width + tch * 128:
                                    xt_base + dc * width + tch * 128 + 128]
                        lsl = arena[:, A_LO + tch * 128: A_LO + tch * 128 + 128]
                        zr = P3[:, tch * 8:(tch + 1) * 8]
                        for pi, (lhs, rhs) in enumerate((
                                (xsl, wgate_hi), (xsl, wgate_lo), (lsl, wgate_hi))):
                            P.emit("tensor", lambda e, zr=zr, l=lhs, r=rhs, dc=dc,
                                   pi=pi, tch=tch: e.matmul(
                                zr, l, r[:, dc * 8:(dc + 1) * 8],
                                start=(dc == 0 and pi == 0 and tch == 0), stop=False,
                                skip_group_check=True),
                                waits=[g_dv] if (pi == 0) else (), inc=S["PE"])
            xt_ready = S["DV"].now()
            wT_ready = None
            if gate:
                ntch = width // 128
                for tch in range(ntch):
                    zr = P3[:, tch * 8:(tch + 1) * 8]
                    P.emit("tensor", lambda e, zr=zr: e.matmul(
                        zr, ones_rb[:], bgate_hi[:], start=False, stop=False,
                        skip_group_check=True))
                    P.emit("tensor", lambda e, zr=zr, tch=tch: e.matmul(
                        zr, ones_rb[:], bgate_lo[:], start=False, stop=True,
                        skip_group_check=True), inc=S["PE"])
                z_done = S["PE"].now()
                nt = ntch * 8

                def vb(fn, extra=()):
                    P.emit("vector", fn, waits=[S["VB"].now()] + list(extra),
                           inc=S["VB"])

                z3 = zsb[:, 0:nt].rearrange("p (t e) -> p t e", e=8)
                zm3 = zm8[:, 0:nt].rearrange("p (t e) -> p t e", e=8)
                mk13 = msk1[:, 0:nt].rearrange("p (t e) -> p t e", e=8)
                mk23 = msk2[:, 0:nt].rearrange("p (t e) -> p t e", e=8)
                m1b = m18[:, 0:ntch].rearrange("p (t e) -> p t e", e=1) \
                    .broadcast_to([128, ntch, 8])
                m2b = m28[:, 0:ntch].rearrange("p (t e) -> p t e", e=1) \
                    .broadcast_to([128, ntch, 8])
                w1b = w1h[:, 0:ntch].rearrange("p (t e) -> p t e", e=1) \
                    .broadcast_to([128, ntch, 8])
                w2b = w2h[:, 0:ntch].rearrange("p (t e) -> p t e", e=1) \
                    .broadcast_to([128, ntch, 8])
                vb(lambda e: e.tensor_copy(out=zsb[:, 0:nt], in_=P3[:, 0:nt]),
                   extra=[z_done])
                vb(lambda e: e.tensor_reduce(out=m18[:, 0:ntch], in_=z3, axis=AX.X,
                                             op=ALU.max))
                vb(lambda e: e.tensor_tensor(out=mk13, in0=z3, in1=m1b,
                                             op=ALU.is_equal))
                vb(lambda e: e.tensor_scalar(out=t8[:, 0:nt], in0=msk1[:, 0:nt],
                                             scalar1=-1e30, scalar2=None, op0=ALU.mult))
                vb(lambda e: e.tensor_tensor(out=zm8[:, 0:nt], in0=zsb[:, 0:nt],
                                             in1=t8[:, 0:nt], op=ALU.add))
                vb(lambda e: e.tensor_reduce(out=m28[:, 0:ntch], in_=zm3, axis=AX.X,
                                             op=ALU.max))
                vb(lambda e: e.tensor_tensor(out=mk23, in0=zm3, in1=m2b,
                                             op=ALU.is_equal))
                vb(lambda e: e.tensor_tensor(out=d8[:, 0:ntch], in0=m18[:, 0:ntch],
                                             in1=m28[:, 0:ntch], op=ALU.subtract))
                P.emit("scalar", lambda e: e.activation(
                    out=w1h[:, 0:ntch], in_=d8[:, 0:ntch], func=AF.Sigmoid),
                    waits=[S["VB"].now(), S["SB"].now()], inc=S["SB"])
                vb(lambda e: e.tensor_scalar(out=w1h[:, 0:ntch], in0=w1h[:, 0:ntch],
                                             scalar1=0.5, scalar2=None, op0=ALU.mult),
                   extra=[S["SB"].now()])
                vb(lambda e: e.tensor_scalar(out=w2h[:, 0:ntch], in0=w1h[:, 0:ntch],
                                             scalar1=-1.0, scalar2=0.5,
                                             op0=ALU.mult, op1=ALU.add))
                vb(lambda e: e.tensor_tensor(out=mk13, in0=mk13, in1=w1b, op=ALU.mult))
                vb(lambda e: e.tensor_tensor(out=mk23, in0=mk23, in1=w2b, op=ALU.mult))
                vb(lambda e: e.tensor_tensor(out=wvec[:, 0:nt], in0=msk1[:, 0:nt],
                                             in1=msk2[:, 0:nt], op=ALU.add))
                wv_done = S["VB"].now()
                P.emit("tensor", lambda e: e.transpose(
                    P3[0:nt, 0:128], wvec[:, 0:nt], idf_sb[:]),
                    waits=[wv_done], inc=S["PE"])
                P.emit("vector", lambda e: e.tensor_copy(
                    out=wT_sb[0:nt, :], in_=P3[0:nt, 0:128]),
                    waits=[S["PE"].now(), S["VB"].now()], inc=S["VB"])
                wT_ready = S["VB"].now()
            return xt_ready, wT_ready

        wr_state = [None, None]

        def emit_ffn(c0, nexp=E, sub='all'):
            xt_ready, wT_ready = emit_ln(c0, 1024, gate=True)
            wg_cons = [None, None]
            w1_cons = [None, None]
            w2_cons = [None, None, None]
            wg_fill = [None, None]
            w1_fill = [None, None]
            w2_fill = [None, None, None]
            wr_fill = [None, None]
            p3_free = S["VB"].now()
            silu_done = None
            h_done = None
            u_done = [None, None]
            sub_r = sub in ('all', 'pa', 'wbs') or 'r' in sub
            sub_d = sub in ('all', 'pa', 'wbs') or 'd' in sub
            sub_m = sub in ('all', 'pa', 'wbs') or 'm' in sub
            sub_s = sub in ('all', 'pa', 'wbs') or 's' in sub
            sub_h = sub in ('all', 'pa', 'wbs') or 'h' in sub
            for e in range(nexp):
                ws = e % 2
                sem = S["WRA" if ws == 0 else "WRB"]
                if sub_r:
                  for n in range(2):
                    for tq in range(4):
                        src = wT_sb[(4 * n + tq) * 8 + e:(4 * n + tq) * 8 + e + 1, :]
                        dst = w_row[0:1, ws * 1024 + n * 512 + tq * 128:
                                    ws * 1024 + n * 512 + tq * 128 + 128]
                        dma("sync", sem, dst, src,
                            waits=[wT_ready] + ([wr_state[ws]] if wr_state[ws] else []))
                wr_fill[ws] = sem.now()
                for mc in range(MC):
                    slot = mc % 2
                    gsem = S["WGA" if slot == 0 else "WGB"]
                    osem = S["W1A" if slot == 0 else "W1B"]
                    if not sub_d:
                        continue
                    dma("gpsimd", gsem,
                        wgbuf[:, slot * 1024:(slot + 1) * 1024]
                        .rearrange("p (dc j) -> p dc j", dc=DC),
                        wg_t[e, :, :, mc * 128:(mc + 1) * 128],
                        waits=[wg_cons[slot]] if wg_cons[slot] else ())
                    wg_fill[slot] = gsem.now()
                    dma("gpsimd", osem,
                        w1buf[:, slot * 1024:(slot + 1) * 1024]
                        .rearrange("p (dc j) -> p dc j", dc=DC),
                        w1_t[e, :, :, mc * 128:(mc + 1) * 128],
                        waits=[w1_cons[slot]] if w1_cons[slot] else ())
                    w1_fill[slot] = osem.now()
                    if not sub_m:
                        continue
                    wA = [wg_fill[slot], xt_ready]
                    if silu_done is not None:
                        wA.append(silu_done)
                    for dc in range(DC):
                        for n in range(2):
                            P.emit("tensor", lambda e_, dc=dc, n=n, slot=slot: e_.matmul(
                                P0[:, n * 512:(n + 1) * 512],
                                wgbuf[:, slot * 1024 + dc * 128:slot * 1024 + (dc + 1) * 128],
                                arena[:, A_XT + dc * 1024 + n * 512:
                                      A_XT + dc * 1024 + n * 512 + 512],
                                start=(dc == 0), stop=(dc == DC - 1),
                                skip_group_check=True),
                                waits=wA if (n == 0 and dc == 0) else (),
                                inc=S["PE"] if (n == 1 and dc == DC - 1) else None)
                    a_done = S["PE"].now()
                    wB = [w1_fill[slot]]
                    if h_done is not None:
                        wB.append(h_done)
                    for dc in range(DC):
                        for n in range(2):
                            P.emit("tensor", lambda e_, dc=dc, n=n, slot=slot: e_.matmul(
                                P1[:, n * 512:(n + 1) * 512],
                                w1buf[:, slot * 1024 + dc * 128:slot * 1024 + (dc + 1) * 128],
                                arena[:, A_XT + dc * 1024 + n * 512:
                                      A_XT + dc * 1024 + n * 512 + 512],
                                start=(dc == 0), stop=(dc == DC - 1),
                                skip_group_check=True),
                                waits=wB if (n == 0 and dc == 0) else (),
                                inc=S["PE"] if (n == 1 and dc == DC - 1) else None)
                    b_done = S["PE"].now()
                    wg_cons[slot] = a_done
                    w1_cons[slot] = b_done
                    if not sub_s:
                        silu_done = a_done
                        h_done = b_done
                        continue
                    bcol = bg_sb[:, e * MC + mc: e * MC + mc + 1]
                    P.emit("scalar", lambda e_, bc=bcol: e_.activation(
                        out=arena[:, A_T1:A_T1 + 1024], in_=P0[:, 0:1024],
                        func=AF.Silu, bias=bc),
                        waits=[a_done, S["DV"].now()], inc=S["SC"])
                    silu_done = S["SC"].now()
                    if not sub_h:
                        h_done = b_done
                        continue
                    b1col = b1_sb[:, e * MC + mc: e * MC + mc + 1]
                    P.emit("vector", lambda e_, mc=mc, bc=b1col: e_.scalar_tensor_tensor(
                        out=arena[:, A_H + mc * 1024:A_H + (mc + 1) * 1024],
                        in0=P1[:, 0:1024], scalar=bc,
                        in1=arena[:, A_T1:A_T1 + 1024], op0=ALU.add, op1=ALU.mult),
                        waits=[b_done, silu_done], inc=S["DV"])
                    h_done = S["DV"].now()
                if sub == 'pa' or sub.startswith('x'):
                    continue
                wbs_done = None
                for n in range(2):
                    P.emit("tensor", lambda e_, ws=ws, n=n: e_.matmul(
                        P3[:, 0:512], ones_r[:],
                        w_row[0:1, ws * 1024 + n * 512: ws * 1024 + n * 512 + 512],
                        start=True, stop=True),
                        waits=[wr_fill[ws], p3_free], inc=S["PE"])
                    P.emit("vector", lambda e_, n=n: e_.tensor_copy(
                        out=af32[:, F_T + n * 512:F_T + n * 512 + 512],
                        in_=P3[:, 0:512]),
                        waits=[S["PE"].now()], inc=S["DV"])
                    p3_free = S["DV"].now()
                wbs_done = S["DV"].now()
                wr_state[ws] = S["PE"].now()
                if sub == 'wbs':
                    continue
                w2_fill_by = {}

                def w2dma(qg):
                    s3 = qg % 3
                    qsem = S[("W2A", "W2B", "W2C")[s3]]
                    dma("gpsimd", qsem,
                        w2buf[:, s3 * 1024:(s3 + 1) * 1024]
                        .rearrange("p (mq j) -> p mq j", mq=8),
                        w2_t[e, :, (qg % 4) * 8:(qg % 4) * 8 + 8,
                             (qg // 4) * 128:(qg // 4) * 128 + 128],
                        waits=[w2_cons[s3]] if w2_cons[s3] else ())
                    w2_fill_by[qg] = qsem.now()

                for qg in range(3):
                    w2dma(qg)
                next_qg = 3
                for dout in range(DC):
                    for j in range(MC):
                        qg = dout * 4 + j // 8
                        s3 = qg % 3
                        for n in range(2):
                            wts = []
                            if n == 0 and j % 8 == 0:
                                wts.append(w2_fill_by[qg])
                            if n == 0 and j == 0:
                                wts.append(h_done)
                                if u_done[0] is not None:
                                    wts.append(u_done[1])
                            P.emit("tensor", lambda e_, j=j, s3=s3, n=n: e_.matmul(
                                P2[:, n * 512:(n + 1) * 512],
                                w2buf[:, s3 * 1024 + (j % 8) * 128:
                                      s3 * 1024 + (j % 8) * 128 + 128],
                                arena[:, A_H + j * 1024 + n * 512:
                                      A_H + j * 1024 + n * 512 + 512],
                                start=(j == 0), stop=(j == MC - 1),
                                skip_group_check=True),
                                waits=wts,
                                inc=S["PE"] if (n == 1 and j % 8 == 7) else None)
                        if j % 8 == 7:
                            w2_cons[s3] = S["PE"].now()
                            if next_qg < 32:
                                w2dma(next_qg)
                                next_qg += 1
                    y_done = S["PE"].now()
                    b2col = b2_sb[:, e * DC + dout: e * DC + dout + 1]
                    for n in range(2):
                        P.emit("vector", lambda e_, n=n, bc=b2col: e_.scalar_tensor_tensor(
                            out=af32[:, F_U:F_U + 512],
                            in0=P2[:, n * 512:(n + 1) * 512], scalar=bc,
                            in1=af32[:, F_T + n * 512:F_T + n * 512 + 512],
                            op0=ALU.add, op1=ALU.mult),
                            waits=[y_done, wbs_done], inc=S["DV"])
                        P.emit("vector", lambda e_, dout=dout, n=n: e_.tensor_tensor(
                            out=x2s(dout, c0 + n * 512, 512),
                            in0=x2s(dout, c0 + n * 512, 512),
                            in1=af32[:, F_U:F_U + 512], op=ALU.add),
                            inc=S["DV"])
                    u_done[0] = u_done[1] = S["DV"].now()

        def emit_kv():
            for ap_, off_ in ((cosq, 0), (sinq, TPB), (cosk, 2 * TPB), (sink, 3 * TPB)):
                dma("sync", S["IN"], arena[:, A_COS + off_:A_COS + off_ + TPB], ap_)
            cos_ready = S["IN"].now()
            for tc in range(2):
                xt_ready, _ = emit_ln(tc * 1024, 1024, gate=False)
                for vc in range(2):
                    asem = S["AWA"]
                    dma("gpsimd", asem,
                        wgbuf[:, 0:1024].rearrange("p (dc j) -> p dc j", dc=DC),
                        wk_t[:, :, vc * 128:(vc + 1) * 128],
                        waits=[S["PE"].now()])
                    kw = asem.now()
                    for dc in range(DC):
                        for n in range(2):
                            P.emit("tensor", lambda e_, dc=dc, n=n: e_.matmul(
                                P1[:, n * 512:(n + 1) * 512],
                                wgbuf[:, dc * 128:(dc + 1) * 128],
                                arena[:, A_XT + dc * 1024 + n * 512:
                                      A_XT + dc * 1024 + n * 512 + 512],
                                start=(dc == 0), stop=(dc == DC - 1),
                                skip_group_check=True),
                                waits=[kw, xt_ready, S["DV"].now()]
                                if (n == 0 and dc == 0) else (),
                                inc=S["PE"] if (n == 1 and dc == DC - 1) else None)
                    kproj = S["PE"].now()
                    P.emit("vector", lambda e_: e_.tensor_copy(
                        out=arena[:, A_KB:A_KB + 1024], in_=P1[:, 0:1024]),
                        waits=[kproj], inc=S["DV"])
                    kb_done = S["DV"].now()
                    for n in range(2):
                        P.emit("tensor", lambda e_, n=n: e_.matmul(
                            P2[:, n * 512:(n + 1) * 512], rot_sb[:],
                            arena[:, A_KB + n * 512:A_KB + n * 512 + 512],
                            start=True, stop=True),
                            waits=[kb_done, S["DV"].now()] if n == 0 else (),
                            inc=S["PE"] if n == 1 else None)
                    krot = S["PE"].now()
                    P.emit("vector", lambda e_: e_.tensor_tensor(
                        out=af32[:, F_T:F_T + 1024], in0=P1[:, 0:1024],
                        in1=arena[:, A_COS + 2 * TPB:A_COS + 3 * TPB], op=ALU.mult),
                        waits=[krot, cos_ready], inc=S["DV"])
                    P.emit("vector", lambda e_: e_.tensor_tensor(
                        out=af32[:, F_U:F_U + 1024], in0=P2[:, 0:1024],
                        in1=arena[:, A_COS + 3 * TPB:A_COS + 4 * TPB], op=ALU.mult),
                        inc=S["DV"])
                    P.emit("vector", lambda e_, tc=tc, vc=vc: e_.tensor_tensor(
                        out=arena[:, A_KC + vc * 2048 + tc * 1024:
                                  A_KC + vc * 2048 + tc * 1024 + 1024],
                        in0=af32[:, F_T:F_T + 1024], in1=af32[:, F_U:F_U + 1024],
                        op=ALU.add), inc=S["DV"])
                    kf_done = S["DV"].now()
                    dma("gpsimd", S["AWB"],
                        w1buf[:, 0:1024].rearrange("p (dc j) -> p dc j", dc=DC),
                        wv_t[:, :, vc * 128:(vc + 1) * 128],
                        waits=[S["PE"].now()])
                    vw = S["AWB"].now()
                    for dc in range(DC):
                        for n in range(2):
                            P.emit("tensor", lambda e_, dc=dc, n=n: e_.matmul(
                                P1[:, n * 512:(n + 1) * 512],
                                w1buf[:, dc * 128:(dc + 1) * 128],
                                arena[:, A_XT + dc * 1024 + n * 512:
                                      A_XT + dc * 1024 + n * 512 + 512],
                                start=(dc == 0), stop=(dc == DC - 1),
                                skip_group_check=True),
                                waits=[vw, kf_done] if (n == 0 and dc == 0) else (),
                                inc=S["PE"] if (n == 1 and dc == DC - 1) else None)
                    vproj = S["PE"].now()
                    P.emit("vector", lambda e_: e_.tensor_copy(
                        out=arena[:, A_VF:A_VF + 1024], in_=P1[:, 0:1024]),
                        waits=[vproj], inc=S["DV"])
                    vf_done = S["DV"].now()
                    for tch in range(8):
                        P.emit("tensor", lambda e_, tch=tch: e_.transpose(
                            PT[:, 0:128],
                            arena[:, A_VF + tch * 128:A_VF + tch * 128 + 128],
                            idb_sb[:]),
                            waits=[vf_done, S["DV"].now()], inc=S["PE"])
                        P.emit("vector", lambda e_, tc=tc, vc=vc, tch=tch: e_.tensor_copy(
                            out=arena[:, A_VT + (tc * 8 + tch) * 256 + vc * 128:
                                      A_VT + (tc * 8 + tch) * 256 + vc * 128 + 128],
                            in_=PT[:, 0:128]),
                            waits=[S["PE"].now()], inc=S["DV"])
            vt_done = S["DV"].now()
            dma("sync", S["KVC"],
                ccki.rearrange("(c p) t -> p c t", p=128),
                arena[:, A_KC:A_KC + 4096].rearrange("p (c t) -> p c t", c=2),
                waits=[vt_done])
            dma("sync", S["KVC"],
                ccvi.rearrange("(tc p) f -> p tc f", p=128),
                arena[:, A_VT:A_VT + 4096].rearrange("p (tc f) -> p tc f", tc=16),
                waits=[vt_done])
            kvc = S["KVC"].now()
            P.emit("gpsimd", lambda e_: e_.collective_compute(
                "AllGather", ALU.bypass, replica_groups=[list(range(NCORE))],
                ins=[ccki.ap().opt()], outs=[ccko.ap().opt()]),
                waits=[kvc], inc=S["CC"])
            P.emit("gpsimd", lambda e_: e_.collective_compute(
                "AllGather", ALU.bypass, replica_groups=[list(range(NCORE))],
                ins=[ccvi.ap().opt()], outs=[ccvo.ap().opt()]),
                inc=S["CC"])
            return S["CC"].now()

        def emit_qc(qc, cc_ready):
            batch, poff = qc // 2, (qc % 2) * 512
            c0 = qc * 512
            xt_ready, _ = emit_ln(c0, 512, gate=False, xt_base=A_XTA)
            for ch in range(DC):
                asem = S["AWA" if ch % 2 == 0 else "AWB"]
                buf = wgbuf if ch % 2 == 0 else w1buf
                dma("gpsimd", asem,
                    buf[:, 0:1024].rearrange("p (dc j) -> p dc j", dc=DC),
                    wq_t[:, :, ch * 128:(ch + 1) * 128],
                    waits=[S["PE"].now()])
                qw_ = asem.now()
                for dc in range(DC):
                    P.emit("tensor", lambda e_, dc=dc, buf=buf: e_.matmul(
                        P1[:, 0:512], buf[:, dc * 128:(dc + 1) * 128],
                        arena[:, A_XTA + dc * 512:A_XTA + dc * 512 + 512],
                        start=(dc == 0), stop=(dc == DC - 1),
                        skip_group_check=True),
                        waits=[qw_, xt_ready, S["DV"].now()] if dc == 0 else (),
                        inc=S["PE"] if dc == DC - 1 else None)
                qproj = S["PE"].now()
                P.emit("vector", lambda e_: e_.tensor_copy(
                    out=arena[:, A_KB:A_KB + 512], in_=P1[:, 0:512]),
                    waits=[qproj], inc=S["DV"])
                qb_done = S["DV"].now()
                P.emit("tensor", lambda e_: e_.matmul(
                    P3[:, 0:512], rot_sb[:], arena[:, A_KB:A_KB + 512],
                    start=True, stop=True),
                    waits=[qb_done, S["DV"].now()], inc=S["PE"])
                qrot = S["PE"].now()
                P.emit("vector", lambda e_: e_.tensor_tensor(
                    out=af32[:, F_T:F_T + 512], in0=P1[:, 0:512],
                    in1=arena[:, A_COS + poff:A_COS + poff + 512], op=ALU.mult),
                    waits=[qrot], inc=S["DV"])
                P.emit("vector", lambda e_: e_.tensor_tensor(
                    out=af32[:, F_U:F_U + 512], in0=P3[:, 0:512],
                    in1=arena[:, A_COS + TPB + poff:A_COS + TPB + poff + 512], op=ALU.mult),
                    inc=S["DV"])
                P.emit("vector", lambda e_, ch=ch: e_.tensor_tensor(
                    out=arena[:, A_Q + ch * 512:A_Q + (ch + 1) * 512],
                    in0=af32[:, F_T:F_T + 512], in1=af32[:, F_U:F_U + 512],
                    op=ALU.add), inc=S["DV"])
            q_ready = S["DV"].now()
            p3_free = S["DV"].now()
            kv_fill = {}
            o_evac = None

            def dma_kv(h):
                par = h % 2
                base = par * 64
                kvh = h // 4
                kc0 = LIVE[h][0]
                ssem = S["KVS0" if par == 0 else "KVS1"]
                wts = [cc_ready]
                if kv_cons[par] is not None:
                    wts += kv_cons[par]
                r0 = kc0 // 8
                for r in range(r0, 8):
                    jlo = max(kc0 - 8 * r, 0)
                    dma("sync", ssem,
                        arena[base:base + 64,
                              A_KH + (8 * r + jlo - kc0) * 128:
                              A_KH + (8 * r + 8 - kc0) * 128],
                        ccko.ap()[256 * r + 128 * (kvh // 2) + 64 * (kvh % 2):
                                  256 * r + 128 * (kvh // 2) + 64 * (kvh % 2) + 64,
                                  batch * 1024 + 128 * jlo:(batch + 1) * 1024],
                        waits=wts)
                    dma("sync", ssem,
                        arena[:, A_VH + par * 4096 + (8 * r + jlo - kc0) * 64:
                              A_VH + par * 4096 + (8 * r + 8 - kc0) * 64]
                        .rearrange("p (j f) -> p j f", f=64),
                        ccvo.ap().rearrange("(rb j p) f -> rb p j f", p=128, j=16)
                        [r, :, 8 * batch + jlo:8 * batch + 8,
                         kvh * 64:(kvh + 1) * 64],
                        waits=wts)
                kv_fill[h] = ssem.now()

            kv_cons = [None, None]
            dma_kv(0)
            dma_kv(1)
            ps_guard = {}
            for h in range(H):
                ch, base, kvh = h // 2, (h % 2) * 64, h // 4
                par = h % 2
                live = LIVE[h]
                kc0 = live[0]
                vh_done = {}
                nl = len(live)
                ngrp = (nl + 3) // 4
                expg = {}
                pvg = {}
                if h > 0:
                    expg[-1] = prev_expg
                    pvg[-1] = prev_pvg
                    pvg[-2] = prev_pvg

                def emit_pv(g):
                    idxs = list(range(4 * g, min(4 * g + 4, nl)))
                    gp = (g % 2) * 2048
                    for i, idx in enumerate(idxs):
                        slot = idx % 8
                        wts = [vh_done[idx]]
                        if i == 0:
                            wts.append(expg[g])
                            if idx == 0 and o_evac is not None:
                                wts.append(o_evac)
                        P.emit("tensor", lambda e_, slot=slot, i=i, idx=idx, gp=gp,
                               nl=nl: e_.matmul(
                            P2[0:65, 0:512], vhb[:, slot * 65:(slot + 1) * 65],
                            arena[:, A_PS + gp + i * 512:A_PS + gp + (i + 1) * 512],
                            start=(idx == 0), stop=(idx == nl - 1),
                            skip_group_check=True),
                            waits=wts, inc=S["PE"] if i == len(idxs) - 1 else None)
                    pvg[g] = S["PE"].now()

                for g in range(ngrp):
                    idxs = list(range(4 * g, min(4 * g + 4, nl)))
                    cnt = len(idxs)
                    gp = (g % 2) * 2048
                    for idx in idxs:
                        kc = live[idx]
                        slot = idx % 8
                        acol = alpha_sb[:, kc * H + h: kc * H + h + 1]
                        vh_w = [kv_fill[h]]
                        if g >= 2:
                            vh_w.append(pvg[g - 2])
                        elif h > 0:
                            vh_w.append(prev_pvg)
                        P.emit("vector", lambda e_, slot=slot, ac=acol, kc=kc,
                               par=par, kc0=kc0: e_.tensor_scalar(
                            out=vhb[:, slot * 65:slot * 65 + 64],
                            in0=arena[:, A_VH + par * 4096 + (kc - kc0) * 64:
                                      A_VH + par * 4096 + (kc - kc0) * 64 + 64],
                            scalar1=ac, scalar2=None, op0=ALU.mult),
                            waits=vh_w, inc=S["DV"])
                        P.emit("vector", lambda e_, slot=slot, ac=acol: e_.tensor_copy(
                            out=vhb[:, slot * 65 + 64:slot * 65 + 65], in_=ac),
                            inc=S["DV"])
                        vh_done[idx] = S["DV"].now()
                    qk_half = None
                    for i, idx in enumerate(idxs):
                        kc = live[idx]
                        PS_T = P0 if i < 2 else P1
                        off = (i % 2) * 512
                        wts = []
                        if i == 0:
                            wts = [kv_fill[h], q_ready]
                            if (g - 1) in expg:
                                wts.append(expg[g - 1])
                        want_inc = (i == cnt - 1) or (i == 1 and cnt > 2)
                        P.emit("tensor", lambda e_, PS_T=PS_T, off=off, base=base,
                               ch=ch, kc=kc, kc0=kc0: e_.matmul(
                            PS_T[:, off:off + 512],
                            arena[base:base + 64, A_KH + (kc - kc0) * 128:
                                  A_KH + (kc - kc0) * 128 + 128],
                            arena[base:base + 64, A_Q + ch * 512:A_Q + (ch + 1) * 512],
                            start=True, stop=True, skip_group_check=True),
                            waits=wts, inc=S["PE"] if want_inc else None)
                        if i == 1 and cnt > 2:
                            qk_half = S["PE"].now()
                    qkg = S["PE"].now()
                    e_w = [qk_half if qk_half is not None else qkg]
                    if (g - 2) in pvg:
                        e_w.append(pvg[g - 2])
                    n0 = min(cnt, 2) * 512
                    P.emit("scalar", lambda e_, n0=n0, gp=gp: e_.activation(
                        out=arena[:, A_PS + gp:A_PS + gp + n0], in_=P0[:, 0:n0],
                        func=AF.Exp),
                        waits=e_w, inc=S["SC"] if cnt <= 2 else None)
                    if cnt > 2:
                        n1 = (cnt - 2) * 512
                        P.emit("scalar", lambda e_, n1=n1, gp=gp: e_.activation(
                            out=arena[:, A_PS + gp + 1024:A_PS + gp + 1024 + n1],
                            in_=P1[:, 0:n1], func=AF.Exp),
                            waits=[qkg], inc=S["SC"])
                    expg[g] = S["SC"].now()
                    if g >= 1:
                        emit_pv(g - 1)
                emit_pv(ngrp - 1)
                kv_cons[par] = [qkg, vh_done[nl - 1]]
                if h + 2 < H:
                    dma_kv(h + 2)
                pv_done = S["PE"].now()
                prev_pvg = pvg[ngrp - 1]
                prev_expg = expg[ngrp - 1]
                P.emit("vector", lambda e_: e_.reciprocal(
                    out=af32[0:1, F_U:F_U + 512], in_=P2[64:65, 0:512]),
                    waits=[pv_done], inc=S["DV"])
                P.emit("tensor", lambda e_: e_.matmul(
                    P3[0:64, 0:512], ones_r[:, 0:64], af32[0:1, F_U:F_U + 512],
                    start=True, stop=True),
                    waits=[S["DV"].now(), p3_free], inc=S["PE"])
                P.emit("vector", lambda e_: e_.tensor_copy(
                    out=af32[0:64, F_T:F_T + 512], in_=P3[0:64, 0:512]),
                    waits=[S["PE"].now()], inc=S["DV"])
                p3_free = S["DV"].now()
                P.emit("vector", lambda e_, base=base, ch=ch: e_.tensor_tensor(
                    out=arena[base:base + 64, A_OC + ch * 512:A_OC + (ch + 1) * 512],
                    in0=P2[0:64, 0:512], in1=af32[0:64, F_T:F_T + 512], op=ALU.mult),
                    inc=S["DV"])
                o_evac = S["DV"].now()
            x2add = [None, None]
            for dout in range(DC):
                asem = S["AWA" if dout % 2 == 0 else "AWB"]
                buf = wgbuf if dout % 2 == 0 else w1buf
                dma("gpsimd", asem,
                    buf[:, 0:1024].rearrange("p (dc j) -> p dc j", dc=DC),
                    wo_t[:, :, dout * 128:(dout + 1) * 128],
                    waits=[S["PE"].now()])
                ww = asem.now()
                half = dout % 2
                for fc in range(DC):
                    wts = [ww, o_evac] if fc == 0 else []
                    if fc == 0 and x2add[half] is not None:
                        wts.append(x2add[half])
                    P.emit("tensor", lambda e_, fc=fc, buf=buf, half=half: e_.matmul(
                        P1[:, half * 512:(half + 1) * 512],
                        buf[:, fc * 128:(fc + 1) * 128],
                        arena[:, A_OC + fc * 512:A_OC + (fc + 1) * 512],
                        start=(fc == 0), stop=(fc == DC - 1),
                        skip_group_check=True),
                        waits=wts,
                        inc=S["PE"] if fc == DC - 1 else None)
                wo_done = S["PE"].now()
                P.emit("vector", lambda e_, dout=dout, half=half: e_.tensor_tensor(
                    out=x2s(dout, c0, 512), in0=x2s(dout, c0, 512),
                    in1=P1[:, half * 512:(half + 1) * 512], op=ALU.add),
                    waits=[wo_done], inc=S["DV"])
                x2add[half] = S["DV"].now()

        stage = globals().get("_STAGE", "full")
        if stage == "xrdw":
            emit_ffn(0, nexp=1, sub="xrd")
            for nm in ("WGA", "WGB", "W1A", "W1B", "WRA"):
                P.wait("gpsimd", [S[nm].now()])
            P.flush()
        if stage == "s0":
            pass
        elif stage == "xrdw":
            pass
        elif stage in ("pa", "wbs") or stage.startswith("x"):
            emit_ffn(0, nexp=1, sub=stage)
            P.flush()
        elif stage.startswith("e"):
            emit_ffn(0, nexp=int(stage[1:]))
            P.flush()
        else:
            emit_ffn(0)
            P.flush()
            if stage != "t0":
                emit_ffn(1024)
                P.flush()
        if stage in ("f1a", "full"):
            cc_ready = emit_kv()
            P.flush()
            for qc in range(4):
                emit_qc(qc, cc_ready)
                P.flush()
        if stage == "full":
            emit_ffn(0)
            P.flush()
            emit_ffn(1024)
            P.flush()
        for dc in range(DC):
            dma("sync", S["ST"], out_t[dc], x2[:, dc * T:(dc + 1) * T])
        P.wait("sync", [S["ST"].now()])
        P.flush()

    return nc


_NC = None


def _get_nc():
    global _NC
    if _NC is None:
        _NC = build_nc()
    return _NC


def kernel(L0_attn_x, L0_attn_Wq, L0_attn_Wk, L0_attn_Wv, L0_attn_Wo,
           L0_ffn_Wgate, L0_ffn_bgate, L0_ffn_Wg, L0_ffn_bg,
           L0_ffn_W1, L0_ffn_b1, L0_ffn_W2, L0_ffn_b2):
    x = np.asarray(L0_attn_x, dtype=np.float32)
    f32c = lambda a: np.ascontiguousarray(np.asarray(a, dtype=np.float32))
    shared = {
        "Wq": f32c(L0_attn_Wq), "Wk": f32c(L0_attn_Wk), "Wv": f32c(L0_attn_Wv),
        "Wo": f32c(L0_attn_Wo), "Wg": f32c(L0_ffn_Wg), "W1": f32c(L0_ffn_W1),
        "W2": f32c(L0_ffn_W2),
        "Wgate_img": np.ascontiguousarray(
            f32c(L0_ffn_Wgate).reshape(DC, 128, E).transpose(1, 0, 2).reshape(128, DC * E)),
        "bgate_r": f32c(L0_ffn_bgate).reshape(1, E),
        "bg_img": np.ascontiguousarray(
            f32c(L0_ffn_bg).reshape(E, MC, 128).transpose(2, 0, 1).reshape(128, E * MC)),
        "b1_img": np.ascontiguousarray(
            f32c(L0_ffn_b1).reshape(E, MC, 128).transpose(2, 0, 1).reshape(128, E * MC)),
        "b2_img": np.ascontiguousarray(
            f32c(L0_ffn_b2).reshape(E, DC, 128).transpose(2, 0, 1).reshape(128, E * DC)),
    }
    in_maps = []
    for c in range(NCORE):
        sl = dict(shared)
        sl.update(_host_consts(c))
        xc = np.concatenate([x[0, TPB * c:TPB * (c + 1)],
                             x[1, TPB * c:TPB * (c + 1)]], axis=0)
        sl["xT"] = np.ascontiguousarray(xc.T)
        in_maps.append(sl)
    nc = _get_nc()
    res = run_bass_kernel_spmd(nc, in_maps, core_ids=list(range(NCORE)))
    out = np.empty((B, L, D), dtype=np.float32)
    for c, r in enumerate(res.results):
        oc = np.asarray(r["out"])
        out[0, TPB * c:TPB * (c + 1)] = oc[:, 0:TPB].T
        out[1, TPB * c:TPB * (c + 1)] = oc[:, TPB:T].T
    return out
